# version 16
# baseline (speedup 1.0000x reference)
import numpy as np
import ml_dtypes
from contextlib import ExitStack
import concourse.bass as bass
import concourse.mybir as mybir
from concourse.bass_utils import run_bass_kernel_spmd

F32 = mybir.dt.float32
BF16 = mybir.dt.bfloat16
I32 = mybir.dt.int32
AF = mybir.ActivationFunctionType
ALU = mybir.AluOpType
AXX = mybir.AxisListType.X

P = 128
H = 4096
DIN = 8192
NH = 8
RK = 256
TLOC = 1024
TEXT = 1027
TFULL = 2048
BF = ml_dtypes.bfloat16
GROUPS = [[0, 1], [2, 3], [4, 5], [6, 7]]
DEBUG_TAPS = False
PHASES = "ABC"
CPARTS = "gof"

ENGINES = ("sync", "vector", "scalar", "tensor", "gpsimd")


class _MockInst:
    def then_inc(self, sem, n=1):
        return self


class _MockEng:
    def __getattr__(self, name):
        def f(*a, **k):
            return _MockInst()
        return f


class Emit:

    def __init__(self):
        self.keys = {}
        self.first = True

    def start_pass(self, engines):
        self.eng = engines
        self.ctr = {}

    def end_pass(self):
        self.first = False

    def cnt(self, sem):
        return self.ctr.get(id(sem), 0)

    def inc(self, sem, n=1):
        self.ctr[id(sem)] = self.cnt(sem) + n

    def mark(self, sem, key):
        v = self.cnt(sem)
        if key not in self.keys:
            self.keys[key] = (sem, v)
        else:
            psem, pv = self.keys[key]
            assert psem is sem and pv == v, (key, pv, v)

    def wait(self, engname, sem, key):
        msem, v = self.keys[key]
        if v > 0:
            self.eng[engname].wait_ge(msem, v)

    def wait_val(self, engname, sem, v):
        if v > 0:
            self.eng[engname].wait_ge(sem, v)


def build_program():
    nc = bass.Bass("TRN2", target_bir_lowering=False, debug=False)

    def ein(name, shape, dt=F32):
        return nc.dram_tensor(name, shape, dt, kind="ExternalInput").ap()

    hs_ext = ein("hs_ext", [TEXT, H])
    win_t = ein("win_t", [128, P, 32 * P])
    woutT_t = ein("woutT_t", [32, P, 64 * P])
    wdtT_in = ein("wdtT_in", [RK, DIN])
    wxpT_t = ein("wxpT_t", [64, P, 288])
    qw2d = ein("qw2d", [H, H], I32)
    scales2 = ein("scales2", [H, 32])
    zeros2 = ein("zeros2", [H, 32])
    alog_l = ein("alog_l", [P, 64 * NH])
    cw4 = ein("cw4", [P, 64 * 4])
    cb1 = ein("cb1", [P, 64])
    dtb1 = ein("dtb1", [P, 64])
    dp1 = ein("dp1", [P, 64])
    fbias_in = ein("fbias_in", [P, 32])
    wln_in = ein("wln_in", [1, 288])
    ident_in = ein("ident_in", [P, P])
    esel_in = ein("esel_in", [16, NH * P])
    fbias_row = ein("fbias_row", [1, H])

    out_ext = nc.dram_tensor("out_ext", [TLOC, H], F32, kind="ExternalOutput").ap()
    dbg = {}
    if DEBUG_TAPS:
        for nm, shape, dt in [("d_xct", [DIN, TLOC], F32),
                              ("d_szt", [DIN, TLOC], F32),
                              ("d_dtr", [RK, TLOC], BF16),
                              ("d_bc", [32, TLOC], BF16),
                              ("d_yown", [DIN, TLOC], BF16),
                              ("d_gt", [H, TLOC], BF16),
                              ("d_st", [H, TLOC], BF16),
                              ("d_yp", [2 * DIN, TLOC], BF16),
                              ("d_wf", [H, H], BF16),
                              ("d_xcbg", [2 * DIN, TLOC], BF16)]:
            dbg[nm] = nc.dram_tensor(nm, shape, dt, kind="ExternalOutput").ap()

    def din_(name, shape, dt):
        return nc.dram_tensor(name, shape, dt).ap()

    xct_d = din_("xct_d", [DIN, TLOC], F32)
    xcb_d = din_("xcb_d", [DIN, TLOC], BF16)
    xcb_g = din_("xcb_g", [2 * DIN, TLOC], BF16)
    szt_d = din_("szt_d", [DIN, TLOC], BF16)
    dtr_d = din_("dtr_d", [RK, TLOC], BF16)
    dtr_g = din_("dtr_g", [2 * RK, TLOC], BF16)
    bc_d = din_("bc_d", [32, TLOC], BF16)
    bc_g = din_("bc_g", [64, TLOC], BF16)
    gt_d = din_("gt_d", [TLOC, H], BF16)
    st_d = din_("st_d", [H, TLOC], BF16)
    wffn_scr = din_("wffn_scr", [H, H], BF16)
    wout_scr = din_("wout_scr", [32, P, 64 * P], BF16)
    wdt_scr = din_("wdt_scr", [2, P, DIN], BF16)
    yp_d = din_("yp_d", [2 * DIN, TLOC], BF16)
    yown_d = din_("yown_d", [DIN, TLOC], BF16)

    E = Emit()

    with ExitStack() as top:
        def sem(name):
            return top.enter_context(nc.semaphore(name))

        sLD = sem("sLD")
        sST = sem("sST")
        sTE = sem("sTE")
        sDV = sem("sDV")
        sAC = sem("sAC")
        sPL = sem("sPL")
        cc1 = sem("cc1")
        cc2 = sem("cc2")
        cc3 = sem("cc3")
        cc4 = sem("cc4")
        SS = {}
        for _nm in ["sCN", "h0", "h1", "w0", "w1", "q0", "q1", "xl0", "xl1",
                    "sBC", "sWT", "sWS", "t0", "t1", "t2", "t3", "cwq",
                    "cxc0", "cxc1", "cxb0", "cxb1", "csz0", "csz1",
                    "cgt0", "cgt1", "sXG", "wd0", "wd1", "cyb", "g0", "g1",
                    "o0", "o1", "cst", "f0", "f1", "cot0", "cot1"]:
            SS[_nm] = sem(_nm)

        def sbt(name, shape, dt=F32):
            return top.enter_context(nc.sbuf_tensor(name, shape, dt))

        ident_f = sbt("ident_f", [P, P])
        ident_b = sbt("ident_b", [P, P], BF16)
        ones1 = sbt("ones1", [1, P])
        cw_sb = sbt("cw_sb", [P, 64 * 4])
        cb_sb = sbt("cb_sb", [P, 64])
        dtb_sb = sbt("dtb_sb", [P, 64])
        dp_sb = sbt("dp_sb", [P, 64])
        fb_sb = sbt("fb_sb", [P, 32])
        wln_sb = sbt("wln_sb", [1, 288])
        wlnbc = sbt("wlnbc", [P, 288])
        alog_sb = sbt("alog_sb", [P, 64 * NH])
        epsc = sbt("epsc", [P, 2])
        adev_sb = sbt("adev_sb", [P, 64 * NH])
        nst = sbt("nst", [P, 48])

        psP = [top.enter_context(nc.psum_tensor("psP%d" % i, [P, 512], F32))
               for i in range(8)]


        def emit_A(G, T):
            sync = G["sync"]; vector = G["vector"]; scalar = G["scalar"]
            tensor = G["tensor"]; gpsimd = G["gpsimd"]

            for i, (dst, src) in enumerate([
                    (T.cw_l, cw4), (T.cb_l, cb1), (T.dtb_l, dtb1),
                    (T.dp_l, dp1), (T.fb_l, fbias_in), (T.wln_l, wln_in),
                    (T.id_l, ident_in), (T.al_l, alog_l)]):
                sync.dma_start(out=dst, in_=src).then_inc(SS["sCN"], 16)
                E.inc(SS["sCN"], 16)
            E.mark(SS["sCN"], "consts")
            E.wait("gpsimd", sLD, "consts")
            gpsimd.memset(epsc[:, 0:1], 1e-5)
            gpsimd.memset(epsc[:, 1:2], 1e-6)
            gpsimd.memset(ones1[:], 1.0)
            gpsimd.tensor_copy(ident_b[:], ident_f[:]).then_inc(sPL, 1)
            E.inc(sPL)
            E.mark(sPL, "idb")
            E.wait("tensor", sPL, "idb")
            tensor.matmul(psP[0][:, 0:288], ones1[:], wln_sb[0:1, :],
                          start=True, stop=True).then_inc(sTE, 1)
            E.inc(sTE)
            E.mark(sTE, "wlnbc_mm")
            E.wait("vector", sTE, "wlnbc_mm")
            vector.tensor_copy(wlnbc[:], psP[0][:, 0:288]).then_inc(sDV, 1)
            E.inc(sDV)
            E.mark(sDV, "wlnbc")

            NT = 9
            ns_p0 = nc.named_scope("p0"); ns_p0.__enter__()
            for tt in range(NT):
                if tt == 8:
                    E.wait("vector", sDV, "p0_apply_6")
                    vector.memset(T.hs_t[0][:], 0.0).then_inc(sDV, 1)
                    E.inc(sDV)
                    E.mark(sDV, "p0_ms8")
                if tt >= 2:
                    E.wait("sync", sDV, "p0_apply_%d" % (tt - 2))
                hsem = SS["h%d" % (tt % 2)]
                if tt < 8:
                    sync.dma_start(out=T.hs_t[tt % 2][:],
                                   in_=hs_ext[tt * P:(tt + 1) * P, :]
                                   ).then_inc(hsem, 16)
                else:
                    E.wait("sync", sDV, "p0_ms8")
                    sync.dma_start(out=T.hs_t[0][0:3, :],
                                   in_=hs_ext[1024:1027, :]).then_inc(hsem, 16)
                E.inc(hsem, 16)
                E.mark(hsem, "p0_ld_%d" % tt)

                E.wait("vector", sLD, "p0_ld_%d" % tt)
                vector.tensor_reduce(nst[:, tt:tt + 1], T.hs_t[tt % 2][:],
                                     AXX, ALU.add, negate=True)
                E.wait("scalar", sLD, "p0_ld_%d" % tt)
                if tt >= 1:
                    E.wait("scalar", sDV, "p0_vsum_%d" % (tt - 1))
                scalar.activation(T.sq_t[:], T.hs_t[tt % 2][:], AF.Square
                                  ).then_inc(sAC, 1)
                E.inc(sAC)
                E.mark(sAC, "p0_sq_%d" % tt)
                E.wait("vector", sAC, "p0_sq_%d" % tt)
                vector.tensor_reduce(nst[:, 9 + tt:10 + tt], T.sq_t[:],
                                     AXX, ALU.add).then_inc(sDV, 1)
                E.inc(sDV)
                E.mark(sDV, "p0_vsum_%d" % tt)
                vector.tensor_scalar(nst[:, tt:tt + 1], nst[:, tt:tt + 1],
                                     1.0 / H, None, ALU.mult).then_inc(sDV, 1)
                E.inc(sDV)
                E.mark(sDV, "p0_negmean_%d" % tt)
                E.wait("gpsimd", sDV, "p0_negmean_%d" % tt)
                gpsimd.tensor_scalar(nst[:, 9 + tt:10 + tt],
                                     nst[:, 9 + tt:10 + tt],
                                     1.0 / H, None, ALU.mult)
                gpsimd.tensor_tensor(nst[:, 18 + tt:19 + tt],
                                     nst[:, tt:tt + 1], nst[:, tt:tt + 1],
                                     ALU.mult).then_inc(sPL, 1)
                E.inc(sPL)
                E.mark(sPL, "p0_m2_%d" % tt)
                E.wait("vector", sPL, "p0_m2_%d" % tt)
                vector.scalar_tensor_tensor(nst[:, 27 + tt:28 + tt],
                                            nst[:, 9 + tt:10 + tt], 1.0,
                                            nst[:, 18 + tt:19 + tt],
                                            ALU.mult, ALU.subtract
                                            ).then_inc(sDV, 1)
                E.inc(sDV)
                E.mark(sDV, "p0_va_%d" % tt)
                E.wait("scalar", sDV, "p0_va_%d" % tt)
                E.wait("scalar", sPL, "idb")
                scalar.activation(nst[:, 9 + tt:10 + tt],
                                  nst[:, 27 + tt:28 + tt], AF.Ln,
                                  bias=epsc[:, 0:1]).then_inc(sAC, 1)
                E.inc(sAC)
                E.mark(sAC, "p0_ln_%d" % tt)
                E.wait("vector", sAC, "p0_ln_%d" % tt)
                vector.tensor_scalar(nst[:, 27 + tt:28 + tt],
                                     nst[:, 9 + tt:10 + tt], -0.5, None,
                                     ALU.mult).then_inc(sDV, 1)
                E.inc(sDV)
                E.mark(sDV, "p0_lnh_%d" % tt)
                E.wait("scalar", sDV, "p0_lnh_%d" % tt)
                scalar.activation(nst[:, 9 + tt:10 + tt],
                                  nst[:, 27 + tt:28 + tt], AF.Exp
                                  ).then_inc(sAC, 1)
                E.inc(sAC)
                E.mark(sAC, "p0_rstd_%d" % tt)
                E.wait("gpsimd", sAC, "p0_rstd_%d" % tt)
                gpsimd.tensor_tensor(nst[:, 18 + tt:19 + tt],
                                     nst[:, tt:tt + 1],
                                     nst[:, 9 + tt:10 + tt], ALU.mult
                                     ).then_inc(sPL, 1)
                E.inc(sPL)
                E.mark(sPL, "p0_bias_%d" % tt)
                E.wait("vector", sPL, "p0_bias_%d" % tt)
                if tt >= 1:
                    E.wait("vector", sTE, "p0_tr_%d_7" % (tt - 1))
                vector.tensor_scalar(T.xn_t[:], T.hs_t[tt % 2][:],
                                     nst[:, 9 + tt:10 + tt],
                                     nst[:, 18 + tt:19 + tt],
                                     ALU.mult, ALU.add).then_inc(sDV, 1)
                E.inc(sDV)
                E.mark(sDV, "p0_apply_%d" % tt)
                ncol = P if tt < 8 else 3
                for q in range(8):
                    E.wait("tensor", sDV, "p0_apply_%d" % tt)
                    E.wait("tensor", sPL, "idb")
                    if tt > 0 or q >= 2:
                        prev = (tt * 8 + q) - 2
                        E.wait("tensor", sDV, "p0_ev_%d" % prev)
                    pb = psP[6 + q % 2]
                    for j in range(4):
                        hc = q * 4 + j
                        inst = tensor.transpose(
                            pb[:, j * P:(j + 1) * P],
                            T.xn_t[:, hc * P:(hc + 1) * P], ident_f[:])
                    inst.then_inc(sTE, 1)
                    E.inc(sTE)
                    E.mark(sTE, "p0_tr_%d_%d" % (tt, q))
                    E.wait("vector", sTE, "p0_tr_%d_%d" % (tt, q))
                    for j in range(4):
                        hc = q * 4 + j
                        inst = vector.tensor_copy(
                            xnT[:, hc * TEXT + tt * P:hc * TEXT + tt * P + ncol],
                            pb[:, j * P:j * P + ncol])
                    inst.then_inc(sDV, 1)
                    E.inc(sDV)
                    E.mark(sDV, "p0_ev_%d" % (tt * 8 + q))

            ns_p0.__exit__(None, None, None)
            ns_p1 = nc.named_scope("p1"); ns_p1.__enter__()
            for fc in range(128):
                xpart = fc < 64
                db = fc
                par = fc % 2
                if fc >= 8 and (fc - 8) % 3 == 0 and (fc - 8) // 3 < 32:
                    ot = (fc - 8) // 3
                    qpar = ot % 2
                    if ot >= 2:
                        E.wait("sync", sDV, "dq_w_%d" % (ot - 2))
                    szo = (ot % 2) * 64
                    qsem = SS["q%d" % qpar]
                    sync.dma_start(out=T.qwt[qpar][:],
                                   in_=qw2d[ot * P:(ot + 1) * P, :]
                                   ).then_inc(qsem, 16)
                    sync.dma_start(out=T.scz[:, szo:szo + 32],
                                   in_=scales2[ot * P:(ot + 1) * P, :]
                                   ).then_inc(qsem, 16)
                    sync.dma_start(out=T.scz[:, szo + 32:szo + 64],
                                   in_=zeros2[ot * P:(ot + 1) * P, :]
                                   ).then_inc(qsem, 16)
                    E.inc(qsem, 48)
                    E.mark(qsem, "dq_ld_%d" % ot)
                    E.wait("vector", sLD, "dq_ld_%d" % ot)
                    if ot == 0:
                        E.wait("vector", sTE, "p0_tr_8_7")
                    if ot >= 1:
                        E.wait("vector", sST, "dq_st_%d" % (ot - 1))
                    for g in range(32):
                        inst = vector.tensor_scalar(
                            T.wdq[:, g * P:(g + 1) * P],
                            T.qwt[qpar][:, g * P:(g + 1) * P],
                            T.scz[:, szo + g:szo + g + 1],
                            T.scz[:, szo + 32 + g:szo + 33 + g],
                            ALU.mult, ALU.add)
                    inst.then_inc(sDV, 1)
                    E.inc(sDV)
                    E.mark(sDV, "dq_w_%d" % ot)
                    E.wait("scalar", sDV, "dq_w_%d" % ot)
                    scalar.dma_start(out=wffn_scr[ot * P:(ot + 1) * P, :],
                                     in_=T.wdq[:]).then_inc(SS["cwq"], 16)
                    E.inc(SS["cwq"], 16)
                    E.mark(SS["cwq"], "dq_st_%d" % ot)
                if fc == 0:
                    E.wait("sync", sDV, "p0_vsum_8")
                if fc >= 2:
                    E.wait("sync", sPL, "p1_cast_%d" % (fc - 2))
                    E.wait("sync", sAC, "p1_casth_%d" % (fc - 2))
                wsem = SS["w%d" % par]
                for qv in range(4):
                    sync.dma_start(out=T.wbuf[par][:, qv * 1024:(qv + 1) * 1024],
                                   in_=win_t[fc][:, qv * 1024:(qv + 1) * 1024]
                                   ).then_inc(wsem, 16)
                E.inc(wsem, 64)
                E.mark(wsem, "p1_w_%d" % fc)
                E.wait("gpsimd", sLD, "p1_w_%d" % fc)
                if fc >= 2:
                    E.wait("gpsimd", sTE, "p1_mm_%d" % (fc - 2))
                gpsimd.tensor_copy(T.wbb[par][:, 0:2048],
                                   T.wbuf[par][:, 0:2048]).then_inc(sPL, 1)
                E.inc(sPL)
                E.mark(sPL, "p1_cast_%d" % fc)
                E.wait("scalar", sLD, "p1_w_%d" % fc)
                if fc >= 2:
                    E.wait("scalar", sTE, "p1_mm_%d" % (fc - 2))
                scalar.activation(T.wbb[par][:, 2048:4096],
                                  T.wbuf[par][:, 2048:4096], AF.Copy
                                  ).then_inc(sAC, 1)
                E.inc(sAC)
                E.mark(sAC, "p1_casth_%d" % fc)
                E.wait("tensor", sPL, "p1_cast_%d" % fc)
                E.wait("tensor", sAC, "p1_casth_%d" % fc)
                if fc >= 2:
                    E.wait("tensor", sAC, "p1_ev_%d" % (fc - 2))
                if fc == 0:
                    E.mark(sDV, "chk")
                    E.wait("tensor", sDV, "p0_ev_%d" % (9 * 8 - 1))
                tcs = [(0, 512), (512, 1024), (1024, 1027)] if xpart else \
                      [(3, 515), (515, 1027)]
                for ti, (t0, t1) in enumerate(tcs):
                    pb = psP[3 * par + ti]
                    for hc in range(32):
                        inst = tensor.matmul(
                            pb[:, 0:t1 - t0],
                            T.wbb[par][:, hc * P:(hc + 1) * P],
                            xnT[:, hc * TEXT + t0:hc * TEXT + t1],
                            start=(hc == 0), stop=(hc == 31))
                inst.then_inc(sTE, 1)
                E.inc(sTE)
                E.mark(sTE, "p1_mm_%d" % fc)

                if xpart:
                    E.wait("scalar", sTE, "p1_mm_%d" % fc)
                    scalar.activation(T.xsb[par][:, 0:512],
                                      psP[3 * par + 0][:], AF.Copy)
                    scalar.activation(T.xsb[par][:, 512:1024],
                                      psP[3 * par + 1][:], AF.Copy)
                    scalar.activation(T.xsb[par][:, 1024:1027],
                                      psP[3 * par + 2][:, 0:3], AF.Copy
                                      ).then_inc(sAC, 1)
                    E.inc(sAC)
                    E.mark(sAC, "p1_ev_%d" % fc)
                    E.wait("vector", sPL, "idb")
                    if fc >= 2:
                        E.wait("vector", sTE, "p1_conv_%d" % (fc - 2))
                    for k in range(4):
                        inst = vector.tensor_scalar(
                            T.diag[par][:, k * P:(k + 1) * P], ident_f[:],
                            cw_sb[:, db * 4 + k:db * 4 + k + 1], None,
                            ALU.mult)
                    inst.then_inc(sDV, 1)
                    E.inc(sDV)
                    E.mark(sDV, "p1_diag_%d" % fc)
                    E.wait("tensor", sAC, "p1_ev_%d" % fc)
                    E.wait("tensor", sDV, "p1_diag_%d" % fc)
                    if fc >= 1:
                        E.wait("tensor", sAC, "p1_silu_%d" % (fc - 1))
                    for tc2 in range(2):
                        pb = psP[6 + tc2]
                        for k in range(4):
                            inst = tensor.matmul(
                                pb[:],
                                T.diag[par][:, k * P:(k + 1) * P],
                                T.xsb[par][:, tc2 * 512 + k:tc2 * 512 + k + 512],
                                start=(k == 0), stop=(k == 3))
                    inst.then_inc(sTE, 1)
                    E.inc(sTE)
                    E.mark(sTE, "p1_conv_%d" % fc)
                    E.wait("scalar", sTE, "p1_conv_%d" % fc)
                    if fc >= 2:
                        E.wait("scalar", sST, "p1_stxb_%d" % (fc - 2))
                        E.wait("scalar", sDV, "p1_cb_%d" % (fc - 2))
                    scalar.activation(T.xc32[par][:, 0:512], psP[6][:],
                                      AF.Silu, bias=cb_sb[:, db:db + 1])
                    scalar.activation(T.xc32[par][:, 512:1024], psP[7][:],
                                      AF.Silu, bias=cb_sb[:, db:db + 1]
                                      ).then_inc(sAC, 1)
                    E.inc(sAC)
                    E.mark(sAC, "p1_silu_%d" % fc)
                    E.wait("vector", sAC, "p1_silu_%d" % fc)
                    if fc >= 2:
                        E.wait("vector", sST, "p1_stxb_%d" % (fc - 2))
                    vector.tensor_copy(T.xcb16[par][:], T.xc32[par][:]
                                       ).then_inc(sDV, 1)
                    E.inc(sDV)
                    E.mark(sDV, "p1_cb_%d" % fc)
                    E.wait("scalar", sDV, "p1_cb_%d" % fc)
                    scalar.dma_start(out=xcb_d[db * P:(db + 1) * P, :],
                                     in_=T.xcb16[par][:]
                                     ).then_inc(SS["cxb%d" % par], 16)
                    E.inc(SS["cxb%d" % par], 16)
                    E.mark(SS["cxb%d" % par], "p1_stxb_%d" % fc)
                else:
                    E.wait("scalar", sTE, "p1_mm_%d" % fc)
                    if fc >= 66:
                        E.wait("scalar", sST, "p1_stz_%d" % (fc - 2))
                    scalar.activation(T.sz32[par][:, 0:512],
                                      psP[3 * par + 0][:], AF.Silu)
                    scalar.activation(T.sz32[par][:, 512:1024],
                                      psP[3 * par + 1][:], AF.Silu
                                      ).then_inc(sAC, 1)
                    E.inc(sAC)
                    E.mark(sAC, "p1_ev_%d" % fc)
                    zb = fc - 64
                    E.wait("scalar", sAC, "p1_ev_%d" % fc)
                    scalar.dma_start(out=szt_d[zb * P:(zb + 1) * P, :],
                                     in_=T.sz32[par][:]
                                     ).then_inc(SS["csz%d" % par], 16)
                    E.inc(SS["csz%d" % par], 16)
                    E.mark(SS["csz%d" % par], "p1_stz_%d" % fc)

            ns_p1.__exit__(None, None, None)
            ns_p2 = nc.named_scope("p2"); ns_p2.__enter__()
            for dc in range(64):
                par = dc % 2
                if dc >= 2:
                    E.wait("sync", sTE, "p2_mm_%d" % (dc - 2))
                E.wait("sync", sST, "p1_stxb_%d" % dc)
                xsem = SS["xl%d" % par]
                sync.dma_start(out=T.xcbL[par][:],
                               in_=xcb_d[dc * P:(dc + 1) * P, :]
                               ).then_inc(xsem, 16)
                sync.dma_start(out=T.wxp[par][:], in_=wxpT_t[dc]
                               ).then_inc(xsem, 16)
                E.inc(xsem, 32)
                E.mark(xsem, "p2_ld_%d" % dc)
                E.wait("gpsimd", sLD, "p2_ld_%d" % dc)
                if dc >= 2:
                    E.wait("gpsimd", sTE, "p2_mm_%d" % (dc - 2))
                gpsimd.tensor_copy(T.wxpb[par][:], T.wxp[par][:]
                                   ).then_inc(sPL, 1)
                E.inc(sPL)
                E.mark(sPL, "p2_cast_%d" % dc)
                E.wait("tensor", sPL, "p2_cast_%d" % dc)
                E.wait("tensor", sLD, "p2_ld_%d" % dc)
                if dc == 0:
                    E.wait("tensor", sAC, "p1_ev_127")
                for tt in range(8):
                    inst = tensor.matmul(
                        psP[tt][:, 0:288],
                        T.xcbL[par][:, tt * P:(tt + 1) * P],
                        T.wxpb[par][:],
                        start=(dc == 0), stop=(dc == 63))
                inst.then_inc(sTE, 1)
                E.inc(sTE)
                E.mark(sTE, "p2_mm_%d" % dc)

            E.wait("scalar", sTE, "p2_mm_63")
            for tt in range(8):
                inst = scalar.activation(T.ssm[:, tt * 288:(tt + 1) * 288],
                                         psP[tt][:, 0:288], AF.Copy)
            inst.then_inc(sAC, 1)
            E.inc(sAC)
            E.mark(sAC, "p2_ssm")
            E.wait("scalar", sAC, "p2_ssm")
            scalar.activation(T.ssq[:], T.ssm[:], AF.Square).then_inc(sAC, 1)
            E.inc(sAC)
            E.mark(sAC, "p2_sq")
            E.wait("vector", sAC, "p2_sq")
            for tt in range(8):
                vector.tensor_reduce(nst[:, tt:tt + 1],
                                     T.ssq[:, tt * 288:tt * 288 + 256],
                                     AXX, ALU.add)
                vector.tensor_reduce(nst[:, 8 + tt:9 + tt],
                                     T.ssq[:, tt * 288 + 256:tt * 288 + 272],
                                     AXX, ALU.add)
                inst = vector.tensor_reduce(
                    nst[:, 16 + tt:17 + tt],
                    T.ssq[:, tt * 288 + 272:tt * 288 + 288], AXX, ALU.add)
            inst.then_inc(sDV, 1)
            E.inc(sDV)
            E.mark(sDV, "p2_sums")
            E.wait("gpsimd", sDV, "p2_sums")
            gpsimd.tensor_scalar(nst[:, 0:8], nst[:, 0:8], 1.0 / RK, None,
                                 ALU.mult)
            gpsimd.tensor_scalar(nst[:, 8:24], nst[:, 8:24], 1.0 / 16.0,
                                 None, ALU.mult).then_inc(sPL, 1)
            E.inc(sPL)
            E.mark(sPL, "p2_ms")
            E.wait("scalar", sPL, "p2_ms")
            scalar.activation(nst[:, 24:48], nst[:, 0:24], AF.Ln,
                              bias=epsc[:, 1:2]).then_inc(sAC, 1)
            E.inc(sAC)
            E.mark(sAC, "p2_lnv")
            E.wait("vector", sAC, "p2_lnv")
            vector.tensor_scalar(nst[:, 24:48], nst[:, 24:48], -0.5, None,
                                 ALU.mult).then_inc(sDV, 1)
            E.inc(sDV)
            E.mark(sDV, "p2_lnh")
            E.wait("scalar", sDV, "p2_lnh")
            scalar.activation(nst[:, 0:24], nst[:, 24:48], AF.Exp
                              ).then_inc(sAC, 1)
            E.inc(sAC)
            E.mark(sAC, "p2_rstd")
            E.wait("vector", sAC, "p2_rstd")
            E.wait("vector", sDV, "wlnbc")
            for tt in range(8):
                vector.scalar_tensor_tensor(
                    T.dtrb[:, tt * RK:(tt + 1) * RK],
                    T.ssm[:, tt * 288:tt * 288 + 256],
                    nst[:, tt:tt + 1], wlnbc[:, 0:256], ALU.mult, ALU.mult)
                vector.scalar_tensor_tensor(
                    T.bmb[:, tt * 16:(tt + 1) * 16],
                    T.ssm[:, tt * 288 + 256:tt * 288 + 272],
                    nst[:, 8 + tt:9 + tt], wlnbc[:, 256:272],
                    ALU.mult, ALU.mult)
                inst = vector.scalar_tensor_tensor(
                    T.cmb[:, tt * 16:(tt + 1) * 16],
                    T.ssm[:, tt * 288 + 272:tt * 288 + 288],
                    nst[:, 16 + tt:17 + tt], wlnbc[:, 272:288],
                    ALU.mult, ALU.mult)
            inst.then_inc(sDV, 1)
            E.inc(sDV)
            E.mark(sDV, "p2_app")
            E.wait("tensor", sDV, "p2_app")
            for grp in range(4):
                pb = psP[6 + grp % 2]
                if grp >= 2:
                    E.wait("tensor", sDV, "p2_trev_%d" % (grp - 2))
                for j in range(4):
                    idx = grp * 4 + j
                    tt, rc = idx // 2, idx % 2
                    inst = tensor.transpose(
                        pb[:, j * P:(j + 1) * P],
                        T.dtrb[:, tt * RK + rc * P:tt * RK + (rc + 1) * P],
                        ident_f[:])
                inst.then_inc(sTE, 1)
                E.inc(sTE)
                E.mark(sTE, "p2_tr_%d" % grp)
                E.wait("vector", sTE, "p2_tr_%d" % grp)
                for j in range(4):
                    idx = grp * 4 + j
                    tt, rc = idx // 2, idx % 2
                    inst = vector.tensor_copy(
                        T.dtrT[:, rc * TLOC + tt * P:rc * TLOC + (tt + 1) * P],
                        pb[:, j * P:(j + 1) * P])
                inst.then_inc(sDV, 1)
                E.inc(sDV)
                E.mark(sDV, "p2_trev_%d" % grp)
            E.wait("tensor", sDV, "p2_trev_3")
            for half, srcmb in ((0, T.bmb), (1, T.cmb)):
                for tt in range(8):
                    pb = psP[4 + 2 * half + tt // 4]
                    inst = tensor.transpose(
                        pb[0:16, (tt % 4) * P:(tt % 4 + 1) * P],
                        srcmb[:, tt * 16:(tt + 1) * 16], ident_f[:])
                inst.then_inc(sTE, 1)
                E.inc(sTE)
                E.mark(sTE, "p2_bctr_%d" % half)
                E.wait("vector", sTE, "p2_bctr_%d" % half)
                dstT = T.bmT if half == 0 else T.cmT
                vector.tensor_copy(dstT[:, 0:512],
                                   psP[4 + 2 * half][0:16, 0:512])
                inst = vector.tensor_copy(dstT[:, 512:1024],
                                          psP[5 + 2 * half][0:16, 0:512])
                inst.then_inc(sDV, 1)
                E.inc(sDV)
                E.mark(sDV, "p2_bcev_%d" % half)
            E.wait("scalar", sDV, "p2_trev_3")
            for rc in range(2):
                scalar.dma_start(out=dtr_d[rc * P:(rc + 1) * P, :],
                                 in_=T.dtrT[:, rc * TLOC:(rc + 1) * TLOC]
                                 ).then_inc(SS["sBC"], 16)
                E.inc(SS["sBC"], 16)
            E.wait("scalar", sDV, "p2_bcev_0")
            E.wait("scalar", sDV, "p2_bcev_1")
            scalar.dma_start(out=bc_d[0:16, :], in_=T.bmT[:]
                             ).then_inc(SS["sBC"], 16)
            scalar.dma_start(out=bc_d[16:32, :], in_=T.cmT[:]
                             ).then_inc(SS["sBC"], 16)
            E.inc(SS["sBC"], 32)
            E.mark(SS["sBC"], "ex1_bc")
            ns_p2.__exit__(None, None, None)
            E.wait("gpsimd", sST, "ex1_bc")
            E.wait("gpsimd", sST, "p1_stxb_62")
            E.wait("gpsimd", sST, "p1_stxb_63")
            gpsimd.collective_compute(
                "AllGather", ALU.bypass, replica_groups=GROUPS,
                ins=[xcb_d.opt()], outs=[xcb_g.opt()]).then_inc(cc1, 1)
            gpsimd.collective_compute(
                "AllGather", ALU.bypass, replica_groups=GROUPS,
                ins=[dtr_d.opt()], outs=[dtr_g.opt()]).then_inc(cc2, 1)
            gpsimd.collective_compute(
                "AllGather", ALU.bypass, replica_groups=GROUPS,
                ins=[bc_d.opt()], outs=[bc_g.opt()]).then_inc(cc3, 1)

        def emit_A2(G, T):
            sync = G["sync"]; vector = G["vector"]; scalar = G["scalar"]
            tensor = G["tensor"]; gpsimd = G["gpsimd"]

            ns_wc = nc.named_scope("a2wc"); ns_wc.__enter__()
            for it in range(2):
                if it >= 1:
                    E.wait("sync", sST, "a2_wst_%d" % (it - 1))
                sync.dma_start(out=T.woc32[0][:],
                               in_=wdtT_in[it * P:(it + 1) * P, :]
                               ).then_inc(SS["sWT"], 16)
                E.inc(SS["sWT"], 16)
                E.mark(SS["sWT"], "a2_wld_%d" % it)
                E.wait("gpsimd", sLD, "a2_wld_%d" % it)
                if it >= 1:
                    E.wait("gpsimd", sST, "a2_wst_%d" % (it - 1))
                gpsimd.tensor_copy(T.woc16[0][:], T.woc32[0][:]
                                   ).then_inc(sPL, 1)
                E.inc(sPL)
                E.mark(sPL, "a2_wc_%d" % it)
                E.wait("scalar", sPL, "a2_wc_%d" % it)
                scalar.dma_start(out=wdt_scr[it], in_=T.woc16[0][:]
                                 ).then_inc(SS["sWS"], 16)
                E.inc(SS["sWS"], 16)
                E.mark(SS["sWS"], "a2_wst_%d" % it)

            ns_wc.__exit__(None, None, None)
            ns_ffn = nc.named_scope("ffn"); ns_ffn.__enter__()
            E.wait("sync", sST, "dq_st_31")
            sync.dma_start(out=T.fbrow[:], in_=fbias_row
                           ).then_inc(SS["sCN"], 16)
            E.inc(SS["sCN"], 16)
            E.mark(SS["sCN"], "ffn_fb")
            E.wait("tensor", sLD, "ffn_fb")
            E.wait("tensor", sPL, "idb")
            E.wait("tensor", sAC, "p2_ssm")
            for og in range(8):
                inst = tensor.matmul(psP[og][:], ones1[:],
                                     T.fbrow[0:1, og * 512:(og + 1) * 512],
                                     start=True, stop=True)
            inst.then_inc(sTE, 1)
            E.inc(sTE)
            E.mark(sTE, "ffn_fbmm")
            E.wait("vector", sTE, "ffn_fbmm")
            for og in range(8):
                inst = vector.tensor_copy(T.fb_bc[:, og * 512:(og + 1) * 512],
                                          psP[og][:])
            inst.then_inc(sDV, 1)
            E.inc(sDV)
            E.mark(sDV, "ffn_fbbc")
            for og in range(8):
                for ic in range(32):
                    it = og * 32 + ic
                    if it >= 4:
                        E.wait("sync", sTE, "ffn_mm_%d" % (it - 4))
                    tsem = SS["t%d" % (ic % 4)]
                    sync.dma_start(
                        out=T.wTr[ic % 4][:],
                        in_=wffn_scr[og * 512:(og + 1) * 512,
                                     ic * P:(ic + 1) * P],
                        transpose=True).then_inc(tsem, 16)
                    E.inc(tsem, 16)
                    E.mark(tsem, "ffn_ld_%d" % it)
                    E.wait("tensor", sLD, "ffn_ld_%d" % it)
                    if ic == 0:
                        E.wait("tensor", sDV, "ffn_fbbc")
                        if og >= 1:
                            E.wait("tensor", sAC, "ffn_ev_%d" % (og - 1))
                    for tt in range(8):
                        inst = tensor.matmul(
                            psP[tt][:],
                            xnT[:, ic * TEXT + 3 + tt * P:
                                ic * TEXT + 3 + (tt + 1) * P],
                            T.wTr[ic % 4][:],
                            start=(ic == 0), stop=(ic == 31))
                    inst.then_inc(sTE, 1)
                    E.inc(sTE)
                    E.mark(sTE, "ffn_mm_%d" % it)
                for tt in range(8):
                    E.wait("vector", sTE, "ffn_mm_%d" % (og * 32 + 31))
                    if og >= 1:
                        E.wait("vector", sAC, "ffn_ev_%d" % (og - 1))
                    if og > 0 or tt >= 2:
                        pog, ptt = (og, tt - 2) if tt >= 2 else (og - 1, 6 + tt)
                        E.wait("vector", sAC, "ffn_sl_%d_%d" % (pog, ptt))
                    inst = vector.scalar_tensor_tensor(
                        T.gbuf[tt % 2][:], psP[tt][:], 1.0,
                        T.fb_bc[:, og * 512:(og + 1) * 512],
                        ALU.mult, ALU.add)
                    inst.then_inc(sDV, 1)
                    E.inc(sDV)
                    E.mark(sDV, "ffn_gb_%d_%d" % (og, tt))
                    E.wait("scalar", sDV, "ffn_gb_%d_%d" % (og, tt))
                    if og >= 1 or tt >= 2:
                        pog, ptt = (og, tt - 2) if tt >= 2 else (og - 1, 6 + tt)
                        E.wait("scalar", sST, "ffn_st_%d_%d" % (pog, ptt))
                    inst = scalar.activation(T.gtt[tt % 2][:], T.gbuf[tt % 2][:],
                                             AF.Silu)
                    inst.then_inc(sAC, 1)
                    E.inc(sAC)
                    E.mark(sAC, "ffn_sl_%d_%d" % (og, tt))
                    E.wait("scalar", sAC, "ffn_sl_%d_%d" % (og, tt))
                    scalar.dma_start(
                        out=gt_d[tt * P:(tt + 1) * P, og * 512:(og + 1) * 512],
                        in_=T.gtt[tt % 2][:]).then_inc(SS["cgt%d" % (tt % 2)], 16)
                    E.inc(SS["cgt%d" % (tt % 2)], 16)
                    E.mark(SS["cgt%d" % (tt % 2)], "ffn_st_%d_%d" % (og, tt))
                E.mark(sAC, "ffn_ev_%d" % og)

            ns_ffn.__exit__(None, None, None)

        def emit_B(G, T):
            sync = G["sync"]; vector = G["vector"]; scalar = G["scalar"]
            tensor = G["tensor"]; gpsimd = G["gpsimd"]

            E.wait("sync", cc2, "never")


        class TensA:
            pass

        with ExitStack() as xnS:
            xnT = xnS.enter_context(nc.sbuf_tensor("xnT", [P, 32 * TEXT], BF16))

            with ExitStack() as sA:
                def sba(name, shape, dt=F32):
                    return sA.enter_context(nc.sbuf_tensor(name, shape, dt))
                T = TensA()
                T.hs_t = [sba("hs_t%d" % i, [P, H]) for i in range(2)]
                T.xn_t = sba("xn_t", [P, H])
                T.wbuf = [sba("wbuf%d" % i, [P, H]) for i in range(2)]
                T.sq_t = T.wbuf[0]
                T.wbb = [sba("wbb%d" % i, [P, H], BF16) for i in range(2)]
                T.xsb = [sba("xsb%d" % i, [P, TEXT], BF16) for i in range(2)]
                T.diag = [sba("diag%d" % i, [P, 4 * P], BF16) for i in range(2)]
                T.xc32 = [sba("xc32%d" % i, [P, TLOC]) for i in range(2)]
                T.xcb16 = [sba("xcb16%d" % i, [P, TLOC], BF16) for i in range(2)]
                T.sz32 = [sba("sz32%d" % i, [P, TLOC], BF16) for i in range(2)]
                T.xcbL = [sba("xcbL%d" % i, [P, TLOC], BF16) for i in range(2)]
                T.wxp = [sba("wxp%d" % i, [P, 288]) for i in range(2)]
                T.wxpb = [sba("wxpb%d" % i, [P, 288], BF16) for i in range(2)]
                T.ssm = T.hs_t[0][:, 0:2304]
                T.ssq = T.wbuf[0][:, 0:2304]
                T.dtrb = T.hs_t[1][:, 0:2048]
                T.bmb = T.hs_t[1][:, 2048:2176]
                T.cmb = T.hs_t[1][:, 2176:2304]
                T.dtrT = sba("dtrT", [P, 2 * TLOC], BF16)
                T.qwt = [T.hs_t[0][:].bitcast(I32),
                         T.hs_t[1][:].bitcast(I32)]
                T.wdq = T.xn_t[:].bitcast(BF16)[:, 0:H]
                T.scz = sba("scz", [P, 128])
                T.bmT = T.xcbL[0][0:16, :]
                T.cmT = T.xcbL[1][0:16, :]
                T.cw_l = cw_sb[:]
                T.cb_l = cb_sb[:]
                T.dtb_l = dtb_sb[:]
                T.dp_l = dp_sb[:]
                T.fb_l = fb_sb[:]
                T.wln_l = wln_sb[:]
                T.id_l = ident_f[:]
                T.al_l = alog_sb[:]

                mock = {e: _MockEng() for e in ENGINES}
                E.start_pass(mock)
                emit_A(mock, T)
                snapshot_ctr_A = dict(E.ctr)
                E.end_pass()

                with nc.Block() as block:
                    for engname in ENGINES:
                        def mk(engname):
                            def body(eng):
                                engs = {e: _MockEng() for e in ENGINES}
                                engs[engname] = eng
                                E.start_pass(engs)
                                emit_A(engs, T)
                                assert E.ctr == snapshot_ctr_A
                            return body
                        getattr(block, engname)(mk(engname))
                E.ctr = dict(snapshot_ctr_A)

            if "B" not in PHASES and False:
                pass
            with ExitStack() as sA2:
                def sba2(name, shape, dt=F32):
                    return sA2.enter_context(nc.sbuf_tensor(name, shape, dt))
                T2 = TensA()
                T2.woc32 = [sba2("woc32", [P, DIN])]
                T2.woc16 = [sba2("woc16", [P, DIN], BF16)]
                T2.wTr = [sba2("wTr%d" % i, [P, 512], BF16) for i in range(4)]
                T2.gbuf = [sba2("gbuf%d" % i, [P, 512]) for i in range(2)]
                T2.gtt = [sba2("gtt%d" % i, [P, 512], BF16) for i in range(2)]
                T2.fb_bc = sba2("fb_bc", [P, H])
                T2.fbrow = sba2("fbrow", [1, H])

                base_ctr = dict(E.ctr)
                E.start_pass({e: _MockEng() for e in ENGINES})
                E.ctr = dict(base_ctr)
                emit_A2({e: _MockEng() for e in ENGINES}, T2)
                snap2 = dict(E.ctr)
                E.end_pass()

                with nc.Block() as block:
                    for engname in ENGINES:
                        def mk2(engname):
                            def body(eng):
                                engs = {e: _MockEng() for e in ENGINES}
                                engs[engname] = eng
                                E.start_pass(engs)
                                E.ctr = dict(base_ctr)
                                emit_A2(engs, T2)
                                assert E.ctr == snap2
                            return body
                        getattr(block, engname)(mk2(engname))
                E.ctr = dict(snap2)

        def emit_Bv2(G, T):
            sync = G["sync"]; vector = G["vector"]; scalar = G["scalar"]
            tensor = G["tensor"]; gpsimd = G["gpsimd"]

            scalar.activation(adev_sb[:], alog_sb[:], AF.Exp).then_inc(sAC, 1)
            E.inc(sAC)
            E.mark(sAC, "b_aexp")
            E.wait("vector", sAC, "b_aexp")
            vector.tensor_scalar(adev_sb[:], adev_sb[:], -1.0, None, ALU.mult
                                 ).then_inc(sDV, 1)
            E.inc(sDV)
            E.mark(sDV, "b_adev")

            E.wait_val("sync", cc2, 1)
            for rc in range(2):
                for rk in range(2):
                    sync.dma_start(
                        out=T.dtrg[rc][:, rk * TLOC:(rk + 1) * TLOC],
                        in_=dtr_g[rk * RK + rc * P:rk * RK + (rc + 1) * P, :]
                        ).then_inc(SS["sCN"], 16)
                    E.inc(SS["sCN"], 16)
            E.wait_val("sync", cc3, 1)
            for rk in range(2):
                sync.dma_start(out=T.bmTg[:, rk * TLOC:(rk + 1) * TLOC],
                               in_=bc_g[rk * 32:rk * 32 + 16, :]
                               ).then_inc(SS["sCN"], 16)
                sync.dma_start(out=T.cmTg[:, rk * TLOC:(rk + 1) * TLOC],
                               in_=bc_g[rk * 32 + 16:rk * 32 + 32, :]
                               ).then_inc(SS["sCN"], 16)
                E.inc(SS["sCN"], 32)
            sync.dma_start(out=T.eself[:], in_=esel_in
                           ).then_inc(SS["sCN"], 16)
            E.inc(SS["sCN"], 16)
            E.mark(SS["sCN"], "b_pro")
            E.wait("gpsimd", sLD, "b_pro")
            gpsimd.tensor_copy(T.eselb[:], T.eself[:]).then_inc(sPL, 1)
            E.inc(sPL)
            E.mark(sPL, "b_esel")

            ns_bc = nc.named_scope("brep"); ns_bc.__enter__()
            for n in range(NH):
                for half, (srcT, dstR) in enumerate(
                        ((T.bmTg, T.B_rep), (T.cmTg, T.C_rep))):
                    for tc in range(4):
                        idx = n * 8 + half * 4 + tc
                        pb = psP[idx % 2]
                        E.wait("tensor", sPL, "b_esel")
                        E.wait("tensor", sLD, "b_pro")
                        if idx == 0:
                            E.wait("tensor", sAC, "ffn_ev_7")
                        if idx >= 2:
                            E.wait("tensor", sAC, "b_bc_%d" % (idx - 2))
                        tensor.matmul(pb[:],
                                      T.eselb[:, n * P:(n + 1) * P],
                                      srcT[:, tc * 512:(tc + 1) * 512],
                                      start=True, stop=True).then_inc(sTE, 1)
                        E.inc(sTE)
                        E.mark(sTE, "b_bcm_%d" % idx)
                        E.wait("scalar", sTE, "b_bcm_%d" % idx)
                        scalar.activation(
                            dstR[:, n * TFULL + tc * 512:
                                 n * TFULL + (tc + 1) * 512],
                            pb[:], AF.Copy).then_inc(sAC, 1)
                        E.inc(sAC)
                        E.mark(sAC, "b_bc_%d" % idx)

            ns_bc.__exit__(None, None, None)
            ns_sc = nc.named_scope("scan"); ns_sc.__enter__()
            for db in range(64):
                par = db % 2
                E.wait_val("sync", cc1, 1)
                if db >= 1:
                    E.wait("sync", sDV, "b_u_%d" % (db - 1))
                for rk in range(2):
                    inst = sync.dma_start(
                        out=T.xcg[0][:, rk * TLOC:(rk + 1) * TLOC],
                        in_=xcb_g[rk * DIN + db * P:rk * DIN + (db + 1) * P, :])
                    inst.then_inc(SS["sXG"], 16)
                    E.inc(SS["sXG"], 16)
                E.mark(SS["sXG"], "b_xcg_%d" % db)
                if db == 0:
                    E.wait("sync", sST, "a2_wst_1")
                if db >= 2:
                    E.wait("sync", sTE, "b_dtmm_%d" % ((db - 2) * 4 + 3))
                for rc in range(2):
                    inst = sync.dma_start(
                        out=T.wdtr[par][:, rc * P:(rc + 1) * P],
                        in_=wdt_scr[rc][:, db * P:(db + 1) * P])
                    inst.then_inc(SS["wd%d" % par], 16)
                    E.inc(SS["wd%d" % par], 16)
                E.mark(SS["wd%d" % par], "b_wdt_%d" % db)
                for tc in range(4):
                    idx = db * 4 + tc
                    E.wait("tensor", sLD, "b_wdt_%d" % db)
                    E.wait("tensor", sLD, "b_pro")
                    if idx >= 2:
                        E.wait("tensor", sAC, "b_spexp_%d" % (idx - 2))
                    if db == 0 and tc == 0:
                        E.wait("tensor", sAC, "b_bc_%d" % (NH * 8 - 1))
                    pb = psP[tc % 2]
                    for rc in range(2):
                        inst = tensor.matmul(
                            pb[:],
                            T.wdtr[par][:, rc * P:(rc + 1) * P],
                            T.dtrg[rc][:, tc * 512:(tc + 1) * 512],
                            start=(rc == 0), stop=(rc == 1))
                    inst.then_inc(sTE, 1)
                    E.inc(sTE)
                    E.mark(sTE, "b_dtmm_%d" % idx)
                    E.wait("scalar", sTE, "b_dtmm_%d" % idx)
                    if db >= 1 and tc == 0:
                        E.wait("scalar", sDV, "b_u_%d" % (db - 1))
                    scalar.activation(T.etmp[tc % 2][:], psP[tc % 2][:],
                                      AF.Exp, bias=dtb_sb[:, db:db + 1]
                                      ).then_inc(sAC, 1)
                    E.inc(sAC)
                    E.mark(sAC, "b_spexp_%d" % idx)
                    if tc >= 1:
                        scalar.activation(
                            T.dtT[0][:, (tc - 1) * 512:tc * 512],
                            T.etmp[(tc - 1) % 2][:], AF.Ln, bias=1.0
                            ).then_inc(sAC, 1)
                        E.inc(sAC)
                        E.mark(sAC, "b_spln_%d" % (idx - 1))
                scalar.activation(T.dtT[0][:, 3 * 512:4 * 512],
                                  T.etmp[1][:], AF.Ln, bias=1.0
                                  ).then_inc(sAC, 1)
                E.inc(sAC)
                E.mark(sAC, "b_spln_%d" % (db * 4 + 3))
                E.mark(sAC, "b_dtT_%d" % db)
                E.wait("vector", sAC, "b_dtT_%d" % db)
                E.wait("vector", sLD, "b_xcg_%d" % db)
                vector.scalar_tensor_tensor(T.u[0][:], T.dtT[0][:], 1.0,
                                            T.xcg[0][:], ALU.mult, ALU.mult
                                            ).then_inc(sDV, 1)
                E.inc(sDV)
                E.mark(sDV, "b_u_%d" % db)

                for np_ in range(NH // 2):
                    for j in range(2):
                        n = np_ * 2 + j
                        npar = n % 3
                        E.wait("scalar", sAC, "b_dtT_%d" % db)
                        if db > 0 or n >= 3:
                            pdb, pn = (db, n - 3) if n >= 3 else                                 (db - 1, NH - 3 + n)
                            E.wait("scalar", sDV, "b_scan_%d_%d" % (pdb, pn))
                        scalar.activation(
                            T.dA[npar][:], T.dtT[0][:], AF.Exp,
                            scale=adev_sb[:, db * NH + n:db * NH + n + 1]
                            ).then_inc(sAC, 1)
                        E.inc(sAC)
                        E.mark(sAC, "b_dA_%d_%d" % (db, n))
                    for j in range(2):
                        n = np_ * 2 + j
                        dpar = n % 3
                        if db > 0 or n >= 3:
                            pdb, pn = (db, n - 3) if n >= 3 else                                 (db - 1, NH - 3 + n)
                            E.wait("vector", sDV, "b_scan_%d_%d" % (pdb, pn))
                        vector.tensor_tensor(
                            T.dBx[dpar][:], T.u[0][:],
                            T.B_rep[:, n * TFULL:(n + 1) * TFULL], ALU.mult
                            ).then_inc(sDV, 1)
                        E.inc(sDV)
                    for j in range(2):
                        n = np_ * 2 + j
                        npar = n % 3
                        dpar = n % 3
                        E.wait("vector", sAC, "b_dA_%d_%d" % (db, n))
                        if n == 0 and db >= 2:
                            E.wait("vector", sTE, "b_ys_%d" % (db - 2))
                        if n == 6 and db >= 2:
                            E.wait("vector", sST, "b_yst_%d" % (db - 2))
                        vector.tensor_tensor_scan(
                            T.hbig[par][:, n * TFULL:(n + 1) * TFULL],
                            T.dA[npar][:], T.dBx[dpar][:], 0.0,
                            ALU.mult, ALU.add).then_inc(sDV, 1)
                        E.inc(sDV)
                        E.mark(sDV, "b_scan_%d_%d" % (db, n))
                vector.tensor_tensor(T.hbig[par][:], T.hbig[par][:],
                                     T.C_rep[:], ALU.mult).then_inc(sDV, 1)
                E.inc(sDV)
                E.mark(sDV, "b_ch_%d" % db)
                E.wait("tensor", sDV, "b_ch_%d" % db)
                if db >= 1:
                    E.wait("tensor", sAC, "b_yev_%d" % (db - 1))
                for tc in range(4):
                    for n in range(NH):
                        inst = tensor.matmul(
                            psP[4 + tc][:], ident_b[:],
                            T.hbig[par][:, n * TFULL + tc * 512:
                                        n * TFULL + (tc + 1) * 512],
                            start=(n == 0), stop=(n == NH - 1))
                inst.then_inc(sTE, 1)
                E.inc(sTE)
                E.mark(sTE, "b_ys_%d" % db)
                E.wait("scalar", sTE, "b_ys_%d" % db)
                if db >= 1:
                    E.wait("scalar", sST, "b_yst_%d" % (db - 1))
                for tc in range(4):
                    inst = scalar.activation(
                        T.ysbv[par][:, tc * 512:(tc + 1) * 512],
                        psP[4 + tc][:], AF.Copy)
                inst.then_inc(sAC, 1)
                E.inc(sAC)
                E.mark(sAC, "b_yev_%d" % db)
                E.wait("scalar", sAC, "b_yev_%d" % db)
                for rk in range(2):
                    inst = scalar.dma_start(
                        out=yp_d[rk * DIN + db * P:rk * DIN + (db + 1) * P, :],
                        in_=T.ysbv[par][:, rk * TLOC:(rk + 1) * TLOC])
                    inst.then_inc(SS["cyb"], 16)
                    E.inc(SS["cyb"], 16)
                E.mark(SS["cyb"], "b_yst_%d" % db)
            ns_sc.__exit__(None, None, None)
            E.wait("gpsimd", sST, "b_yst_63")
            gpsimd.collective_compute(
                "ReduceScatter", ALU.add, replica_groups=GROUPS,
                ins=[yp_d.opt()], outs=[yown_d.opt()]).then_inc(cc4, 1)

        with ExitStack() as sB:
            def sbb(name, shape, dt=F32):
                return sB.enter_context(nc.sbuf_tensor(name, shape, dt))
            TB = TensA()
            TB.dtrg = [sbb("dtrg%d" % i, [P, TFULL], BF16) for i in range(2)]

            TB.eselb = sbb("eselb", [16, NH * P], BF16)
            TB.wdtr = [sbb("wdtr%d" % i, [P, 2 * P], BF16) for i in range(2)]
            TB.B_rep = sbb("B_rep", [P, NH * TFULL], BF16)
            TB.C_rep = sbb("C_rep", [P, NH * TFULL], BF16)
            TB.xcg = [sbb("xcg0", [P, TFULL], BF16)]
            TB.dtT = [sbb("dtT0", [P, TFULL])]
            TB.eself = TB.dtT[0][0:16, 0:NH * P]
            TB.etmp = [sbb("etmp%d" % i, [P, 512]) for i in range(2)]
            TB.u = [sbb("u", [P, TFULL], BF16)]
            TB.dA = [sbb("dA%d" % i, [P, TFULL]) for i in range(3)]
            TB.dBx = [sbb("dBx%d" % i, [P, TFULL], BF16) for i in range(3)]
            TB.hbig = [sbb("hbig%d" % i, [P, NH * TFULL], BF16) for i in range(2)]
            TB.bmTg = TB.hbig[0][0:16, 0:TFULL]
            TB.cmTg = TB.hbig[0][0:16, TFULL:2 * TFULL]
            TB.ysbv = [TB.hbig[0][:, 6 * TFULL:7 * TFULL],
                       TB.hbig[1][:, 6 * TFULL:7 * TFULL]]
            TB.ysb = [None]

            if "B" in PHASES:
                base_ctr = dict(E.ctr)
                E.start_pass({e: _MockEng() for e in ENGINES})
                E.ctr = dict(base_ctr)
                emit_Bv2({e: _MockEng() for e in ENGINES}, TB)
                snapB = dict(E.ctr)
                E.end_pass()
                with nc.Block() as block:
                    for engname in ENGINES:
                        def mkB(engname):
                            def body(eng):
                                engs = {e: _MockEng() for e in ENGINES}
                                engs[engname] = eng
                                E.start_pass(engs)
                                E.ctr = dict(base_ctr)
                                emit_Bv2(engs, TB)
                                assert E.ctr == snapB
                            return body
                        getattr(block, engname)(mkB(engname))
                E.ctr = dict(snapB)

        def emit_C(G, T):
            sync = G["sync"]; vector = G["vector"]; scalar = G["scalar"]
            tensor = G["tensor"]; gpsimd = G["gpsimd"]

            ns_g = nc.named_scope("gate"); ns_g.__enter__()
            for db in (range(64) if "g" in CPARTS else []):
                par = db % 2
                E.wait_val("sync", cc4, 1)
                if db == 0:
                    E.wait("sync", sST, "p1_stxb_62")
                    E.wait("sync", sST, "p1_stxb_63")
                    E.wait("sync", sST, "p1_stz_126")
                    E.wait("sync", sST, "p1_stz_127")
                if db >= 2:
                    E.wait("sync", sDV, "c_yg_%d" % (db - 2))
                gsem = SS["g%d" % par]
                sync.dma_start(out=T.yown[par][:],
                               in_=yown_d[db * P:(db + 1) * P, :]
                               ).then_inc(gsem, 16)
                sync.dma_start(out=T.xcL[par][:],
                               in_=xcb_d[db * P:(db + 1) * P, :]
                               ).then_inc(gsem, 16)
                sync.dma_start(out=T.szL[par][:],
                               in_=szt_d[db * P:(db + 1) * P, :]
                               ).then_inc(gsem, 16)
                E.inc(gsem, 48)
                E.mark(gsem, "c_ld_%d" % db)
                E.wait("vector", sLD, "c_ld_%d" % db)
                vector.scalar_tensor_tensor(T.t1[par][:], T.xcL[par][:],
                                            dp_sb[:, db:db + 1],
                                            T.yown[par][:], ALU.mult, ALU.add)
                vector.scalar_tensor_tensor(
                    T.ygate[:, db * TLOC:(db + 1) * TLOC],
                    T.t1[par][:], 1.0, T.szL[par][:], ALU.mult, ALU.mult
                    ).then_inc(sDV, 1)
                E.inc(sDV)
                E.mark(sDV, "c_yg_%d" % db)

            ns_g.__exit__(None, None, None)
            ns_op = nc.named_scope("oproj"); ns_op.__enter__()
            for hc in (range(32) if "o" in CPARTS else []):
                for ck in range(2):
                    it = hc * 2 + ck
                    if it >= 2:
                        E.wait("sync", sAC, "c_wc_%d" % (it - 2))
                    osem = SS["o%d" % ck]
                    for hv in range(2):
                        inst = sync.dma_start(
                            out=T.wo2[ck][:, hv * 2048:(hv + 1) * 2048],
                            in_=woutT_t[hc][:, ck * 4096 + hv * 2048:
                                            ck * 4096 + (hv + 1) * 2048])
                        inst.then_inc(osem, 16)
                        E.inc(osem, 16)
                    E.mark(osem, "c_wo_%d" % it)
                    E.wait("scalar", sLD, "c_wo_%d" % it)
                    if it >= 2:
                        E.wait("scalar", sTE, "c_mm_%d_%d" % (hc - 1, ck))
                    scalar.activation(T.wo2b[ck][:], T.wo2[ck][:], AF.Copy
                                      ).then_inc(sAC, 1)
                    E.inc(sAC)
                    E.mark(sAC, "c_wc_%d" % it)
                for ck in range(2):
                    E.wait("tensor", sAC, "c_wc_%d" % (hc * 2 + ck))
                    if ck == 0:
                        E.wait("tensor", sDV, "c_yg_63")
                        if hc >= 1:
                            E.wait("tensor", sAC, "c_sev_%d" % (hc - 1))
                    for dcl in range(32):
                        dc = ck * 32 + dcl
                        for tc in range(2):
                            inst = tensor.matmul(
                                psP[6 + tc][:],
                                T.wo2b[ck][:, dcl * P:(dcl + 1) * P],
                                T.ygate[:, dc * TLOC + tc * 512:
                                        dc * TLOC + (tc + 1) * 512],
                                start=(dc == 0), stop=(dc == 63))
                    inst.then_inc(sTE, 1)
                    E.inc(sTE)
                    E.mark(sTE, "c_mm_%d_%d" % (hc, ck))
                E.mark(sTE, "c_mm_%d" % hc)
                E.wait("scalar", sTE, "c_mm_%d_1" % hc)
                if hc >= 1:
                    E.wait("scalar", sST, "c_sst_%d" % (hc - 1))
                scalar.activation(T.stt_[0][:, 0:512], psP[6][:], AF.Copy)
                scalar.activation(T.stt_[0][:, 512:1024], psP[7][:],
                                  AF.Copy).then_inc(sAC, 1)
                E.inc(sAC)
                E.mark(sAC, "c_sev_%d" % hc)
                E.wait("scalar", sAC, "c_sev_%d" % hc)
                scalar.dma_start(out=st_d[hc * P:(hc + 1) * P, :],
                                 in_=T.stt_[0][:]).then_inc(SS["cst"], 16)
                E.inc(SS["cst"], 16)
                E.mark(SS["cst"], "c_sst_%d" % hc)

            ns_op.__exit__(None, None, None)
            ns_fin = nc.named_scope("fin"); ns_fin.__enter__()
            for tt in (range(8) if "f" in CPARTS else []):
                par = tt % 2
                E.wait("sync", sST, "c_sst_31")
                E.wait("sync", sST, "ffn_st_7_6")
                E.wait("sync", sST, "ffn_st_7_7")
                E.wait("sync", sTE, "c_mm_31_1")
                if tt >= 2:
                    E.wait("sync", sDV, "c_out_%d" % (tt - 2))
                fsem = SS["f%d" % par]
                sync.dma_start(out=T.s_tok[par][:],
                               in_=st_d[:, tt * P:(tt + 1) * P],
                               transpose=True).then_inc(fsem, 16)
                sync.dma_start(out=T.g_tok[par][:],
                               in_=gt_d[tt * P:(tt + 1) * P, :]
                               ).then_inc(fsem, 16)
                sync.dma_start(out=T.xrs[par][:],
                               in_=hs_ext[3 + tt * P:3 + (tt + 1) * P, :]
                               ).then_inc(fsem, 16)
                E.inc(fsem, 48)
                E.mark(fsem, "c_fin_%d" % tt)
                E.wait("vector", sLD, "c_fin_%d" % tt)
                if tt >= 2:
                    E.wait("vector", sST, "c_ost_%d" % (tt - 2))
                vector.tensor_tensor(T.sg[par][:], T.s_tok[par][:],
                                     T.g_tok[par][:], ALU.mult)
                vector.scalar_tensor_tensor(T.outt[par][:], T.sg[par][:],
                                            1.0, T.xrs[par][:],
                                            ALU.mult, ALU.add).then_inc(sDV, 1)
                E.inc(sDV)
                E.mark(sDV, "c_out_%d" % tt)
                E.wait("scalar", sDV, "c_out_%d" % tt)
                scalar.dma_start(out=out_ext[tt * P:(tt + 1) * P, :],
                                 in_=T.outt[par][:]
                                 ).then_inc(SS["cot%d" % par], 16)
                E.inc(SS["cot%d" % par], 16)
                E.mark(SS["cot%d" % par], "c_ost_%d" % tt)
            if DEBUG_TAPS:
                for nm, srcd in [("d_xct", xct_d), ("d_szt", szt_d),
                                 ("d_dtr", dtr_d), ("d_bc", bc_d),
                                 ("d_yown", yown_d), ("d_gt", gt_d),
                                 ("d_st", st_d), ("d_yp", yp_d),
                                 ("d_wf", wffn_scr),
                                 ("d_xcbg", xcb_g)]:
                    scalar.dma_start(out=dbg[nm], in_=srcd).then_inc(sST, 16)
                    E.inc(sST, 16)
            ns_fin.__exit__(None, None, None)
            if "f" in CPARTS:
                E.wait("scalar", sST, "c_ost_6")
                E.wait("scalar", sST, "c_ost_7")

        with ExitStack() as sC:
            def sbc(name, shape, dt=F32):
                return sC.enter_context(nc.sbuf_tensor(name, shape, dt))
            TC = TensA()
            TC.ygate = sbc("ygate", [P, 64 * TLOC], BF16)
            TC.yown = [sbc("yown%d" % i, [P, TLOC], BF16) for i in range(2)]
            TC.t1 = [sbc("t1%d" % i, [P, TLOC]) for i in range(2)]
            TC.xcL = [sbc("xcL%d" % i, [P, TLOC], BF16) for i in range(2)]
            TC.szL = [sbc("szL%d" % i, [P, TLOC], BF16) for i in range(2)]

            TC.wo2 = [sbc("wo2_%d" % i, [P, 32 * P]) for i in range(2)]
            TC.wo2b = [sbc("wo2b_%d" % i, [P, 32 * P], BF16) for i in range(2)]
            TC.stt_ = [sbc("stt_0", [P, TLOC], BF16)]
            ygf = TC.ygate[:].bitcast(F32)
            TC.xrs = [ygf[:, 0:H], ygf[:, H:2 * H]]
            TC.outt = [ygf[:, 2 * H:3 * H], ygf[:, 3 * H:4 * H]]
            TC.s_tok = [TC.ygate[:, 8 * H:9 * H], TC.ygate[:, 9 * H:10 * H]]
            TC.g_tok = [TC.ygate[:, 10 * H:11 * H], TC.ygate[:, 11 * H:12 * H]]
            TC.sg = [TC.ygate[:, 12 * H:13 * H], TC.ygate[:, 13 * H:14 * H]]

            if "C" in PHASES:
                base_ctr = dict(E.ctr)
                E.start_pass({e: _MockEng() for e in ENGINES})
                E.ctr = dict(base_ctr)
                emit_C({e: _MockEng() for e in ENGINES}, TC)
                snapC = dict(E.ctr)
                E.end_pass()
                with nc.Block() as block:
                    for engname in ENGINES:
                        def mkC(engname):
                            def body(eng):
                                engs = {e: _MockEng() for e in ENGINES}
                                engs[engname] = eng
                                E.start_pass(engs)
                                E.ctr = dict(base_ctr)
                                emit_C(engs, TC)
                                assert E.ctr == snapC
                            return body
                        getattr(block, engname)(mkC(engname))

    return nc


def _stage_inputs(inputs):
    hs = inputs["L0_attn_hidden_states"]
    win = inputs["L0_attn_in_proj_w"]
    convw = inputs["L0_attn_conv_w"]
    convb = inputs["L0_attn_conv_b"]
    xpw = inputs["L0_attn_x_proj_w"]
    dtw = inputs["L0_attn_dt_proj_w"]
    dtb = inputs["L0_attn_dt_bias"]
    alog = inputs["L0_attn_A_log"]
    dpar = inputs["L0_attn_D_param"]
    wout = inputs["L0_attn_out_proj_w"]
    dtln = inputs["L0_attn_dt_ln_w"]
    bln = inputs["L0_attn_B_ln_w"]
    cln = inputs["L0_attn_C_ln_w"]
    qw = inputs["L0_ffn_qweight"]
    scl = inputs["L0_ffn_scales"]
    zrs = inputs["L0_ffn_scaled_zeros"]
    fbias = inputs["L0_ffn_bias"]

    f32 = np.float32
    win_t = np.ascontiguousarray(
        win.reshape(32, P, 128, P).transpose(2, 1, 0, 3)
    ).reshape(128, P, 32 * P).astype(f32)
    woT = np.ascontiguousarray(wout.T)
    woutT_t = np.ascontiguousarray(
        woT.reshape(64, P, 32, P).transpose(2, 1, 0, 3)
    ).reshape(32, P, 64 * P).astype(f32)
    wdtT = np.ascontiguousarray(dtw.T).astype(f32)
    wxpT_t = np.ascontiguousarray(xpw.T).reshape(64, P, 288).astype(f32)
    qw2d = np.ascontiguousarray(qw.reshape(H, H)).astype(np.int32)
    scales2 = np.ascontiguousarray(scl.reshape(H, 32)).astype(f32)
    zeros2 = np.ascontiguousarray(zrs.reshape(H, 32)).astype(f32)
    cw4 = np.ascontiguousarray(
        convw.reshape(64, P, 4).transpose(1, 0, 2)).reshape(P, 256).astype(f32)
    cb1 = np.ascontiguousarray(convb.reshape(64, P).T).astype(f32)
    dtb1 = np.ascontiguousarray(dtb.reshape(64, P).T).astype(f32)
    dp1 = np.ascontiguousarray(dpar.reshape(64, P).T).astype(f32)
    fb = np.ascontiguousarray(np.asarray(fbias).reshape(32, P).T).astype(f32)
    fb_row = np.ascontiguousarray(np.asarray(fbias).reshape(1, H)).astype(f32)
    wln = np.concatenate([dtln, bln, cln]).reshape(1, 288).astype(f32)
    ident = np.eye(P, dtype=f32)

    shared = dict(win_t=win_t, woutT_t=woutT_t, wdtT_in=wdtT, wxpT_t=wxpT_t,
                  qw2d=qw2d, scales2=scales2, zeros2=zeros2, cw4=cw4, cb1=cb1,
                  dtb1=dtb1, dp1=dp1, fbias_in=fb, fbias_row=fb_row,
                  wln_in=wln, ident_in=ident)

    in_maps = []
    for c in range(8):
        b, hf = c // 2, c % 2
        t0 = hf * TLOC
        if hf == 0:
            halo = np.zeros((3, H), dtype=f32)
        else:
            halo = hs[b, t0 - 3:t0, :].astype(f32)
        hs_ext = np.ascontiguousarray(
            np.concatenate([halo, hs[b, t0:t0 + TLOC, :].astype(f32)], axis=0))
        alog_l = np.ascontiguousarray(
            alog.reshape(64, P, 16)[:, :, hf * NH:(hf + 1) * NH]
            .transpose(1, 0, 2)).reshape(P, 64 * NH).astype(f32)
        esel = np.zeros((16, NH, P), dtype=f32)
        for n in range(NH):
            esel[hf * NH + n, n, :] = 1.0
        esel = esel.reshape(16, NH * P)
        m = dict(shared)
        m.update(hs_ext=hs_ext, alog_l=alog_l, esel_in=esel)
        in_maps.append(m)
    return in_maps


_NC_CACHE = {}


def kernel(**inputs):
    in_maps = _stage_inputs(inputs)
    if "nc" not in _NC_CACHE:
        _NC_CACHE["nc"] = build_program()
    nc = _NC_CACHE["nc"]
    res = run_bass_kernel_spmd(nc, in_maps, core_ids=list(range(8)))
    outs = [res.results[c]["out_ext"] for c in range(8)]
    full = np.concatenate(outs, axis=0).reshape(4, 2048, H).astype(np.float32)
    return full


# version 17
# speedup vs baseline: 1.0484x; 1.0484x over previous
import numpy as np
import ml_dtypes
from contextlib import ExitStack
import concourse.bass as bass
import concourse.mybir as mybir
from concourse.bass_utils import run_bass_kernel_spmd

F32 = mybir.dt.float32
BF16 = mybir.dt.bfloat16
I32 = mybir.dt.int32
AF = mybir.ActivationFunctionType
ALU = mybir.AluOpType
AXX = mybir.AxisListType.X

P = 128
H = 4096
DIN = 8192
NH = 8
RK = 256
TLOC = 1024
TEXT = 1027
TFULL = 2048
BF = ml_dtypes.bfloat16
GROUPS = [[0, 1], [2, 3], [4, 5], [6, 7]]
DEBUG_TAPS = False
PHASES = "ABC"
CPARTS = "gof"

ENGINES = ("sync", "vector", "scalar", "tensor", "gpsimd")


class _MockInst:
    def then_inc(self, sem, n=1):
        return self


class _MockEng:
    def __getattr__(self, name):
        def f(*a, **k):
            return _MockInst()
        return f


class Emit:

    def __init__(self):
        self.keys = {}
        self.first = True

    def start_pass(self, engines):
        self.eng = engines
        self.ctr = {}

    def end_pass(self):
        self.first = False

    def cnt(self, sem):
        return self.ctr.get(id(sem), 0)

    def inc(self, sem, n=1):
        self.ctr[id(sem)] = self.cnt(sem) + n

    def mark(self, sem, key):
        v = self.cnt(sem)
        if key not in self.keys:
            self.keys[key] = (sem, v)
        else:
            psem, pv = self.keys[key]
            assert psem is sem and pv == v, (key, pv, v)

    def wait(self, engname, sem, key):
        msem, v = self.keys[key]
        if v > 0:
            self.eng[engname].wait_ge(msem, v)

    def wait_val(self, engname, sem, v):
        if v > 0:
            self.eng[engname].wait_ge(sem, v)


def build_program():
    nc = bass.Bass("TRN2", target_bir_lowering=False, debug=False)

    def ein(name, shape, dt=F32):
        return nc.dram_tensor(name, shape, dt, kind="ExternalInput").ap()

    hs_ext = ein("hs_ext", [TEXT, H])
    win_t = ein("win_t", [128, P, 32 * P])
    woutT_t = ein("woutT_t", [32, P, 64 * P])
    wdtT_in = ein("wdtT_in", [RK, DIN])
    wxpT_t = ein("wxpT_t", [64, P, 288])
    qw2d = ein("qw2d", [H, H], I32)
    scales2 = ein("scales2", [H, 32])
    zeros2 = ein("zeros2", [H, 32])
    alog_l = ein("alog_l", [P, 64 * NH])
    cw4 = ein("cw4", [P, 64 * 4])
    cb1 = ein("cb1", [P, 64])
    dtb1 = ein("dtb1", [P, 64])
    dp1 = ein("dp1", [P, 64])
    fbias_in = ein("fbias_in", [P, 32])
    wln_in = ein("wln_in", [1, 288])
    ident_in = ein("ident_in", [P, P])
    esel_in = ein("esel_in", [16, NH * P])
    fbias_row = ein("fbias_row", [1, H])

    out_ext = nc.dram_tensor("out_ext", [TLOC, H], F32, kind="ExternalOutput").ap()
    dbg = {}
    if DEBUG_TAPS:
        for nm, shape, dt in [("d_xct", [DIN, TLOC], F32),
                              ("d_szt", [DIN, TLOC], F32),
                              ("d_dtr", [RK, TLOC], BF16),
                              ("d_bc", [32, TLOC], BF16),
                              ("d_yown", [DIN, TLOC], BF16),
                              ("d_gt", [H, TLOC], BF16),
                              ("d_st", [H, TLOC], BF16),
                              ("d_yp", [2 * DIN, TLOC], BF16),
                              ("d_wf", [H, H], BF16),
                              ("d_xcbg", [2 * DIN, TLOC], BF16)]:
            dbg[nm] = nc.dram_tensor(nm, shape, dt, kind="ExternalOutput").ap()

    def din_(name, shape, dt):
        return nc.dram_tensor(name, shape, dt).ap()

    xct_d = din_("xct_d", [DIN, TLOC], F32)
    xcb_d = din_("xcb_d", [DIN, TLOC], BF16)
    xcb_g = din_("xcb_g", [2 * DIN, TLOC], BF16)
    szt_d = din_("szt_d", [DIN, TLOC], BF16)
    dtr_d = din_("dtr_d", [RK, TLOC], BF16)
    dtr_g = din_("dtr_g", [2 * RK, TLOC], BF16)
    bc_d = din_("bc_d", [32, TLOC], BF16)
    bc_g = din_("bc_g", [64, TLOC], BF16)
    gt_d = din_("gt_d", [TLOC, H], BF16)
    st_d = din_("st_d", [H, TLOC], BF16)
    wffn_scr = din_("wffn_scr", [H, H], BF16)
    wout_scr = din_("wout_scr", [32, P, 64 * P], BF16)
    wdt_scr = din_("wdt_scr", [2, P, DIN], BF16)
    yp_d = din_("yp_d", [2 * DIN, TLOC], BF16)
    yown_d = din_("yown_d", [DIN, TLOC], BF16)

    E = Emit()

    with ExitStack() as top:
        def sem(name):
            return top.enter_context(nc.semaphore(name))

        sLD = sem("sLD")
        sST = sem("sST")
        sTE = sem("sTE")
        sDV = sem("sDV")
        sAC = sem("sAC")
        sPL = sem("sPL")
        cc1 = sem("cc1")
        cc2 = sem("cc2")
        cc3 = sem("cc3")
        cc4 = sem("cc4")
        SS = {}
        for _nm in ["sCN", "h0", "h1", "w0", "w1", "q0", "q1", "xl0", "xl1",
                    "sBC", "sWT", "sWS", "t0", "t1", "t2", "t3", "cwq",
                    "cxc0", "cxc1", "cxb0", "cxb1", "csz0", "csz1",
                    "cgt0", "cgt1", "sXG", "wd0", "wd1", "cyb", "g0", "g1",
                    "o0", "o1", "cst", "f0", "f1", "cot0", "cot1"]:
            SS[_nm] = sem(_nm)

        def sbt(name, shape, dt=F32):
            return top.enter_context(nc.sbuf_tensor(name, shape, dt))

        ident_f = sbt("ident_f", [P, P])
        ident_b = sbt("ident_b", [P, P], BF16)
        ones1 = sbt("ones1", [1, P])
        cw_sb = sbt("cw_sb", [P, 64 * 4])
        cb_sb = sbt("cb_sb", [P, 64])
        dtb_sb = sbt("dtb_sb", [P, 64])
        dp_sb = sbt("dp_sb", [P, 64])
        fb_sb = sbt("fb_sb", [P, 32])
        wln_sb = sbt("wln_sb", [1, 288])
        wlnbc = sbt("wlnbc", [P, 288])
        alog_sb = sbt("alog_sb", [P, 64 * NH])
        epsc = sbt("epsc", [P, 2])
        adev_sb = sbt("adev_sb", [P, 64 * NH])
        nst = sbt("nst", [P, 48])

        psP = [top.enter_context(nc.psum_tensor("psP%d" % i, [P, 512], F32))
               for i in range(8)]


        def emit_A(G, T):
            sync = G["sync"]; vector = G["vector"]; scalar = G["scalar"]
            tensor = G["tensor"]; gpsimd = G["gpsimd"]

            for i, (dst, src) in enumerate([
                    (T.cw_l, cw4), (T.cb_l, cb1), (T.dtb_l, dtb1),
                    (T.dp_l, dp1), (T.fb_l, fbias_in), (T.wln_l, wln_in),
                    (T.id_l, ident_in), (T.al_l, alog_l)]):
                sync.dma_start(out=dst, in_=src).then_inc(SS["sCN"], 16)
                E.inc(SS["sCN"], 16)
            E.mark(SS["sCN"], "consts")
            E.wait("gpsimd", sLD, "consts")
            gpsimd.memset(epsc[:, 0:1], 1e-5)
            gpsimd.memset(epsc[:, 1:2], 1e-6)
            gpsimd.memset(ones1[:], 1.0)
            gpsimd.tensor_copy(ident_b[:], ident_f[:]).then_inc(sPL, 1)
            E.inc(sPL)
            E.mark(sPL, "idb")
            E.wait("tensor", sPL, "idb")
            tensor.matmul(psP[0][:, 0:288], ones1[:], wln_sb[0:1, :],
                          start=True, stop=True).then_inc(sTE, 1)
            E.inc(sTE)
            E.mark(sTE, "wlnbc_mm")
            E.wait("vector", sTE, "wlnbc_mm")
            vector.tensor_copy(wlnbc[:], psP[0][:, 0:288]).then_inc(sDV, 1)
            E.inc(sDV)
            E.mark(sDV, "wlnbc")

            NT = 9
            ns_p0 = nc.named_scope("p0"); ns_p0.__enter__()
            for tt in range(NT):
                if tt == 8:
                    E.wait("vector", sDV, "p0_apply_6")
                    vector.memset(T.hs_t[0][:], 0.0).then_inc(sDV, 1)
                    E.inc(sDV)
                    E.mark(sDV, "p0_ms8")
                if tt >= 2:
                    E.wait("sync", sDV, "p0_apply_%d" % (tt - 2))
                hsem = SS["h%d" % (tt % 2)]
                if tt < 8:
                    sync.dma_start(out=T.hs_t[tt % 2][:],
                                   in_=hs_ext[tt * P:(tt + 1) * P, :]
                                   ).then_inc(hsem, 16)
                else:
                    E.wait("sync", sDV, "p0_ms8")
                    sync.dma_start(out=T.hs_t[0][0:3, :],
                                   in_=hs_ext[1024:1027, :]).then_inc(hsem, 16)
                E.inc(hsem, 16)
                E.mark(hsem, "p0_ld_%d" % tt)

                E.wait("vector", sLD, "p0_ld_%d" % tt)
                vector.tensor_reduce(nst[:, tt:tt + 1], T.hs_t[tt % 2][:],
                                     AXX, ALU.add, negate=True)
                E.wait("scalar", sLD, "p0_ld_%d" % tt)
                if tt >= 1:
                    E.wait("scalar", sDV, "p0_vsum_%d" % (tt - 1))
                scalar.activation(T.sq_t[:], T.hs_t[tt % 2][:], AF.Square
                                  ).then_inc(sAC, 1)
                E.inc(sAC)
                E.mark(sAC, "p0_sq_%d" % tt)
                E.wait("vector", sAC, "p0_sq_%d" % tt)
                vector.tensor_reduce(nst[:, 9 + tt:10 + tt], T.sq_t[:],
                                     AXX, ALU.add).then_inc(sDV, 1)
                E.inc(sDV)
                E.mark(sDV, "p0_vsum_%d" % tt)
                vector.tensor_scalar(nst[:, tt:tt + 1], nst[:, tt:tt + 1],
                                     1.0 / H, None, ALU.mult).then_inc(sDV, 1)
                E.inc(sDV)
                E.mark(sDV, "p0_negmean_%d" % tt)
                E.wait("gpsimd", sDV, "p0_negmean_%d" % tt)
                gpsimd.tensor_scalar(nst[:, 9 + tt:10 + tt],
                                     nst[:, 9 + tt:10 + tt],
                                     1.0 / H, None, ALU.mult)
                gpsimd.tensor_tensor(nst[:, 18 + tt:19 + tt],
                                     nst[:, tt:tt + 1], nst[:, tt:tt + 1],
                                     ALU.mult).then_inc(sPL, 1)
                E.inc(sPL)
                E.mark(sPL, "p0_m2_%d" % tt)
                E.wait("vector", sPL, "p0_m2_%d" % tt)
                vector.scalar_tensor_tensor(nst[:, 27 + tt:28 + tt],
                                            nst[:, 9 + tt:10 + tt], 1.0,
                                            nst[:, 18 + tt:19 + tt],
                                            ALU.mult, ALU.subtract
                                            ).then_inc(sDV, 1)
                E.inc(sDV)
                E.mark(sDV, "p0_va_%d" % tt)
                E.wait("scalar", sDV, "p0_va_%d" % tt)
                E.wait("scalar", sPL, "idb")
                scalar.activation(nst[:, 9 + tt:10 + tt],
                                  nst[:, 27 + tt:28 + tt], AF.Ln,
                                  bias=epsc[:, 0:1]).then_inc(sAC, 1)
                E.inc(sAC)
                E.mark(sAC, "p0_ln_%d" % tt)
                E.wait("vector", sAC, "p0_ln_%d" % tt)
                vector.tensor_scalar(nst[:, 27 + tt:28 + tt],
                                     nst[:, 9 + tt:10 + tt], -0.5, None,
                                     ALU.mult).then_inc(sDV, 1)
                E.inc(sDV)
                E.mark(sDV, "p0_lnh_%d" % tt)
                E.wait("scalar", sDV, "p0_lnh_%d" % tt)
                scalar.activation(nst[:, 9 + tt:10 + tt],
                                  nst[:, 27 + tt:28 + tt], AF.Exp
                                  ).then_inc(sAC, 1)
                E.inc(sAC)
                E.mark(sAC, "p0_rstd_%d" % tt)
                E.wait("gpsimd", sAC, "p0_rstd_%d" % tt)
                gpsimd.tensor_tensor(nst[:, 18 + tt:19 + tt],
                                     nst[:, tt:tt + 1],
                                     nst[:, 9 + tt:10 + tt], ALU.mult
                                     ).then_inc(sPL, 1)
                E.inc(sPL)
                E.mark(sPL, "p0_bias_%d" % tt)
                E.wait("vector", sPL, "p0_bias_%d" % tt)
                if tt >= 1:
                    E.wait("vector", sTE, "p0_tr_%d_7" % (tt - 1))
                vector.tensor_scalar(T.xn_t[:], T.hs_t[tt % 2][:],
                                     nst[:, 9 + tt:10 + tt],
                                     nst[:, 18 + tt:19 + tt],
                                     ALU.mult, ALU.add).then_inc(sDV, 1)
                E.inc(sDV)
                E.mark(sDV, "p0_apply_%d" % tt)
                ncol = P if tt < 8 else 3
                for q in range(8):
                    E.wait("tensor", sDV, "p0_apply_%d" % tt)
                    E.wait("tensor", sPL, "idb")
                    if tt > 0 or q >= 2:
                        prev = (tt * 8 + q) - 2
                        E.wait("tensor", sDV, "p0_ev_%d" % prev)
                    pb = psP[6 + q % 2]
                    for j in range(4):
                        hc = q * 4 + j
                        inst = tensor.transpose(
                            pb[:, j * P:(j + 1) * P],
                            T.xn_t[:, hc * P:(hc + 1) * P], ident_f[:])
                    inst.then_inc(sTE, 1)
                    E.inc(sTE)
                    E.mark(sTE, "p0_tr_%d_%d" % (tt, q))
                    E.wait("vector", sTE, "p0_tr_%d_%d" % (tt, q))
                    for j in range(4):
                        hc = q * 4 + j
                        inst = vector.tensor_copy(
                            xnT[:, hc * TEXT + tt * P:hc * TEXT + tt * P + ncol],
                            pb[:, j * P:j * P + ncol])
                    inst.then_inc(sDV, 1)
                    E.inc(sDV)
                    E.mark(sDV, "p0_ev_%d" % (tt * 8 + q))

            ns_p0.__exit__(None, None, None)
            ns_p1 = nc.named_scope("p1"); ns_p1.__enter__()
            for fc in range(128):
                xpart = fc < 64
                db = fc
                par = fc % 2
                if fc >= 8 and (fc - 8) % 3 == 0 and (fc - 8) // 3 < 32:
                    ot = (fc - 8) // 3
                    qpar = ot % 2
                    if ot >= 2:
                        E.wait("sync", sDV, "dq_w_%d" % (ot - 2))
                    szo = (ot % 2) * 64
                    qsem = SS["q%d" % qpar]
                    sync.dma_start(out=T.qwt[qpar][:],
                                   in_=qw2d[ot * P:(ot + 1) * P, :]
                                   ).then_inc(qsem, 16)
                    sync.dma_start(out=T.scz[:, szo:szo + 32],
                                   in_=scales2[ot * P:(ot + 1) * P, :]
                                   ).then_inc(qsem, 16)
                    sync.dma_start(out=T.scz[:, szo + 32:szo + 64],
                                   in_=zeros2[ot * P:(ot + 1) * P, :]
                                   ).then_inc(qsem, 16)
                    E.inc(qsem, 48)
                    E.mark(qsem, "dq_ld_%d" % ot)
                    E.wait("vector", sLD, "dq_ld_%d" % ot)
                    if ot == 0:
                        E.wait("vector", sTE, "p0_tr_8_7")
                    if ot >= 1:
                        E.wait("vector", sST, "dq_st_%d" % (ot - 1))
                    for g in range(32):
                        inst = vector.tensor_scalar(
                            T.wdq[:, g * P:(g + 1) * P],
                            T.qwt[qpar][:, g * P:(g + 1) * P],
                            T.scz[:, szo + g:szo + g + 1],
                            T.scz[:, szo + 32 + g:szo + 33 + g],
                            ALU.mult, ALU.add)
                    inst.then_inc(sDV, 1)
                    E.inc(sDV)
                    E.mark(sDV, "dq_w_%d" % ot)
                    E.wait("scalar", sDV, "dq_w_%d" % ot)
                    scalar.dma_start(out=wffn_scr[ot * P:(ot + 1) * P, :],
                                     in_=T.wdq[:]).then_inc(SS["cwq"], 16)
                    E.inc(SS["cwq"], 16)
                    E.mark(SS["cwq"], "dq_st_%d" % ot)
                if fc == 0:
                    E.wait("sync", sDV, "p0_vsum_8")
                if fc >= 2:
                    E.wait("sync", sPL, "p1_cast_%d" % (fc - 2))
                wsem = SS["w%d" % par]
                for qv in range(2):
                    sync.dma_start(out=T.wbuf[par][:, qv * 2048:(qv + 1) * 2048],
                                   in_=win_t[fc][:, qv * 2048:(qv + 1) * 2048]
                                   ).then_inc(wsem, 16)
                E.inc(wsem, 32)
                E.mark(wsem, "p1_w_%d" % fc)
                E.wait("gpsimd", sLD, "p1_w_%d" % fc)
                if fc >= 2:
                    E.wait("gpsimd", sTE, "p1_mm_%d" % (fc - 2))
                gpsimd.tensor_copy(T.wbb[par][:], T.wbuf[par][:]
                                   ).then_inc(sPL, 1)
                E.inc(sPL)
                E.mark(sPL, "p1_cast_%d" % fc)
                E.wait("tensor", sPL, "p1_cast_%d" % fc)
                if fc >= 2:
                    E.wait("tensor", sAC, "p1_ev_%d" % (fc - 2))
                if fc == 0:
                    E.mark(sDV, "chk")
                    E.wait("tensor", sDV, "p0_ev_%d" % (9 * 8 - 1))
                tcs = [(0, 512), (512, 1024), (1024, 1027)] if xpart else \
                      [(3, 515), (515, 1027)]
                for ti, (t0, t1) in enumerate(tcs):
                    pb = psP[3 * par + ti]
                    for hc in range(32):
                        inst = tensor.matmul(
                            pb[:, 0:t1 - t0],
                            T.wbb[par][:, hc * P:(hc + 1) * P],
                            xnT[:, hc * TEXT + t0:hc * TEXT + t1],
                            start=(hc == 0), stop=(hc == 31))
                inst.then_inc(sTE, 1)
                E.inc(sTE)
                E.mark(sTE, "p1_mm_%d" % fc)

                if xpart:
                    E.wait("scalar", sTE, "p1_mm_%d" % fc)
                    scalar.activation(T.xsb[par][:, 0:512],
                                      psP[3 * par + 0][:], AF.Copy)
                    scalar.activation(T.xsb[par][:, 512:1024],
                                      psP[3 * par + 1][:], AF.Copy)
                    scalar.activation(T.xsb[par][:, 1024:1027],
                                      psP[3 * par + 2][:, 0:3], AF.Copy
                                      ).then_inc(sAC, 1)
                    E.inc(sAC)
                    E.mark(sAC, "p1_ev_%d" % fc)
                    E.wait("vector", sPL, "idb")
                    if fc >= 2:
                        E.wait("vector", sTE, "p1_conv_%d" % (fc - 2))
                    for k in range(4):
                        inst = vector.tensor_scalar(
                            T.diag[par][:, k * P:(k + 1) * P], ident_f[:],
                            cw_sb[:, db * 4 + k:db * 4 + k + 1], None,
                            ALU.mult)
                    inst.then_inc(sDV, 1)
                    E.inc(sDV)
                    E.mark(sDV, "p1_diag_%d" % fc)
                    E.wait("tensor", sAC, "p1_ev_%d" % fc)
                    E.wait("tensor", sDV, "p1_diag_%d" % fc)
                    if fc >= 1:
                        E.wait("tensor", sAC, "p1_silu_%d" % (fc - 1))
                    for tc2 in range(2):
                        pb = psP[6 + tc2]
                        for k in range(4):
                            inst = tensor.matmul(
                                pb[:],
                                T.diag[par][:, k * P:(k + 1) * P],
                                T.xsb[par][:, tc2 * 512 + k:tc2 * 512 + k + 512],
                                start=(k == 0), stop=(k == 3))
                    inst.then_inc(sTE, 1)
                    E.inc(sTE)
                    E.mark(sTE, "p1_conv_%d" % fc)
                    E.wait("scalar", sTE, "p1_conv_%d" % fc)
                    if fc >= 2:
                        E.wait("scalar", sST, "p1_stxb_%d" % (fc - 2))
                        E.wait("scalar", sDV, "p1_cb_%d" % (fc - 2))
                    scalar.activation(T.xc32[par][:, 0:512], psP[6][:],
                                      AF.Silu, bias=cb_sb[:, db:db + 1])
                    scalar.activation(T.xc32[par][:, 512:1024], psP[7][:],
                                      AF.Silu, bias=cb_sb[:, db:db + 1]
                                      ).then_inc(sAC, 1)
                    E.inc(sAC)
                    E.mark(sAC, "p1_silu_%d" % fc)
                    E.wait("vector", sAC, "p1_silu_%d" % fc)
                    if fc >= 2:
                        E.wait("vector", sST, "p1_stxb_%d" % (fc - 2))
                    vector.tensor_copy(T.xcb16[par][:], T.xc32[par][:]
                                       ).then_inc(sDV, 1)
                    E.inc(sDV)
                    E.mark(sDV, "p1_cb_%d" % fc)
                    E.wait("scalar", sDV, "p1_cb_%d" % fc)
                    scalar.dma_start(out=xcb_d[db * P:(db + 1) * P, :],
                                     in_=T.xcb16[par][:]
                                     ).then_inc(SS["cxb%d" % par], 16)
                    E.inc(SS["cxb%d" % par], 16)
                    E.mark(SS["cxb%d" % par], "p1_stxb_%d" % fc)
                else:
                    E.wait("scalar", sTE, "p1_mm_%d" % fc)
                    if fc >= 66:
                        E.wait("scalar", sST, "p1_stz_%d" % (fc - 2))
                    scalar.activation(T.sz32[par][:, 0:512],
                                      psP[3 * par + 0][:], AF.Silu)
                    scalar.activation(T.sz32[par][:, 512:1024],
                                      psP[3 * par + 1][:], AF.Silu
                                      ).then_inc(sAC, 1)
                    E.inc(sAC)
                    E.mark(sAC, "p1_ev_%d" % fc)
                    zb = fc - 64
                    E.wait("scalar", sAC, "p1_ev_%d" % fc)
                    scalar.dma_start(out=szt_d[zb * P:(zb + 1) * P, :],
                                     in_=T.sz32[par][:]
                                     ).then_inc(SS["csz%d" % par], 16)
                    E.inc(SS["csz%d" % par], 16)
                    E.mark(SS["csz%d" % par], "p1_stz_%d" % fc)

            ns_p1.__exit__(None, None, None)
            ns_p2 = nc.named_scope("p2"); ns_p2.__enter__()
            for dc in range(64):
                par = dc % 2
                if dc >= 2:
                    E.wait("sync", sTE, "p2_mm_%d" % (dc - 2))
                E.wait("sync", sST, "p1_stxb_%d" % dc)
                xsem = SS["xl%d" % par]
                sync.dma_start(out=T.xcbL[par][:],
                               in_=xcb_d[dc * P:(dc + 1) * P, :]
                               ).then_inc(xsem, 16)
                sync.dma_start(out=T.wxp[par][:], in_=wxpT_t[dc]
                               ).then_inc(xsem, 16)
                E.inc(xsem, 32)
                E.mark(xsem, "p2_ld_%d" % dc)
                E.wait("gpsimd", sLD, "p2_ld_%d" % dc)
                if dc >= 2:
                    E.wait("gpsimd", sTE, "p2_mm_%d" % (dc - 2))
                gpsimd.tensor_copy(T.wxpb[par][:], T.wxp[par][:]
                                   ).then_inc(sPL, 1)
                E.inc(sPL)
                E.mark(sPL, "p2_cast_%d" % dc)
                E.wait("tensor", sPL, "p2_cast_%d" % dc)
                E.wait("tensor", sLD, "p2_ld_%d" % dc)
                if dc == 0:
                    E.wait("tensor", sAC, "p1_ev_127")
                for tt in range(8):
                    inst = tensor.matmul(
                        psP[tt][:, 0:288],
                        T.xcbL[par][:, tt * P:(tt + 1) * P],
                        T.wxpb[par][:],
                        start=(dc == 0), stop=(dc == 63))
                inst.then_inc(sTE, 1)
                E.inc(sTE)
                E.mark(sTE, "p2_mm_%d" % dc)

            E.wait("scalar", sTE, "p2_mm_63")
            for tt in range(8):
                inst = scalar.activation(T.ssm[:, tt * 288:(tt + 1) * 288],
                                         psP[tt][:, 0:288], AF.Copy)
            inst.then_inc(sAC, 1)
            E.inc(sAC)
            E.mark(sAC, "p2_ssm")
            E.wait("scalar", sAC, "p2_ssm")
            scalar.activation(T.ssq[:], T.ssm[:], AF.Square).then_inc(sAC, 1)
            E.inc(sAC)
            E.mark(sAC, "p2_sq")
            E.wait("vector", sAC, "p2_sq")
            for tt in range(8):
                vector.tensor_reduce(nst[:, tt:tt + 1],
                                     T.ssq[:, tt * 288:tt * 288 + 256],
                                     AXX, ALU.add)
                vector.tensor_reduce(nst[:, 8 + tt:9 + tt],
                                     T.ssq[:, tt * 288 + 256:tt * 288 + 272],
                                     AXX, ALU.add)
                inst = vector.tensor_reduce(
                    nst[:, 16 + tt:17 + tt],
                    T.ssq[:, tt * 288 + 272:tt * 288 + 288], AXX, ALU.add)
            inst.then_inc(sDV, 1)
            E.inc(sDV)
            E.mark(sDV, "p2_sums")
            E.wait("gpsimd", sDV, "p2_sums")
            gpsimd.tensor_scalar(nst[:, 0:8], nst[:, 0:8], 1.0 / RK, None,
                                 ALU.mult)
            gpsimd.tensor_scalar(nst[:, 8:24], nst[:, 8:24], 1.0 / 16.0,
                                 None, ALU.mult).then_inc(sPL, 1)
            E.inc(sPL)
            E.mark(sPL, "p2_ms")
            E.wait("scalar", sPL, "p2_ms")
            scalar.activation(nst[:, 24:48], nst[:, 0:24], AF.Ln,
                              bias=epsc[:, 1:2]).then_inc(sAC, 1)
            E.inc(sAC)
            E.mark(sAC, "p2_lnv")
            E.wait("vector", sAC, "p2_lnv")
            vector.tensor_scalar(nst[:, 24:48], nst[:, 24:48], -0.5, None,
                                 ALU.mult).then_inc(sDV, 1)
            E.inc(sDV)
            E.mark(sDV, "p2_lnh")
            E.wait("scalar", sDV, "p2_lnh")
            scalar.activation(nst[:, 0:24], nst[:, 24:48], AF.Exp
                              ).then_inc(sAC, 1)
            E.inc(sAC)
            E.mark(sAC, "p2_rstd")
            E.wait("vector", sAC, "p2_rstd")
            E.wait("vector", sDV, "wlnbc")
            for tt in range(8):
                vector.scalar_tensor_tensor(
                    T.dtrb[:, tt * RK:(tt + 1) * RK],
                    T.ssm[:, tt * 288:tt * 288 + 256],
                    nst[:, tt:tt + 1], wlnbc[:, 0:256], ALU.mult, ALU.mult)
                vector.scalar_tensor_tensor(
                    T.bmb[:, tt * 16:(tt + 1) * 16],
                    T.ssm[:, tt * 288 + 256:tt * 288 + 272],
                    nst[:, 8 + tt:9 + tt], wlnbc[:, 256:272],
                    ALU.mult, ALU.mult)
                inst = vector.scalar_tensor_tensor(
                    T.cmb[:, tt * 16:(tt + 1) * 16],
                    T.ssm[:, tt * 288 + 272:tt * 288 + 288],
                    nst[:, 16 + tt:17 + tt], wlnbc[:, 272:288],
                    ALU.mult, ALU.mult)
            inst.then_inc(sDV, 1)
            E.inc(sDV)
            E.mark(sDV, "p2_app")
            E.wait("tensor", sDV, "p2_app")
            for grp in range(4):
                pb = psP[6 + grp % 2]
                if grp >= 2:
                    E.wait("tensor", sDV, "p2_trev_%d" % (grp - 2))
                for j in range(4):
                    idx = grp * 4 + j
                    tt, rc = idx // 2, idx % 2
                    inst = tensor.transpose(
                        pb[:, j * P:(j + 1) * P],
                        T.dtrb[:, tt * RK + rc * P:tt * RK + (rc + 1) * P],
                        ident_f[:])
                inst.then_inc(sTE, 1)
                E.inc(sTE)
                E.mark(sTE, "p2_tr_%d" % grp)
                E.wait("vector", sTE, "p2_tr_%d" % grp)
                for j in range(4):
                    idx = grp * 4 + j
                    tt, rc = idx // 2, idx % 2
                    inst = vector.tensor_copy(
                        T.dtrT[:, rc * TLOC + tt * P:rc * TLOC + (tt + 1) * P],
                        pb[:, j * P:(j + 1) * P])
                inst.then_inc(sDV, 1)
                E.inc(sDV)
                E.mark(sDV, "p2_trev_%d" % grp)
            E.wait("tensor", sDV, "p2_trev_3")
            for half, srcmb in ((0, T.bmb), (1, T.cmb)):
                for tt in range(8):
                    pb = psP[4 + 2 * half + tt // 4]
                    inst = tensor.transpose(
                        pb[0:16, (tt % 4) * P:(tt % 4 + 1) * P],
                        srcmb[:, tt * 16:(tt + 1) * 16], ident_f[:])
                inst.then_inc(sTE, 1)
                E.inc(sTE)
                E.mark(sTE, "p2_bctr_%d" % half)
                E.wait("vector", sTE, "p2_bctr_%d" % half)
                dstT = T.bmT if half == 0 else T.cmT
                vector.tensor_copy(dstT[:, 0:512],
                                   psP[4 + 2 * half][0:16, 0:512])
                inst = vector.tensor_copy(dstT[:, 512:1024],
                                          psP[5 + 2 * half][0:16, 0:512])
                inst.then_inc(sDV, 1)
                E.inc(sDV)
                E.mark(sDV, "p2_bcev_%d" % half)
            E.wait("scalar", sDV, "p2_trev_3")
            for rc in range(2):
                scalar.dma_start(out=dtr_d[rc * P:(rc + 1) * P, :],
                                 in_=T.dtrT[:, rc * TLOC:(rc + 1) * TLOC]
                                 ).then_inc(SS["sBC"], 16)
                E.inc(SS["sBC"], 16)
            E.wait("scalar", sDV, "p2_bcev_0")
            E.wait("scalar", sDV, "p2_bcev_1")
            scalar.dma_start(out=bc_d[0:16, :], in_=T.bmT[:]
                             ).then_inc(SS["sBC"], 16)
            scalar.dma_start(out=bc_d[16:32, :], in_=T.cmT[:]
                             ).then_inc(SS["sBC"], 16)
            E.inc(SS["sBC"], 32)
            E.mark(SS["sBC"], "ex1_bc")
            ns_p2.__exit__(None, None, None)
            E.wait("gpsimd", sST, "ex1_bc")
            E.wait("gpsimd", sST, "p1_stxb_62")
            E.wait("gpsimd", sST, "p1_stxb_63")
            gpsimd.collective_compute(
                "AllGather", ALU.bypass, replica_groups=GROUPS,
                ins=[xcb_d.opt()], outs=[xcb_g.opt()]).then_inc(cc1, 1)
            gpsimd.collective_compute(
                "AllGather", ALU.bypass, replica_groups=GROUPS,
                ins=[dtr_d.opt()], outs=[dtr_g.opt()]).then_inc(cc2, 1)
            gpsimd.collective_compute(
                "AllGather", ALU.bypass, replica_groups=GROUPS,
                ins=[bc_d.opt()], outs=[bc_g.opt()]).then_inc(cc3, 1)

        def emit_A2(G, T):
            sync = G["sync"]; vector = G["vector"]; scalar = G["scalar"]
            tensor = G["tensor"]; gpsimd = G["gpsimd"]

            ns_wc = nc.named_scope("a2wc"); ns_wc.__enter__()
            for it in range(2):
                if it >= 1:
                    E.wait("sync", sST, "a2_wst_%d" % (it - 1))
                sync.dma_start(out=T.woc32[0][:],
                               in_=wdtT_in[it * P:(it + 1) * P, :]
                               ).then_inc(SS["sWT"], 16)
                E.inc(SS["sWT"], 16)
                E.mark(SS["sWT"], "a2_wld_%d" % it)
                E.wait("gpsimd", sLD, "a2_wld_%d" % it)
                if it >= 1:
                    E.wait("gpsimd", sST, "a2_wst_%d" % (it - 1))
                gpsimd.tensor_copy(T.woc16[0][:], T.woc32[0][:]
                                   ).then_inc(sPL, 1)
                E.inc(sPL)
                E.mark(sPL, "a2_wc_%d" % it)
                E.wait("scalar", sPL, "a2_wc_%d" % it)
                scalar.dma_start(out=wdt_scr[it], in_=T.woc16[0][:]
                                 ).then_inc(SS["sWS"], 16)
                E.inc(SS["sWS"], 16)
                E.mark(SS["sWS"], "a2_wst_%d" % it)

            ns_wc.__exit__(None, None, None)
            ns_ffn = nc.named_scope("ffn"); ns_ffn.__enter__()
            E.wait("sync", sST, "dq_st_31")
            sync.dma_start(out=T.fbrow[:], in_=fbias_row
                           ).then_inc(SS["sCN"], 16)
            E.inc(SS["sCN"], 16)
            E.mark(SS["sCN"], "ffn_fb")
            E.wait("tensor", sLD, "ffn_fb")
            E.wait("tensor", sPL, "idb")
            E.wait("tensor", sAC, "p2_ssm")
            for og in range(8):
                inst = tensor.matmul(psP[og][:], ones1[:],
                                     T.fbrow[0:1, og * 512:(og + 1) * 512],
                                     start=True, stop=True)
            inst.then_inc(sTE, 1)
            E.inc(sTE)
            E.mark(sTE, "ffn_fbmm")
            E.wait("vector", sTE, "ffn_fbmm")
            for og in range(8):
                inst = vector.tensor_copy(T.fb_bc[:, og * 512:(og + 1) * 512],
                                          psP[og][:])
            inst.then_inc(sDV, 1)
            E.inc(sDV)
            E.mark(sDV, "ffn_fbbc")
            for og in range(8):
                for ic in range(32):
                    it = og * 32 + ic
                    if it >= 4:
                        E.wait("sync", sTE, "ffn_mm_%d" % (it - 4))
                    tsem = SS["t%d" % (ic % 4)]
                    sync.dma_start(
                        out=T.wTr[ic % 4][:],
                        in_=wffn_scr[og * 512:(og + 1) * 512,
                                     ic * P:(ic + 1) * P],
                        transpose=True).then_inc(tsem, 16)
                    E.inc(tsem, 16)
                    E.mark(tsem, "ffn_ld_%d" % it)
                    E.wait("tensor", sLD, "ffn_ld_%d" % it)
                    if ic == 0:
                        E.wait("tensor", sDV, "ffn_fbbc")
                        if og >= 1:
                            E.wait("tensor", sAC, "ffn_ev_%d" % (og - 1))
                    for tt in range(8):
                        inst = tensor.matmul(
                            psP[tt][:],
                            xnT[:, ic * TEXT + 3 + tt * P:
                                ic * TEXT + 3 + (tt + 1) * P],
                            T.wTr[ic % 4][:],
                            start=(ic == 0), stop=(ic == 31))
                    inst.then_inc(sTE, 1)
                    E.inc(sTE)
                    E.mark(sTE, "ffn_mm_%d" % it)
                for tt in range(8):
                    E.wait("vector", sTE, "ffn_mm_%d" % (og * 32 + 31))
                    if og >= 1:
                        E.wait("vector", sAC, "ffn_ev_%d" % (og - 1))
                    if og > 0 or tt >= 2:
                        pog, ptt = (og, tt - 2) if tt >= 2 else (og - 1, 6 + tt)
                        E.wait("vector", sAC, "ffn_sl_%d_%d" % (pog, ptt))
                    inst = vector.scalar_tensor_tensor(
                        T.gbuf[tt % 2][:], psP[tt][:], 1.0,
                        T.fb_bc[:, og * 512:(og + 1) * 512],
                        ALU.mult, ALU.add)
                    inst.then_inc(sDV, 1)
                    E.inc(sDV)
                    E.mark(sDV, "ffn_gb_%d_%d" % (og, tt))
                    E.wait("scalar", sDV, "ffn_gb_%d_%d" % (og, tt))
                    if og >= 1 or tt >= 2:
                        pog, ptt = (og, tt - 2) if tt >= 2 else (og - 1, 6 + tt)
                        E.wait("scalar", sST, "ffn_st_%d_%d" % (pog, ptt))
                    inst = scalar.activation(T.gtt[tt % 2][:], T.gbuf[tt % 2][:],
                                             AF.Silu)
                    inst.then_inc(sAC, 1)
                    E.inc(sAC)
                    E.mark(sAC, "ffn_sl_%d_%d" % (og, tt))
                    E.wait("scalar", sAC, "ffn_sl_%d_%d" % (og, tt))
                    scalar.dma_start(
                        out=gt_d[tt * P:(tt + 1) * P, og * 512:(og + 1) * 512],
                        in_=T.gtt[tt % 2][:]).then_inc(SS["cgt%d" % (tt % 2)], 16)
                    E.inc(SS["cgt%d" % (tt % 2)], 16)
                    E.mark(SS["cgt%d" % (tt % 2)], "ffn_st_%d_%d" % (og, tt))
                E.mark(sAC, "ffn_ev_%d" % og)

            ns_ffn.__exit__(None, None, None)

        def emit_B(G, T):
            sync = G["sync"]; vector = G["vector"]; scalar = G["scalar"]
            tensor = G["tensor"]; gpsimd = G["gpsimd"]

            E.wait("sync", cc2, "never")


        class TensA:
            pass

        with ExitStack() as xnS:
            xnT = xnS.enter_context(nc.sbuf_tensor("xnT", [P, 32 * TEXT], BF16))

            with ExitStack() as sA:
                def sba(name, shape, dt=F32):
                    return sA.enter_context(nc.sbuf_tensor(name, shape, dt))
                T = TensA()
                T.hs_t = [sba("hs_t%d" % i, [P, H]) for i in range(2)]
                T.xn_t = sba("xn_t", [P, H])
                T.wbuf = [sba("wbuf%d" % i, [P, H]) for i in range(2)]
                T.sq_t = T.wbuf[0]
                T.wbb = [sba("wbb%d" % i, [P, H], BF16) for i in range(2)]
                T.xsb = [sba("xsb%d" % i, [P, TEXT], BF16) for i in range(2)]
                T.diag = [sba("diag%d" % i, [P, 4 * P], BF16) for i in range(2)]
                T.xc32 = [sba("xc32%d" % i, [P, TLOC]) for i in range(2)]
                T.xcb16 = [sba("xcb16%d" % i, [P, TLOC], BF16) for i in range(2)]
                T.sz32 = [sba("sz32%d" % i, [P, TLOC], BF16) for i in range(2)]
                T.xcbL = [sba("xcbL%d" % i, [P, TLOC], BF16) for i in range(2)]
                T.wxp = [sba("wxp%d" % i, [P, 288]) for i in range(2)]
                T.wxpb = [sba("wxpb%d" % i, [P, 288], BF16) for i in range(2)]
                T.ssm = T.hs_t[0][:, 0:2304]
                T.ssq = T.wbuf[0][:, 0:2304]
                T.dtrb = T.hs_t[1][:, 0:2048]
                T.bmb = T.hs_t[1][:, 2048:2176]
                T.cmb = T.hs_t[1][:, 2176:2304]
                T.dtrT = sba("dtrT", [P, 2 * TLOC], BF16)
                T.qwt = [T.hs_t[0][:].bitcast(I32),
                         T.hs_t[1][:].bitcast(I32)]
                T.wdq = T.xn_t[:].bitcast(BF16)[:, 0:H]
                T.scz = sba("scz", [P, 128])
                T.bmT = T.xcbL[0][0:16, :]
                T.cmT = T.xcbL[1][0:16, :]
                T.cw_l = cw_sb[:]
                T.cb_l = cb_sb[:]
                T.dtb_l = dtb_sb[:]
                T.dp_l = dp_sb[:]
                T.fb_l = fb_sb[:]
                T.wln_l = wln_sb[:]
                T.id_l = ident_f[:]
                T.al_l = alog_sb[:]

                mock = {e: _MockEng() for e in ENGINES}
                E.start_pass(mock)
                emit_A(mock, T)
                snapshot_ctr_A = dict(E.ctr)
                E.end_pass()

                with nc.Block() as block:
                    for engname in ENGINES:
                        def mk(engname):
                            def body(eng):
                                engs = {e: _MockEng() for e in ENGINES}
                                engs[engname] = eng
                                E.start_pass(engs)
                                emit_A(engs, T)
                                assert E.ctr == snapshot_ctr_A
                            return body
                        getattr(block, engname)(mk(engname))
                E.ctr = dict(snapshot_ctr_A)

            if "B" not in PHASES and False:
                pass
            with ExitStack() as sA2:
                def sba2(name, shape, dt=F32):
                    return sA2.enter_context(nc.sbuf_tensor(name, shape, dt))
                T2 = TensA()
                T2.woc32 = [sba2("woc32", [P, DIN])]
                T2.woc16 = [sba2("woc16", [P, DIN], BF16)]
                T2.wTr = [sba2("wTr%d" % i, [P, 512], BF16) for i in range(4)]
                T2.gbuf = [sba2("gbuf%d" % i, [P, 512]) for i in range(2)]
                T2.gtt = [sba2("gtt%d" % i, [P, 512], BF16) for i in range(2)]
                T2.fb_bc = sba2("fb_bc", [P, H])
                T2.fbrow = sba2("fbrow", [1, H])

                base_ctr = dict(E.ctr)
                E.start_pass({e: _MockEng() for e in ENGINES})
                E.ctr = dict(base_ctr)
                emit_A2({e: _MockEng() for e in ENGINES}, T2)
                snap2 = dict(E.ctr)
                E.end_pass()

                with nc.Block() as block:
                    for engname in ENGINES:
                        def mk2(engname):
                            def body(eng):
                                engs = {e: _MockEng() for e in ENGINES}
                                engs[engname] = eng
                                E.start_pass(engs)
                                E.ctr = dict(base_ctr)
                                emit_A2(engs, T2)
                                assert E.ctr == snap2
                            return body
                        getattr(block, engname)(mk2(engname))
                E.ctr = dict(snap2)

        def emit_Bv2(G, T):
            sync = G["sync"]; vector = G["vector"]; scalar = G["scalar"]
            tensor = G["tensor"]; gpsimd = G["gpsimd"]

            scalar.activation(adev_sb[:], alog_sb[:], AF.Exp).then_inc(sAC, 1)
            E.inc(sAC)
            E.mark(sAC, "b_aexp")
            E.wait("vector", sAC, "b_aexp")
            vector.tensor_scalar(adev_sb[:], adev_sb[:], -1.0, None, ALU.mult
                                 ).then_inc(sDV, 1)
            E.inc(sDV)
            E.mark(sDV, "b_adev")

            E.wait_val("sync", cc2, 1)
            for rc in range(2):
                for rk in range(2):
                    sync.dma_start(
                        out=T.dtrg[rc][:, rk * TLOC:(rk + 1) * TLOC],
                        in_=dtr_g[rk * RK + rc * P:rk * RK + (rc + 1) * P, :]
                        ).then_inc(SS["sCN"], 16)
                    E.inc(SS["sCN"], 16)
            E.wait_val("sync", cc3, 1)
            for rk in range(2):
                sync.dma_start(out=T.bmTg[:, rk * TLOC:(rk + 1) * TLOC],
                               in_=bc_g[rk * 32:rk * 32 + 16, :]
                               ).then_inc(SS["sCN"], 16)
                sync.dma_start(out=T.cmTg[:, rk * TLOC:(rk + 1) * TLOC],
                               in_=bc_g[rk * 32 + 16:rk * 32 + 32, :]
                               ).then_inc(SS["sCN"], 16)
                E.inc(SS["sCN"], 32)
            sync.dma_start(out=T.eself[:], in_=esel_in
                           ).then_inc(SS["sCN"], 16)
            E.inc(SS["sCN"], 16)
            E.mark(SS["sCN"], "b_pro")
            E.wait("gpsimd", sLD, "b_pro")
            gpsimd.tensor_copy(T.eselb[:], T.eself[:]).then_inc(sPL, 1)
            E.inc(sPL)
            E.mark(sPL, "b_esel")

            ns_bc = nc.named_scope("brep"); ns_bc.__enter__()
            for n in range(NH):
                for half, (srcT, dstR) in enumerate(
                        ((T.bmTg, T.B_rep), (T.cmTg, T.C_rep))):
                    for tc in range(4):
                        idx = n * 8 + half * 4 + tc
                        pb = psP[idx % 2]
                        E.wait("tensor", sPL, "b_esel")
                        E.wait("tensor", sLD, "b_pro")
                        if idx == 0:
                            E.wait("tensor", sAC, "ffn_ev_7")
                        if idx >= 2:
                            E.wait("tensor", sAC, "b_bc_%d" % (idx - 2))
                        tensor.matmul(pb[:],
                                      T.eselb[:, n * P:(n + 1) * P],
                                      srcT[:, tc * 512:(tc + 1) * 512],
                                      start=True, stop=True).then_inc(sTE, 1)
                        E.inc(sTE)
                        E.mark(sTE, "b_bcm_%d" % idx)
                        E.wait("scalar", sTE, "b_bcm_%d" % idx)
                        scalar.activation(
                            dstR[:, n * TFULL + tc * 512:
                                 n * TFULL + (tc + 1) * 512],
                            pb[:], AF.Copy).then_inc(sAC, 1)
                        E.inc(sAC)
                        E.mark(sAC, "b_bc_%d" % idx)

            ns_bc.__exit__(None, None, None)
            ns_sc = nc.named_scope("scan"); ns_sc.__enter__()
            for db in range(64):
                par = db % 2
                E.wait_val("sync", cc1, 1)
                if db >= 1:
                    E.wait("sync", sDV, "b_u_%d" % (db - 1))
                for rk in range(2):
                    inst = sync.dma_start(
                        out=T.xcg[0][:, rk * TLOC:(rk + 1) * TLOC],
                        in_=xcb_g[rk * DIN + db * P:rk * DIN + (db + 1) * P, :])
                    inst.then_inc(SS["sXG"], 16)
                    E.inc(SS["sXG"], 16)
                E.mark(SS["sXG"], "b_xcg_%d" % db)
                if db == 0:
                    E.wait("sync", sST, "a2_wst_1")
                if db >= 2:
                    E.wait("sync", sTE, "b_dtmm_%d" % ((db - 2) * 4 + 3))
                for rc in range(2):
                    inst = sync.dma_start(
                        out=T.wdtr[par][:, rc * P:(rc + 1) * P],
                        in_=wdt_scr[rc][:, db * P:(db + 1) * P])
                    inst.then_inc(SS["wd%d" % par], 16)
                    E.inc(SS["wd%d" % par], 16)
                E.mark(SS["wd%d" % par], "b_wdt_%d" % db)
                for tc in range(4):
                    idx = db * 4 + tc
                    E.wait("tensor", sLD, "b_wdt_%d" % db)
                    E.wait("tensor", sLD, "b_pro")
                    if idx >= 2:
                        E.wait("tensor", sAC, "b_spexp_%d" % (idx - 2))
                    if db == 0 and tc == 0:
                        E.wait("tensor", sAC, "b_bc_%d" % (NH * 8 - 1))
                    pb = psP[tc % 2]
                    for rc in range(2):
                        inst = tensor.matmul(
                            pb[:],
                            T.wdtr[par][:, rc * P:(rc + 1) * P],
                            T.dtrg[rc][:, tc * 512:(tc + 1) * 512],
                            start=(rc == 0), stop=(rc == 1))
                    inst.then_inc(sTE, 1)
                    E.inc(sTE)
                    E.mark(sTE, "b_dtmm_%d" % idx)
                    E.wait("scalar", sTE, "b_dtmm_%d" % idx)
                    if db >= 1 and tc == 0:
                        E.wait("scalar", sDV, "b_u_%d" % (db - 1))
                    scalar.activation(T.etmp[tc % 2][:], psP[tc % 2][:],
                                      AF.Exp, bias=dtb_sb[:, db:db + 1]
                                      ).then_inc(sAC, 1)
                    E.inc(sAC)
                    E.mark(sAC, "b_spexp_%d" % idx)
                    if tc >= 1:
                        scalar.activation(
                            T.dtT[0][:, (tc - 1) * 512:tc * 512],
                            T.etmp[(tc - 1) % 2][:], AF.Ln, bias=1.0
                            ).then_inc(sAC, 1)
                        E.inc(sAC)
                        E.mark(sAC, "b_spln_%d" % (idx - 1))
                scalar.activation(T.dtT[0][:, 3 * 512:4 * 512],
                                  T.etmp[1][:], AF.Ln, bias=1.0
                                  ).then_inc(sAC, 1)
                E.inc(sAC)
                E.mark(sAC, "b_spln_%d" % (db * 4 + 3))
                E.mark(sAC, "b_dtT_%d" % db)
                E.wait("vector", sAC, "b_dtT_%d" % db)
                E.wait("vector", sLD, "b_xcg_%d" % db)
                vector.scalar_tensor_tensor(T.u[0][:], T.dtT[0][:], 1.0,
                                            T.xcg[0][:], ALU.mult, ALU.mult
                                            ).then_inc(sDV, 1)
                E.inc(sDV)
                E.mark(sDV, "b_u_%d" % db)

                for np_ in range(NH // 2):
                    for j in range(2):
                        n = np_ * 2 + j
                        npar = n % 3
                        E.wait("scalar", sAC, "b_dtT_%d" % db)
                        if db > 0 or n >= 3:
                            pdb, pn = (db, n - 3) if n >= 3 else                                 (db - 1, NH - 3 + n)
                            E.wait("scalar", sDV, "b_scan_%d_%d" % (pdb, pn))
                        scalar.activation(
                            T.dA[npar][:], T.dtT[0][:], AF.Exp,
                            scale=adev_sb[:, db * NH + n:db * NH + n + 1]
                            ).then_inc(sAC, 1)
                        E.inc(sAC)
                        E.mark(sAC, "b_dA_%d_%d" % (db, n))
                    for j in range(2):
                        n = np_ * 2 + j
                        dpar = n % 3
                        if db > 0 or n >= 3:
                            pdb, pn = (db, n - 3) if n >= 3 else                                 (db - 1, NH - 3 + n)
                            E.wait("vector", sDV, "b_scan_%d_%d" % (pdb, pn))
                        vector.tensor_tensor(
                            T.dBx[dpar][:], T.u[0][:],
                            T.B_rep[:, n * TFULL:(n + 1) * TFULL], ALU.mult
                            ).then_inc(sDV, 1)
                        E.inc(sDV)
                    for j in range(2):
                        n = np_ * 2 + j
                        npar = n % 3
                        dpar = n % 3
                        E.wait("vector", sAC, "b_dA_%d_%d" % (db, n))
                        if n == 0 and db >= 2:
                            E.wait("vector", sTE, "b_ys_%d" % (db - 2))
                        if n == 6 and db >= 2:
                            E.wait("vector", sST, "b_yst_%d" % (db - 2))
                        vector.tensor_tensor_scan(
                            T.hbig[par][:, n * TFULL:(n + 1) * TFULL],
                            T.dA[npar][:], T.dBx[dpar][:], 0.0,
                            ALU.mult, ALU.add).then_inc(sDV, 1)
                        E.inc(sDV)
                        E.mark(sDV, "b_scan_%d_%d" % (db, n))
                vector.tensor_tensor(T.hbig[par][:], T.hbig[par][:],
                                     T.C_rep[:], ALU.mult).then_inc(sDV, 1)
                E.inc(sDV)
                E.mark(sDV, "b_ch_%d" % db)
                E.wait("tensor", sDV, "b_ch_%d" % db)
                if db >= 1:
                    E.wait("tensor", sAC, "b_yev_%d" % (db - 1))
                for tc in range(4):
                    for n in range(NH):
                        inst = tensor.matmul(
                            psP[4 + tc][:], ident_b[:],
                            T.hbig[par][:, n * TFULL + tc * 512:
                                        n * TFULL + (tc + 1) * 512],
                            start=(n == 0), stop=(n == NH - 1))
                inst.then_inc(sTE, 1)
                E.inc(sTE)
                E.mark(sTE, "b_ys_%d" % db)
                E.wait("scalar", sTE, "b_ys_%d" % db)
                if db >= 1:
                    E.wait("scalar", sST, "b_yst_%d" % (db - 1))
                for tc in range(4):
                    inst = scalar.activation(
                        T.ysbv[par][:, tc * 512:(tc + 1) * 512],
                        psP[4 + tc][:], AF.Copy)
                inst.then_inc(sAC, 1)
                E.inc(sAC)
                E.mark(sAC, "b_yev_%d" % db)
                E.wait("scalar", sAC, "b_yev_%d" % db)
                for rk in range(2):
                    inst = scalar.dma_start(
                        out=yp_d[rk * DIN + db * P:rk * DIN + (db + 1) * P, :],
                        in_=T.ysbv[par][:, rk * TLOC:(rk + 1) * TLOC])
                    inst.then_inc(SS["cyb"], 16)
                    E.inc(SS["cyb"], 16)
                E.mark(SS["cyb"], "b_yst_%d" % db)
            ns_sc.__exit__(None, None, None)
            E.wait("gpsimd", sST, "b_yst_63")
            gpsimd.collective_compute(
                "ReduceScatter", ALU.add, replica_groups=GROUPS,
                ins=[yp_d.opt()], outs=[yown_d.opt()]).then_inc(cc4, 1)

        with ExitStack() as sB:
            def sbb(name, shape, dt=F32):
                return sB.enter_context(nc.sbuf_tensor(name, shape, dt))
            TB = TensA()
            TB.dtrg = [sbb("dtrg%d" % i, [P, TFULL], BF16) for i in range(2)]

            TB.eselb = sbb("eselb", [16, NH * P], BF16)
            TB.wdtr = [sbb("wdtr%d" % i, [P, 2 * P], BF16) for i in range(2)]
            TB.B_rep = sbb("B_rep", [P, NH * TFULL], BF16)
            TB.C_rep = sbb("C_rep", [P, NH * TFULL], BF16)
            TB.xcg = [sbb("xcg0", [P, TFULL], BF16)]
            TB.dtT = [sbb("dtT0", [P, TFULL])]
            TB.eself = TB.dtT[0][0:16, 0:NH * P]
            TB.etmp = [sbb("etmp%d" % i, [P, 512]) for i in range(2)]
            TB.u = [sbb("u", [P, TFULL], BF16)]
            TB.dA = [sbb("dA%d" % i, [P, TFULL]) for i in range(3)]
            TB.dBx = [sbb("dBx%d" % i, [P, TFULL], BF16) for i in range(3)]
            TB.hbig = [sbb("hbig%d" % i, [P, NH * TFULL], BF16) for i in range(2)]
            TB.bmTg = TB.hbig[0][0:16, 0:TFULL]
            TB.cmTg = TB.hbig[0][0:16, TFULL:2 * TFULL]
            TB.ysbv = [TB.hbig[0][:, 6 * TFULL:7 * TFULL],
                       TB.hbig[1][:, 6 * TFULL:7 * TFULL]]
            TB.ysb = [None]

            if "B" in PHASES:
                base_ctr = dict(E.ctr)
                E.start_pass({e: _MockEng() for e in ENGINES})
                E.ctr = dict(base_ctr)
                emit_Bv2({e: _MockEng() for e in ENGINES}, TB)
                snapB = dict(E.ctr)
                E.end_pass()
                with nc.Block() as block:
                    for engname in ENGINES:
                        def mkB(engname):
                            def body(eng):
                                engs = {e: _MockEng() for e in ENGINES}
                                engs[engname] = eng
                                E.start_pass(engs)
                                E.ctr = dict(base_ctr)
                                emit_Bv2(engs, TB)
                                assert E.ctr == snapB
                            return body
                        getattr(block, engname)(mkB(engname))
                E.ctr = dict(snapB)

        def emit_C(G, T):
            sync = G["sync"]; vector = G["vector"]; scalar = G["scalar"]
            tensor = G["tensor"]; gpsimd = G["gpsimd"]

            ns_g = nc.named_scope("gate"); ns_g.__enter__()
            for db in (range(64) if "g" in CPARTS else []):
                par = db % 2
                E.wait_val("sync", cc4, 1)
                if db == 0:
                    E.wait("sync", sST, "p1_stxb_62")
                    E.wait("sync", sST, "p1_stxb_63")
                    E.wait("sync", sST, "p1_stz_126")
                    E.wait("sync", sST, "p1_stz_127")
                if db >= 2:
                    E.wait("sync", sDV, "c_yg_%d" % (db - 2))
                gsem = SS["g%d" % par]
                sync.dma_start(out=T.yown[par][:],
                               in_=yown_d[db * P:(db + 1) * P, :]
                               ).then_inc(gsem, 16)
                sync.dma_start(out=T.xcL[par][:],
                               in_=xcb_d[db * P:(db + 1) * P, :]
                               ).then_inc(gsem, 16)
                sync.dma_start(out=T.szL[par][:],
                               in_=szt_d[db * P:(db + 1) * P, :]
                               ).then_inc(gsem, 16)
                E.inc(gsem, 48)
                E.mark(gsem, "c_ld_%d" % db)
                E.wait("vector", sLD, "c_ld_%d" % db)
                vector.scalar_tensor_tensor(T.t1[par][:], T.xcL[par][:],
                                            dp_sb[:, db:db + 1],
                                            T.yown[par][:], ALU.mult, ALU.add)
                vector.scalar_tensor_tensor(
                    T.ygate[:, db * TLOC:(db + 1) * TLOC],
                    T.t1[par][:], 1.0, T.szL[par][:], ALU.mult, ALU.mult
                    ).then_inc(sDV, 1)
                E.inc(sDV)
                E.mark(sDV, "c_yg_%d" % db)

            ns_g.__exit__(None, None, None)
            ns_op = nc.named_scope("oproj"); ns_op.__enter__()
            for hc in (range(32) if "o" in CPARTS else []):
                for ck in range(2):
                    it = hc * 2 + ck
                    if it >= 2:
                        E.wait("sync", sAC, "c_wc_%d" % (it - 2))
                    osem = SS["o%d" % ck]
                    for hv in range(2):
                        inst = sync.dma_start(
                            out=T.wo2[ck][:, hv * 2048:(hv + 1) * 2048],
                            in_=woutT_t[hc][:, ck * 4096 + hv * 2048:
                                            ck * 4096 + (hv + 1) * 2048])
                        inst.then_inc(osem, 16)
                        E.inc(osem, 16)
                    E.mark(osem, "c_wo_%d" % it)
                    E.wait("scalar", sLD, "c_wo_%d" % it)
                    if it >= 2:
                        E.wait("scalar", sTE, "c_mm_%d_%d" % (hc - 1, ck))
                    scalar.activation(T.wo2b[ck][:], T.wo2[ck][:], AF.Copy
                                      ).then_inc(sAC, 1)
                    E.inc(sAC)
                    E.mark(sAC, "c_wc_%d" % it)
                for ck in range(2):
                    E.wait("tensor", sAC, "c_wc_%d" % (hc * 2 + ck))
                    if ck == 0:
                        E.wait("tensor", sDV, "c_yg_63")
                        if hc >= 1:
                            E.wait("tensor", sAC, "c_sev_%d" % (hc - 1))
                    for dcl in range(32):
                        dc = ck * 32 + dcl
                        for tc in range(2):
                            inst = tensor.matmul(
                                psP[6 + tc][:],
                                T.wo2b[ck][:, dcl * P:(dcl + 1) * P],
                                T.ygate[:, dc * TLOC + tc * 512:
                                        dc * TLOC + (tc + 1) * 512],
                                start=(dc == 0), stop=(dc == 63))
                    inst.then_inc(sTE, 1)
                    E.inc(sTE)
                    E.mark(sTE, "c_mm_%d_%d" % (hc, ck))
                E.mark(sTE, "c_mm_%d" % hc)
                E.wait("scalar", sTE, "c_mm_%d_1" % hc)
                if hc >= 1:
                    E.wait("scalar", sST, "c_sst_%d" % (hc - 1))
                scalar.activation(T.stt_[0][:, 0:512], psP[6][:], AF.Copy)
                scalar.activation(T.stt_[0][:, 512:1024], psP[7][:],
                                  AF.Copy).then_inc(sAC, 1)
                E.inc(sAC)
                E.mark(sAC, "c_sev_%d" % hc)
                E.wait("scalar", sAC, "c_sev_%d" % hc)
                scalar.dma_start(out=st_d[hc * P:(hc + 1) * P, :],
                                 in_=T.stt_[0][:]).then_inc(SS["cst"], 16)
                E.inc(SS["cst"], 16)
                E.mark(SS["cst"], "c_sst_%d" % hc)

            ns_op.__exit__(None, None, None)
            ns_fin = nc.named_scope("fin"); ns_fin.__enter__()
            for tt in (range(8) if "f" in CPARTS else []):
                par = tt % 2
                E.wait("sync", sST, "c_sst_31")
                E.wait("sync", sST, "ffn_st_7_6")
                E.wait("sync", sST, "ffn_st_7_7")
                E.wait("sync", sTE, "c_mm_31_1")
                if tt >= 2:
                    E.wait("sync", sDV, "c_out_%d" % (tt - 2))
                fsem = SS["f%d" % par]
                sync.dma_start(out=T.s_tok[par][:],
                               in_=st_d[:, tt * P:(tt + 1) * P],
                               transpose=True).then_inc(fsem, 16)
                sync.dma_start(out=T.g_tok[par][:],
                               in_=gt_d[tt * P:(tt + 1) * P, :]
                               ).then_inc(fsem, 16)
                sync.dma_start(out=T.xrs[par][:],
                               in_=hs_ext[3 + tt * P:3 + (tt + 1) * P, :]
                               ).then_inc(fsem, 16)
                E.inc(fsem, 48)
                E.mark(fsem, "c_fin_%d" % tt)
                E.wait("vector", sLD, "c_fin_%d" % tt)
                if tt >= 2:
                    E.wait("vector", sST, "c_ost_%d" % (tt - 2))
                vector.tensor_tensor(T.sg[par][:], T.s_tok[par][:],
                                     T.g_tok[par][:], ALU.mult)
                vector.scalar_tensor_tensor(T.outt[par][:], T.sg[par][:],
                                            1.0, T.xrs[par][:],
                                            ALU.mult, ALU.add).then_inc(sDV, 1)
                E.inc(sDV)
                E.mark(sDV, "c_out_%d" % tt)
                E.wait("scalar", sDV, "c_out_%d" % tt)
                scalar.dma_start(out=out_ext[tt * P:(tt + 1) * P, :],
                                 in_=T.outt[par][:]
                                 ).then_inc(SS["cot%d" % par], 16)
                E.inc(SS["cot%d" % par], 16)
                E.mark(SS["cot%d" % par], "c_ost_%d" % tt)
            if DEBUG_TAPS:
                for nm, srcd in [("d_xct", xct_d), ("d_szt", szt_d),
                                 ("d_dtr", dtr_d), ("d_bc", bc_d),
                                 ("d_yown", yown_d), ("d_gt", gt_d),
                                 ("d_st", st_d), ("d_yp", yp_d),
                                 ("d_wf", wffn_scr),
                                 ("d_xcbg", xcb_g)]:
                    scalar.dma_start(out=dbg[nm], in_=srcd).then_inc(sST, 16)
                    E.inc(sST, 16)
            ns_fin.__exit__(None, None, None)
            if "f" in CPARTS:
                E.wait("scalar", sST, "c_ost_6")
                E.wait("scalar", sST, "c_ost_7")

        with ExitStack() as sC:
            def sbc(name, shape, dt=F32):
                return sC.enter_context(nc.sbuf_tensor(name, shape, dt))
            TC = TensA()
            TC.ygate = sbc("ygate", [P, 64 * TLOC], BF16)
            TC.yown = [sbc("yown%d" % i, [P, TLOC], BF16) for i in range(2)]
            TC.t1 = [sbc("t1%d" % i, [P, TLOC]) for i in range(2)]
            TC.xcL = [sbc("xcL%d" % i, [P, TLOC], BF16) for i in range(2)]
            TC.szL = [sbc("szL%d" % i, [P, TLOC], BF16) for i in range(2)]

            TC.wo2 = [sbc("wo2_%d" % i, [P, 32 * P]) for i in range(2)]
            TC.wo2b = [sbc("wo2b_%d" % i, [P, 32 * P], BF16) for i in range(2)]
            TC.stt_ = [sbc("stt_0", [P, TLOC], BF16)]
            ygf = TC.ygate[:].bitcast(F32)
            TC.xrs = [ygf[:, 0:H], ygf[:, H:2 * H]]
            TC.outt = [ygf[:, 2 * H:3 * H], ygf[:, 3 * H:4 * H]]
            TC.s_tok = [TC.ygate[:, 8 * H:9 * H], TC.ygate[:, 9 * H:10 * H]]
            TC.g_tok = [TC.ygate[:, 10 * H:11 * H], TC.ygate[:, 11 * H:12 * H]]
            TC.sg = [TC.ygate[:, 12 * H:13 * H], TC.ygate[:, 13 * H:14 * H]]

            if "C" in PHASES:
                base_ctr = dict(E.ctr)
                E.start_pass({e: _MockEng() for e in ENGINES})
                E.ctr = dict(base_ctr)
                emit_C({e: _MockEng() for e in ENGINES}, TC)
                snapC = dict(E.ctr)
                E.end_pass()
                with nc.Block() as block:
                    for engname in ENGINES:
                        def mkC(engname):
                            def body(eng):
                                engs = {e: _MockEng() for e in ENGINES}
                                engs[engname] = eng
                                E.start_pass(engs)
                                E.ctr = dict(base_ctr)
                                emit_C(engs, TC)
                                assert E.ctr == snapC
                            return body
                        getattr(block, engname)(mkC(engname))

    return nc


def _stage_inputs(inputs):
    hs = inputs["L0_attn_hidden_states"]
    win = inputs["L0_attn_in_proj_w"]
    convw = inputs["L0_attn_conv_w"]
    convb = inputs["L0_attn_conv_b"]
    xpw = inputs["L0_attn_x_proj_w"]
    dtw = inputs["L0_attn_dt_proj_w"]
    dtb = inputs["L0_attn_dt_bias"]
    alog = inputs["L0_attn_A_log"]
    dpar = inputs["L0_attn_D_param"]
    wout = inputs["L0_attn_out_proj_w"]
    dtln = inputs["L0_attn_dt_ln_w"]
    bln = inputs["L0_attn_B_ln_w"]
    cln = inputs["L0_attn_C_ln_w"]
    qw = inputs["L0_ffn_qweight"]
    scl = inputs["L0_ffn_scales"]
    zrs = inputs["L0_ffn_scaled_zeros"]
    fbias = inputs["L0_ffn_bias"]

    f32 = np.float32
    win_t = np.ascontiguousarray(
        win.reshape(32, P, 128, P).transpose(2, 1, 0, 3)
    ).reshape(128, P, 32 * P).astype(f32)
    woT = np.ascontiguousarray(wout.T)
    woutT_t = np.ascontiguousarray(
        woT.reshape(64, P, 32, P).transpose(2, 1, 0, 3)
    ).reshape(32, P, 64 * P).astype(f32)
    wdtT = np.ascontiguousarray(dtw.T).astype(f32)
    wxpT_t = np.ascontiguousarray(xpw.T).reshape(64, P, 288).astype(f32)
    qw2d = np.ascontiguousarray(qw.reshape(H, H)).astype(np.int32)
    scales2 = np.ascontiguousarray(scl.reshape(H, 32)).astype(f32)
    zeros2 = np.ascontiguousarray(zrs.reshape(H, 32)).astype(f32)
    cw4 = np.ascontiguousarray(
        convw.reshape(64, P, 4).transpose(1, 0, 2)).reshape(P, 256).astype(f32)
    cb1 = np.ascontiguousarray(convb.reshape(64, P).T).astype(f32)
    dtb1 = np.ascontiguousarray(dtb.reshape(64, P).T).astype(f32)
    dp1 = np.ascontiguousarray(dpar.reshape(64, P).T).astype(f32)
    fb = np.ascontiguousarray(np.asarray(fbias).reshape(32, P).T).astype(f32)
    fb_row = np.ascontiguousarray(np.asarray(fbias).reshape(1, H)).astype(f32)
    wln = np.concatenate([dtln, bln, cln]).reshape(1, 288).astype(f32)
    ident = np.eye(P, dtype=f32)

    shared = dict(win_t=win_t, woutT_t=woutT_t, wdtT_in=wdtT, wxpT_t=wxpT_t,
                  qw2d=qw2d, scales2=scales2, zeros2=zeros2, cw4=cw4, cb1=cb1,
                  dtb1=dtb1, dp1=dp1, fbias_in=fb, fbias_row=fb_row,
                  wln_in=wln, ident_in=ident)

    in_maps = []
    for c in range(8):
        b, hf = c // 2, c % 2
        t0 = hf * TLOC
        if hf == 0:
            halo = np.zeros((3, H), dtype=f32)
        else:
            halo = hs[b, t0 - 3:t0, :].astype(f32)
        hs_ext = np.ascontiguousarray(
            np.concatenate([halo, hs[b, t0:t0 + TLOC, :].astype(f32)], axis=0))
        alog_l = np.ascontiguousarray(
            alog.reshape(64, P, 16)[:, :, hf * NH:(hf + 1) * NH]
            .transpose(1, 0, 2)).reshape(P, 64 * NH).astype(f32)
        esel = np.zeros((16, NH, P), dtype=f32)
        for n in range(NH):
            esel[hf * NH + n, n, :] = 1.0
        esel = esel.reshape(16, NH * P)
        m = dict(shared)
        m.update(hs_ext=hs_ext, alog_l=alog_l, esel_in=esel)
        in_maps.append(m)
    return in_maps


_NC_CACHE = {}


def kernel(**inputs):
    in_maps = _stage_inputs(inputs)
    if "nc" not in _NC_CACHE:
        _NC_CACHE["nc"] = build_program()
    nc = _NC_CACHE["nc"]
    res = run_bass_kernel_spmd(nc, in_maps, core_ids=list(range(8)))
    outs = [res.results[c]["out_ext"] for c in range(8)]
    full = np.concatenate(outs, axis=0).reshape(4, 2048, H).astype(np.float32)
    return full


# version 18
# speedup vs baseline: 1.0790x; 1.0292x over previous
import numpy as np
import ml_dtypes
from contextlib import ExitStack
import concourse.bass as bass
import concourse.mybir as mybir
from concourse.bass_utils import run_bass_kernel_spmd

F32 = mybir.dt.float32
BF16 = mybir.dt.bfloat16
I32 = mybir.dt.int32
AF = mybir.ActivationFunctionType
ALU = mybir.AluOpType
AXX = mybir.AxisListType.X

P = 128
H = 4096
DIN = 8192
NH = 8
RK = 256
TLOC = 1024
TEXT = 1027
TFULL = 2048
BF = ml_dtypes.bfloat16
GROUPS = [[0, 1], [2, 3], [4, 5], [6, 7]]
DEBUG_TAPS = False
PHASES = "ABC"
CPARTS = "gof"

ENGINES = ("sync", "vector", "scalar", "tensor", "gpsimd")


class _MockInst:
    def then_inc(self, sem, n=1):
        return self


class _MockEng:
    def __getattr__(self, name):
        def f(*a, **k):
            return _MockInst()
        return f


class Emit:

    def __init__(self):
        self.keys = {}
        self.first = True

    def start_pass(self, engines):
        self.eng = engines
        self.ctr = {}

    def end_pass(self):
        self.first = False

    def cnt(self, sem):
        return self.ctr.get(id(sem), 0)

    def inc(self, sem, n=1):
        self.ctr[id(sem)] = self.cnt(sem) + n

    def mark(self, sem, key):
        v = self.cnt(sem)
        if key not in self.keys:
            self.keys[key] = (sem, v)
        else:
            psem, pv = self.keys[key]
            assert psem is sem and pv == v, (key, pv, v)

    def wait(self, engname, sem, key):
        msem, v = self.keys[key]
        if v > 0:
            self.eng[engname].wait_ge(msem, v)

    def wait_val(self, engname, sem, v):
        if v > 0:
            self.eng[engname].wait_ge(sem, v)


def build_program():
    nc = bass.Bass("TRN2", target_bir_lowering=False, debug=False)

    def ein(name, shape, dt=F32):
        return nc.dram_tensor(name, shape, dt, kind="ExternalInput").ap()

    hs_ext = ein("hs_ext", [TEXT, H])
    win_t = ein("win_t", [128, P, 32 * P])
    woutT_t = ein("woutT_t", [32, P, 64 * P])
    wdtT_in = ein("wdtT_in", [RK, DIN])
    wxpT_t = ein("wxpT_t", [64, P, 288])
    qw2d = ein("qw2d", [H, H], I32)
    scales2 = ein("scales2", [H, 32])
    zeros2 = ein("zeros2", [H, 32])
    alog_l = ein("alog_l", [P, 64 * NH])
    cw4 = ein("cw4", [P, 64 * 4])
    cb1 = ein("cb1", [P, 64])
    dtb1 = ein("dtb1", [P, 64])
    dp1 = ein("dp1", [P, 64])
    fbias_in = ein("fbias_in", [P, 32])
    wln_in = ein("wln_in", [1, 288])
    ident_in = ein("ident_in", [P, P])
    esel_in = ein("esel_in", [16, NH * P])
    fbias_row = ein("fbias_row", [1, H])

    out_ext = nc.dram_tensor("out_ext", [TLOC, H], F32, kind="ExternalOutput").ap()
    dbg = {}
    if DEBUG_TAPS:
        for nm, shape, dt in [("d_xct", [DIN, TLOC], F32),
                              ("d_szt", [DIN, TLOC], F32),
                              ("d_dtr", [RK, TLOC], BF16),
                              ("d_bc", [32, TLOC], BF16),
                              ("d_yown", [DIN, TLOC], BF16),
                              ("d_gt", [H, TLOC], BF16),
                              ("d_st", [H, TLOC], BF16),
                              ("d_yp", [2 * DIN, TLOC], BF16),
                              ("d_wf", [H, H], BF16),
                              ("d_xcbg", [2 * DIN, TLOC], BF16)]:
            dbg[nm] = nc.dram_tensor(nm, shape, dt, kind="ExternalOutput").ap()

    def din_(name, shape, dt):
        return nc.dram_tensor(name, shape, dt).ap()

    xct_d = din_("xct_d", [DIN, TLOC], F32)
    xcb_d = din_("xcb_d", [DIN, TLOC], BF16)
    xcb_g = din_("xcb_g", [2 * DIN, TLOC], BF16)
    szt_d = din_("szt_d", [DIN, TLOC], BF16)
    dtr_d = din_("dtr_d", [RK, TLOC], BF16)
    dtr_g = din_("dtr_g", [2 * RK, TLOC], BF16)
    bc_d = din_("bc_d", [32, TLOC], BF16)
    bc_g = din_("bc_g", [64, TLOC], BF16)
    gt_d = din_("gt_d", [TLOC, H], BF16)
    st_d = din_("st_d", [H, TLOC], BF16)
    wffn_scr = din_("wffn_scr", [H, H], BF16)
    wout_scr = din_("wout_scr", [32, P, 64 * P], BF16)
    wdt_scr = din_("wdt_scr", [2, P, DIN], BF16)
    yp_d = din_("yp_d", [2 * DIN, TLOC], BF16)
    yown_d = din_("yown_d", [DIN, TLOC], BF16)

    E = Emit()

    with ExitStack() as top:
        def sem(name):
            return top.enter_context(nc.semaphore(name))

        sLD = sem("sLD")
        sST = sem("sST")
        sTE = sem("sTE")
        sDV = sem("sDV")
        sAC = sem("sAC")
        sPL = sem("sPL")
        cc1 = sem("cc1")
        cc2 = sem("cc2")
        cc3 = sem("cc3")
        cc4 = sem("cc4")
        SS = {}
        for _nm in ["sCN", "h0", "h1", "w0", "w1", "q0", "q1", "xl0", "xl1",
                    "sBC", "sWT", "sWS", "t0", "t1", "t2", "t3", "cwq",
                    "cxc0", "cxc1", "cxb0", "cxb1", "csz0", "csz1",
                    "cgt0", "cgt1", "sXG", "wd0", "wd1", "cyb", "g0", "g1",
                    "o0", "o1", "cst", "f0", "f1", "cot0", "cot1"]:
            SS[_nm] = sem(_nm)

        def sbt(name, shape, dt=F32):
            return top.enter_context(nc.sbuf_tensor(name, shape, dt))

        ident_f = sbt("ident_f", [P, P])
        ident_b = sbt("ident_b", [P, P], BF16)
        ones1 = sbt("ones1", [1, P])
        cw_sb = sbt("cw_sb", [P, 64 * 4])
        cb_sb = sbt("cb_sb", [P, 64])
        dtb_sb = sbt("dtb_sb", [P, 64])
        dp_sb = sbt("dp_sb", [P, 64])
        fb_sb = sbt("fb_sb", [P, 32])
        wln_sb = sbt("wln_sb", [1, 288])
        wlnbc = sbt("wlnbc", [P, 288])
        alog_sb = sbt("alog_sb", [P, 64 * NH])
        epsc = sbt("epsc", [P, 2])
        adev_sb = sbt("adev_sb", [P, 64 * NH])
        nst = sbt("nst", [P, 48])

        psP = [top.enter_context(nc.psum_tensor("psP%d" % i, [P, 512], F32))
               for i in range(8)]


        def emit_A(G, T):
            sync = G["sync"]; vector = G["vector"]; scalar = G["scalar"]
            tensor = G["tensor"]; gpsimd = G["gpsimd"]

            for i, (dst, src) in enumerate([
                    (T.cw_l, cw4), (T.cb_l, cb1), (T.dtb_l, dtb1),
                    (T.dp_l, dp1), (T.fb_l, fbias_in), (T.wln_l, wln_in),
                    (T.id_l, ident_in), (T.al_l, alog_l)]):
                sync.dma_start(out=dst, in_=src).then_inc(SS["sCN"], 16)
                E.inc(SS["sCN"], 16)
            E.mark(SS["sCN"], "consts")
            E.wait("gpsimd", sLD, "consts")
            gpsimd.memset(epsc[:, 0:1], 1e-5)
            gpsimd.memset(epsc[:, 1:2], 1e-6)
            gpsimd.memset(ones1[:], 1.0)
            gpsimd.tensor_copy(ident_b[:], ident_f[:]).then_inc(sPL, 1)
            E.inc(sPL)
            E.mark(sPL, "idb")
            E.wait("tensor", sPL, "idb")
            tensor.matmul(psP[0][:, 0:288], ones1[:], wln_sb[0:1, :],
                          start=True, stop=True).then_inc(sTE, 1)
            E.inc(sTE)
            E.mark(sTE, "wlnbc_mm")
            E.wait("vector", sTE, "wlnbc_mm")
            vector.tensor_copy(wlnbc[:], psP[0][:, 0:288]).then_inc(sDV, 1)
            E.inc(sDV)
            E.mark(sDV, "wlnbc")

            NT = 9
            ns_p0 = nc.named_scope("p0"); ns_p0.__enter__()
            for tt in range(NT):
                if tt == 8:
                    E.wait("vector", sDV, "p0_apply_6")
                    vector.memset(T.hs_t[0][:], 0.0).then_inc(sDV, 1)
                    E.inc(sDV)
                    E.mark(sDV, "p0_ms8")
                if tt >= 2:
                    E.wait("sync", sDV, "p0_apply_%d" % (tt - 2))
                hsem = SS["h%d" % (tt % 2)]
                if tt < 8:
                    sync.dma_start(out=T.hs_t[tt % 2][:],
                                   in_=hs_ext[tt * P:(tt + 1) * P, :]
                                   ).then_inc(hsem, 16)
                else:
                    E.wait("sync", sDV, "p0_ms8")
                    sync.dma_start(out=T.hs_t[0][0:3, :],
                                   in_=hs_ext[1024:1027, :]).then_inc(hsem, 16)
                E.inc(hsem, 16)
                E.mark(hsem, "p0_ld_%d" % tt)

                E.wait("vector", sLD, "p0_ld_%d" % tt)
                vector.tensor_reduce(nst[:, tt:tt + 1], T.hs_t[tt % 2][:],
                                     AXX, ALU.add, negate=True)
                E.wait("scalar", sLD, "p0_ld_%d" % tt)
                if tt >= 1:
                    E.wait("scalar", sDV, "p0_vsum_%d" % (tt - 1))
                scalar.activation(T.sq_t[:], T.hs_t[tt % 2][:], AF.Square
                                  ).then_inc(sAC, 1)
                E.inc(sAC)
                E.mark(sAC, "p0_sq_%d" % tt)
                E.wait("vector", sAC, "p0_sq_%d" % tt)
                vector.tensor_reduce(nst[:, 9 + tt:10 + tt], T.sq_t[:],
                                     AXX, ALU.add).then_inc(sDV, 1)
                E.inc(sDV)
                E.mark(sDV, "p0_vsum_%d" % tt)
                vector.tensor_scalar(nst[:, tt:tt + 1], nst[:, tt:tt + 1],
                                     1.0 / H, None, ALU.mult).then_inc(sDV, 1)
                E.inc(sDV)
                E.mark(sDV, "p0_negmean_%d" % tt)
                E.wait("gpsimd", sDV, "p0_negmean_%d" % tt)
                gpsimd.tensor_scalar(nst[:, 9 + tt:10 + tt],
                                     nst[:, 9 + tt:10 + tt],
                                     1.0 / H, None, ALU.mult)
                gpsimd.tensor_tensor(nst[:, 18 + tt:19 + tt],
                                     nst[:, tt:tt + 1], nst[:, tt:tt + 1],
                                     ALU.mult).then_inc(sPL, 1)
                E.inc(sPL)
                E.mark(sPL, "p0_m2_%d" % tt)
                E.wait("vector", sPL, "p0_m2_%d" % tt)
                vector.scalar_tensor_tensor(nst[:, 27 + tt:28 + tt],
                                            nst[:, 9 + tt:10 + tt], 1.0,
                                            nst[:, 18 + tt:19 + tt],
                                            ALU.mult, ALU.subtract
                                            ).then_inc(sDV, 1)
                E.inc(sDV)
                E.mark(sDV, "p0_va_%d" % tt)
                E.wait("scalar", sDV, "p0_va_%d" % tt)
                E.wait("scalar", sPL, "idb")
                scalar.activation(nst[:, 9 + tt:10 + tt],
                                  nst[:, 27 + tt:28 + tt], AF.Ln,
                                  bias=epsc[:, 0:1]).then_inc(sAC, 1)
                E.inc(sAC)
                E.mark(sAC, "p0_ln_%d" % tt)
                E.wait("vector", sAC, "p0_ln_%d" % tt)
                vector.tensor_scalar(nst[:, 27 + tt:28 + tt],
                                     nst[:, 9 + tt:10 + tt], -0.5, None,
                                     ALU.mult).then_inc(sDV, 1)
                E.inc(sDV)
                E.mark(sDV, "p0_lnh_%d" % tt)
                E.wait("scalar", sDV, "p0_lnh_%d" % tt)
                scalar.activation(nst[:, 9 + tt:10 + tt],
                                  nst[:, 27 + tt:28 + tt], AF.Exp
                                  ).then_inc(sAC, 1)
                E.inc(sAC)
                E.mark(sAC, "p0_rstd_%d" % tt)
                E.wait("gpsimd", sAC, "p0_rstd_%d" % tt)
                gpsimd.tensor_tensor(nst[:, 18 + tt:19 + tt],
                                     nst[:, tt:tt + 1],
                                     nst[:, 9 + tt:10 + tt], ALU.mult
                                     ).then_inc(sPL, 1)
                E.inc(sPL)
                E.mark(sPL, "p0_bias_%d" % tt)
                E.wait("vector", sPL, "p0_bias_%d" % tt)
                if tt >= 1:
                    E.wait("vector", sTE, "p0_tr_%d_7" % (tt - 1))
                vector.tensor_scalar(T.xn_t[:], T.hs_t[tt % 2][:],
                                     nst[:, 9 + tt:10 + tt],
                                     nst[:, 18 + tt:19 + tt],
                                     ALU.mult, ALU.add).then_inc(sDV, 1)
                E.inc(sDV)
                E.mark(sDV, "p0_apply_%d" % tt)
                ncol = P if tt < 8 else 3
                for q in range(8):
                    E.wait("tensor", sDV, "p0_apply_%d" % tt)
                    E.wait("tensor", sPL, "idb")
                    if tt > 0 or q >= 2:
                        prev = (tt * 8 + q) - 2
                        E.wait("tensor", sDV, "p0_ev_%d" % prev)
                    pb = psP[6 + q % 2]
                    for j in range(4):
                        hc = q * 4 + j
                        inst = tensor.transpose(
                            pb[:, j * P:(j + 1) * P],
                            T.xn_t[:, hc * P:(hc + 1) * P], ident_f[:])
                    inst.then_inc(sTE, 1)
                    E.inc(sTE)
                    E.mark(sTE, "p0_tr_%d_%d" % (tt, q))
                    E.wait("vector", sTE, "p0_tr_%d_%d" % (tt, q))
                    for j in range(4):
                        hc = q * 4 + j
                        inst = vector.tensor_copy(
                            xnT[:, hc * TEXT + tt * P:hc * TEXT + tt * P + ncol],
                            pb[:, j * P:j * P + ncol])
                    inst.then_inc(sDV, 1)
                    E.inc(sDV)
                    E.mark(sDV, "p0_ev_%d" % (tt * 8 + q))

            ns_p0.__exit__(None, None, None)
            ns_p1 = nc.named_scope("p1"); ns_p1.__enter__()
            for fc in range(128):
                xpart = fc < 64
                db = fc
                par = fc % 2
                if fc >= 8 and (fc - 8) % 3 == 0 and (fc - 8) // 3 < 32:
                    ot = (fc - 8) // 3
                    qpar = ot % 2
                    if ot >= 2:
                        E.wait("sync", sDV, "dq_w_%d" % (ot - 2))
                    szo = (ot % 2) * 64
                    qsem = SS["q%d" % qpar]
                    sync.dma_start(out=T.qwt[qpar][:],
                                   in_=qw2d[ot * P:(ot + 1) * P, :]
                                   ).then_inc(qsem, 16)
                    sync.dma_start(out=T.scz[:, szo:szo + 32],
                                   in_=scales2[ot * P:(ot + 1) * P, :]
                                   ).then_inc(qsem, 16)
                    sync.dma_start(out=T.scz[:, szo + 32:szo + 64],
                                   in_=zeros2[ot * P:(ot + 1) * P, :]
                                   ).then_inc(qsem, 16)
                    E.inc(qsem, 48)
                    E.mark(qsem, "dq_ld_%d" % ot)
                    E.wait("vector", sLD, "dq_ld_%d" % ot)
                    if ot == 0:
                        E.wait("vector", sTE, "p0_tr_8_7")
                    if ot >= 1:
                        E.wait("vector", sST, "dq_st_%d" % (ot - 1))
                    for g in range(32):
                        inst = vector.tensor_scalar(
                            T.wdq[:, g * P:(g + 1) * P],
                            T.qwt[qpar][:, g * P:(g + 1) * P],
                            T.scz[:, szo + g:szo + g + 1],
                            T.scz[:, szo + 32 + g:szo + 33 + g],
                            ALU.mult, ALU.add)
                    inst.then_inc(sDV, 1)
                    E.inc(sDV)
                    E.mark(sDV, "dq_w_%d" % ot)
                    E.wait("scalar", sDV, "dq_w_%d" % ot)
                    scalar.dma_start(out=wffn_scr[ot * P:(ot + 1) * P, :],
                                     in_=T.wdq[:]).then_inc(SS["cwq"], 16)
                    E.inc(SS["cwq"], 16)
                    E.mark(SS["cwq"], "dq_st_%d" % ot)
                if fc == 0:
                    E.wait("sync", sDV, "p0_vsum_8")
                if fc >= 2:
                    E.wait("sync", sPL, "p1_cast_%d" % (fc - 2))
                wsem = SS["w%d" % par]
                for qv in range(2):
                    sync.dma_start(out=T.wbuf[par][:, qv * 2048:(qv + 1) * 2048],
                                   in_=win_t[fc][:, qv * 2048:(qv + 1) * 2048]
                                   ).then_inc(wsem, 16)
                E.inc(wsem, 32)
                E.mark(wsem, "p1_w_%d" % fc)
                E.wait("gpsimd", sLD, "p1_w_%d" % fc)
                if fc >= 2:
                    E.wait("gpsimd", sTE, "p1_mm_%d" % (fc - 2))
                gpsimd.tensor_copy(T.wbb[par][:], T.wbuf[par][:]
                                   ).then_inc(sPL, 1)
                E.inc(sPL)
                E.mark(sPL, "p1_cast_%d" % fc)
                E.wait("tensor", sPL, "p1_cast_%d" % fc)
                if fc >= 2:
                    E.wait("tensor", sAC, "p1_ev_%d" % (fc - 2))
                if fc == 0:
                    E.mark(sDV, "chk")
                    E.wait("tensor", sDV, "p0_ev_%d" % (9 * 8 - 1))
                tcs = [(0, 512), (512, 1024), (1024, 1027)] if xpart else \
                      [(3, 515), (515, 1027)]
                for ti, (t0, t1) in enumerate(tcs):
                    pb = psP[3 * par + ti]
                    for hc in range(32):
                        inst = tensor.matmul(
                            pb[:, 0:t1 - t0],
                            T.wbb[par][:, hc * P:(hc + 1) * P],
                            xnT[:, hc * TEXT + t0:hc * TEXT + t1],
                            start=(hc == 0), stop=(hc == 31))
                inst.then_inc(sTE, 1)
                E.inc(sTE)
                E.mark(sTE, "p1_mm_%d" % fc)

                if xpart:
                    E.wait("scalar", sTE, "p1_mm_%d" % fc)
                    scalar.activation(T.xsb[par][:, 0:512],
                                      psP[3 * par + 0][:], AF.Copy)
                    scalar.activation(T.xsb[par][:, 512:1024],
                                      psP[3 * par + 1][:], AF.Copy)
                    scalar.activation(T.xsb[par][:, 1024:1027],
                                      psP[3 * par + 2][:, 0:3], AF.Copy
                                      ).then_inc(sAC, 1)
                    E.inc(sAC)
                    E.mark(sAC, "p1_ev_%d" % fc)
                    E.wait("vector", sPL, "idb")
                    if fc >= 2:
                        E.wait("vector", sTE, "p1_conv_%d" % (fc - 2))
                    for k in range(4):
                        inst = vector.tensor_scalar(
                            T.diag[par][:, k * P:(k + 1) * P], ident_f[:],
                            cw_sb[:, db * 4 + k:db * 4 + k + 1], None,
                            ALU.mult)
                    inst.then_inc(sDV, 1)
                    E.inc(sDV)
                    E.mark(sDV, "p1_diag_%d" % fc)
                    E.wait("tensor", sAC, "p1_ev_%d" % fc)
                    E.wait("tensor", sDV, "p1_diag_%d" % fc)
                    if fc >= 1:
                        E.wait("tensor", sAC, "p1_silu_%d" % (fc - 1))
                    for tc2 in range(2):
                        pb = psP[6 + tc2]
                        for k in range(4):
                            inst = tensor.matmul(
                                pb[:],
                                T.diag[par][:, k * P:(k + 1) * P],
                                T.xsb[par][:, tc2 * 512 + k:tc2 * 512 + k + 512],
                                start=(k == 0), stop=(k == 3))
                    inst.then_inc(sTE, 1)
                    E.inc(sTE)
                    E.mark(sTE, "p1_conv_%d" % fc)
                    E.wait("scalar", sTE, "p1_conv_%d" % fc)
                    if fc >= 2:
                        E.wait("scalar", sST, "p1_stxb_%d" % (fc - 2))
                        E.wait("scalar", sDV, "p1_cb_%d" % (fc - 2))
                    scalar.activation(T.xc32[par][:, 0:512], psP[6][:],
                                      AF.Silu, bias=cb_sb[:, db:db + 1])
                    scalar.activation(T.xc32[par][:, 512:1024], psP[7][:],
                                      AF.Silu, bias=cb_sb[:, db:db + 1]
                                      ).then_inc(sAC, 1)
                    E.inc(sAC)
                    E.mark(sAC, "p1_silu_%d" % fc)
                    E.wait("vector", sAC, "p1_silu_%d" % fc)
                    if fc >= 2:
                        E.wait("vector", sST, "p1_stxb_%d" % (fc - 2))
                    vector.tensor_copy(T.xcb16[par][:], T.xc32[par][:]
                                       ).then_inc(sDV, 1)
                    E.inc(sDV)
                    E.mark(sDV, "p1_cb_%d" % fc)
                    E.wait("scalar", sDV, "p1_cb_%d" % fc)
                    scalar.dma_start(out=xcb_d[db * P:(db + 1) * P, :],
                                     in_=T.xcb16[par][:]
                                     ).then_inc(SS["cxb%d" % par], 16)
                    E.inc(SS["cxb%d" % par], 16)
                    E.mark(SS["cxb%d" % par], "p1_stxb_%d" % fc)
                else:
                    E.wait("scalar", sTE, "p1_mm_%d" % fc)
                    if fc >= 66:
                        E.wait("scalar", sST, "p1_stz_%d" % (fc - 2))
                    scalar.activation(T.sz32[par][:, 0:512],
                                      psP[3 * par + 0][:], AF.Silu)
                    scalar.activation(T.sz32[par][:, 512:1024],
                                      psP[3 * par + 1][:], AF.Silu
                                      ).then_inc(sAC, 1)
                    E.inc(sAC)
                    E.mark(sAC, "p1_ev_%d" % fc)
                    zb = fc - 64
                    E.wait("scalar", sAC, "p1_ev_%d" % fc)
                    scalar.dma_start(out=szt_d[zb * P:(zb + 1) * P, :],
                                     in_=T.sz32[par][:]
                                     ).then_inc(SS["csz%d" % par], 16)
                    E.inc(SS["csz%d" % par], 16)
                    E.mark(SS["csz%d" % par], "p1_stz_%d" % fc)

            ns_p1.__exit__(None, None, None)
            ns_p2 = nc.named_scope("p2"); ns_p2.__enter__()
            for dc in range(64):
                par = dc % 2
                if dc >= 2:
                    E.wait("sync", sTE, "p2_mm_%d" % (dc - 2))
                E.wait("sync", sST, "p1_stxb_%d" % dc)
                xsem = SS["xl%d" % par]
                sync.dma_start(out=T.xcbL[par][:],
                               in_=xcb_d[dc * P:(dc + 1) * P, :]
                               ).then_inc(xsem, 16)
                sync.dma_start(out=T.wxp[par][:], in_=wxpT_t[dc]
                               ).then_inc(xsem, 16)
                E.inc(xsem, 32)
                E.mark(xsem, "p2_ld_%d" % dc)
                E.wait("gpsimd", sLD, "p2_ld_%d" % dc)
                if dc >= 2:
                    E.wait("gpsimd", sTE, "p2_mm_%d" % (dc - 2))
                gpsimd.tensor_copy(T.wxpb[par][:], T.wxp[par][:]
                                   ).then_inc(sPL, 1)
                E.inc(sPL)
                E.mark(sPL, "p2_cast_%d" % dc)
                E.wait("tensor", sPL, "p2_cast_%d" % dc)
                E.wait("tensor", sLD, "p2_ld_%d" % dc)
                if dc == 0:
                    E.wait("tensor", sAC, "p1_ev_127")
                for tt in range(8):
                    inst = tensor.matmul(
                        psP[tt][:, 0:288],
                        T.xcbL[par][:, tt * P:(tt + 1) * P],
                        T.wxpb[par][:],
                        start=(dc == 0), stop=(dc == 63))
                inst.then_inc(sTE, 1)
                E.inc(sTE)
                E.mark(sTE, "p2_mm_%d" % dc)

            E.wait("scalar", sTE, "p2_mm_63")
            for tt in range(8):
                inst = scalar.activation(T.ssm[:, tt * 288:(tt + 1) * 288],
                                         psP[tt][:, 0:288], AF.Copy)
            inst.then_inc(sAC, 1)
            E.inc(sAC)
            E.mark(sAC, "p2_ssm")
            E.wait("scalar", sAC, "p2_ssm")
            scalar.activation(T.ssq[:], T.ssm[:], AF.Square).then_inc(sAC, 1)
            E.inc(sAC)
            E.mark(sAC, "p2_sq")
            E.wait("vector", sAC, "p2_sq")
            for tt in range(8):
                vector.tensor_reduce(nst[:, tt:tt + 1],
                                     T.ssq[:, tt * 288:tt * 288 + 256],
                                     AXX, ALU.add)
                vector.tensor_reduce(nst[:, 8 + tt:9 + tt],
                                     T.ssq[:, tt * 288 + 256:tt * 288 + 272],
                                     AXX, ALU.add)
                inst = vector.tensor_reduce(
                    nst[:, 16 + tt:17 + tt],
                    T.ssq[:, tt * 288 + 272:tt * 288 + 288], AXX, ALU.add)
            inst.then_inc(sDV, 1)
            E.inc(sDV)
            E.mark(sDV, "p2_sums")
            E.wait("gpsimd", sDV, "p2_sums")
            gpsimd.tensor_scalar(nst[:, 0:8], nst[:, 0:8], 1.0 / RK, None,
                                 ALU.mult)
            gpsimd.tensor_scalar(nst[:, 8:24], nst[:, 8:24], 1.0 / 16.0,
                                 None, ALU.mult).then_inc(sPL, 1)
            E.inc(sPL)
            E.mark(sPL, "p2_ms")
            E.wait("scalar", sPL, "p2_ms")
            scalar.activation(nst[:, 24:48], nst[:, 0:24], AF.Ln,
                              bias=epsc[:, 1:2]).then_inc(sAC, 1)
            E.inc(sAC)
            E.mark(sAC, "p2_lnv")
            E.wait("vector", sAC, "p2_lnv")
            vector.tensor_scalar(nst[:, 24:48], nst[:, 24:48], -0.5, None,
                                 ALU.mult).then_inc(sDV, 1)
            E.inc(sDV)
            E.mark(sDV, "p2_lnh")
            E.wait("scalar", sDV, "p2_lnh")
            scalar.activation(nst[:, 0:24], nst[:, 24:48], AF.Exp
                              ).then_inc(sAC, 1)
            E.inc(sAC)
            E.mark(sAC, "p2_rstd")
            E.wait("vector", sAC, "p2_rstd")
            E.wait("vector", sDV, "wlnbc")
            for tt in range(8):
                vector.scalar_tensor_tensor(
                    T.dtrb[:, tt * RK:(tt + 1) * RK],
                    T.ssm[:, tt * 288:tt * 288 + 256],
                    nst[:, tt:tt + 1], wlnbc[:, 0:256], ALU.mult, ALU.mult)
                vector.scalar_tensor_tensor(
                    T.bmb[:, tt * 16:(tt + 1) * 16],
                    T.ssm[:, tt * 288 + 256:tt * 288 + 272],
                    nst[:, 8 + tt:9 + tt], wlnbc[:, 256:272],
                    ALU.mult, ALU.mult)
                inst = vector.scalar_tensor_tensor(
                    T.cmb[:, tt * 16:(tt + 1) * 16],
                    T.ssm[:, tt * 288 + 272:tt * 288 + 288],
                    nst[:, 16 + tt:17 + tt], wlnbc[:, 272:288],
                    ALU.mult, ALU.mult)
            inst.then_inc(sDV, 1)
            E.inc(sDV)
            E.mark(sDV, "p2_app")
            E.wait("tensor", sDV, "p2_app")
            for grp in range(4):
                pb = psP[6 + grp % 2]
                if grp >= 2:
                    E.wait("tensor", sDV, "p2_trev_%d" % (grp - 2))
                for j in range(4):
                    idx = grp * 4 + j
                    tt, rc = idx // 2, idx % 2
                    inst = tensor.transpose(
                        pb[:, j * P:(j + 1) * P],
                        T.dtrb[:, tt * RK + rc * P:tt * RK + (rc + 1) * P],
                        ident_f[:])
                inst.then_inc(sTE, 1)
                E.inc(sTE)
                E.mark(sTE, "p2_tr_%d" % grp)
                E.wait("vector", sTE, "p2_tr_%d" % grp)
                for j in range(4):
                    idx = grp * 4 + j
                    tt, rc = idx // 2, idx % 2
                    inst = vector.tensor_copy(
                        T.dtrT[:, rc * TLOC + tt * P:rc * TLOC + (tt + 1) * P],
                        pb[:, j * P:(j + 1) * P])
                inst.then_inc(sDV, 1)
                E.inc(sDV)
                E.mark(sDV, "p2_trev_%d" % grp)
            E.wait("tensor", sDV, "p2_trev_3")
            for half, srcmb in ((0, T.bmb), (1, T.cmb)):
                for tt in range(8):
                    pb = psP[4 + 2 * half + tt // 4]
                    inst = tensor.transpose(
                        pb[0:16, (tt % 4) * P:(tt % 4 + 1) * P],
                        srcmb[:, tt * 16:(tt + 1) * 16], ident_f[:])
                inst.then_inc(sTE, 1)
                E.inc(sTE)
                E.mark(sTE, "p2_bctr_%d" % half)
                E.wait("vector", sTE, "p2_bctr_%d" % half)
                dstT = T.bmT if half == 0 else T.cmT
                vector.tensor_copy(dstT[:, 0:512],
                                   psP[4 + 2 * half][0:16, 0:512])
                inst = vector.tensor_copy(dstT[:, 512:1024],
                                          psP[5 + 2 * half][0:16, 0:512])
                inst.then_inc(sDV, 1)
                E.inc(sDV)
                E.mark(sDV, "p2_bcev_%d" % half)
            E.wait("scalar", sDV, "p2_trev_3")
            for rc in range(2):
                scalar.dma_start(out=dtr_d[rc * P:(rc + 1) * P, :],
                                 in_=T.dtrT[:, rc * TLOC:(rc + 1) * TLOC]
                                 ).then_inc(SS["sBC"], 16)
                E.inc(SS["sBC"], 16)
            E.wait("scalar", sDV, "p2_bcev_0")
            E.wait("scalar", sDV, "p2_bcev_1")
            scalar.dma_start(out=bc_d[0:16, :], in_=T.bmT[:]
                             ).then_inc(SS["sBC"], 16)
            scalar.dma_start(out=bc_d[16:32, :], in_=T.cmT[:]
                             ).then_inc(SS["sBC"], 16)
            E.inc(SS["sBC"], 32)
            E.mark(SS["sBC"], "ex1_bc")
            ns_p2.__exit__(None, None, None)
            E.wait("gpsimd", sST, "ex1_bc")
            E.wait("gpsimd", sST, "p1_stxb_62")
            E.wait("gpsimd", sST, "p1_stxb_63")
            gpsimd.collective_compute(
                "AllGather", ALU.bypass, replica_groups=GROUPS,
                ins=[xcb_d.opt()], outs=[xcb_g.opt()]).then_inc(cc1, 1)
            gpsimd.collective_compute(
                "AllGather", ALU.bypass, replica_groups=GROUPS,
                ins=[dtr_d.opt()], outs=[dtr_g.opt()]).then_inc(cc2, 1)
            gpsimd.collective_compute(
                "AllGather", ALU.bypass, replica_groups=GROUPS,
                ins=[bc_d.opt()], outs=[bc_g.opt()]).then_inc(cc3, 1)

        def emit_A2(G, T):
            sync = G["sync"]; vector = G["vector"]; scalar = G["scalar"]
            tensor = G["tensor"]; gpsimd = G["gpsimd"]

            ns_wc = nc.named_scope("a2wc"); ns_wc.__enter__()
            for it in range(2):
                if it >= 1:
                    E.wait("sync", sST, "a2_wst_%d" % (it - 1))
                sync.dma_start(out=T.woc32[0][:],
                               in_=wdtT_in[it * P:(it + 1) * P, :]
                               ).then_inc(SS["sWT"], 16)
                E.inc(SS["sWT"], 16)
                E.mark(SS["sWT"], "a2_wld_%d" % it)
                E.wait("gpsimd", sLD, "a2_wld_%d" % it)
                if it >= 1:
                    E.wait("gpsimd", sST, "a2_wst_%d" % (it - 1))
                gpsimd.tensor_copy(T.woc16[0][:], T.woc32[0][:]
                                   ).then_inc(sPL, 1)
                E.inc(sPL)
                E.mark(sPL, "a2_wc_%d" % it)
                E.wait("scalar", sPL, "a2_wc_%d" % it)
                scalar.dma_start(out=wdt_scr[it], in_=T.woc16[0][:]
                                 ).then_inc(SS["sWS"], 16)
                E.inc(SS["sWS"], 16)
                E.mark(SS["sWS"], "a2_wst_%d" % it)

            ns_wc.__exit__(None, None, None)
            ns_ffn = nc.named_scope("ffn"); ns_ffn.__enter__()
            E.wait("sync", sST, "dq_st_31")
            sync.dma_start(out=T.fbrow[:], in_=fbias_row
                           ).then_inc(SS["sCN"], 16)
            E.inc(SS["sCN"], 16)
            E.mark(SS["sCN"], "ffn_fb")
            E.wait("tensor", sLD, "ffn_fb")
            E.wait("tensor", sPL, "idb")
            E.wait("tensor", sAC, "p2_ssm")
            for og in range(8):
                inst = tensor.matmul(psP[og][:], ones1[:],
                                     T.fbrow[0:1, og * 512:(og + 1) * 512],
                                     start=True, stop=True)
            inst.then_inc(sTE, 1)
            E.inc(sTE)
            E.mark(sTE, "ffn_fbmm")
            E.wait("vector", sTE, "ffn_fbmm")
            for og in range(8):
                inst = vector.tensor_copy(T.fb_bc[:, og * 512:(og + 1) * 512],
                                          psP[og][:])
            inst.then_inc(sDV, 1)
            E.inc(sDV)
            E.mark(sDV, "ffn_fbbc")
            for og in range(8):
                for ic in range(32):
                    it = og * 32 + ic
                    if it >= 4:
                        E.wait("sync", sTE, "ffn_mm_%d" % (it - 4))
                    tsem = SS["t%d" % (ic % 4)]
                    sync.dma_start(
                        out=T.wTr[ic % 4][:],
                        in_=wffn_scr[og * 512:(og + 1) * 512,
                                     ic * P:(ic + 1) * P],
                        transpose=True).then_inc(tsem, 16)
                    E.inc(tsem, 16)
                    E.mark(tsem, "ffn_ld_%d" % it)
                    E.wait("tensor", sLD, "ffn_ld_%d" % it)
                    if ic == 0:
                        E.wait("tensor", sDV, "ffn_fbbc")
                        if og >= 1:
                            E.wait("tensor", sAC, "ffn_ev_%d" % (og - 1))
                    for tt in range(8):
                        inst = tensor.matmul(
                            psP[tt][:],
                            xnT[:, ic * TEXT + 3 + tt * P:
                                ic * TEXT + 3 + (tt + 1) * P],
                            T.wTr[ic % 4][:],
                            start=(ic == 0), stop=(ic == 31))
                    inst.then_inc(sTE, 1)
                    E.inc(sTE)
                    E.mark(sTE, "ffn_mm_%d" % it)
                for tt in range(8):
                    E.wait("vector", sTE, "ffn_mm_%d" % (og * 32 + 31))
                    if og >= 1:
                        E.wait("vector", sAC, "ffn_ev_%d" % (og - 1))
                    if og > 0 or tt >= 2:
                        pog, ptt = (og, tt - 2) if tt >= 2 else (og - 1, 6 + tt)
                        E.wait("vector", sAC, "ffn_sl_%d_%d" % (pog, ptt))
                    inst = vector.scalar_tensor_tensor(
                        T.gbuf[tt % 2][:], psP[tt][:], 1.0,
                        T.fb_bc[:, og * 512:(og + 1) * 512],
                        ALU.mult, ALU.add)
                    inst.then_inc(sDV, 1)
                    E.inc(sDV)
                    E.mark(sDV, "ffn_gb_%d_%d" % (og, tt))
                    E.wait("scalar", sDV, "ffn_gb_%d_%d" % (og, tt))
                    if og >= 1 or tt >= 2:
                        pog, ptt = (og, tt - 2) if tt >= 2 else (og - 1, 6 + tt)
                        E.wait("scalar", sST, "ffn_st_%d_%d" % (pog, ptt))
                    inst = scalar.activation(T.gtt[tt % 2][:], T.gbuf[tt % 2][:],
                                             AF.Silu)
                    inst.then_inc(sAC, 1)
                    E.inc(sAC)
                    E.mark(sAC, "ffn_sl_%d_%d" % (og, tt))
                    E.wait("scalar", sAC, "ffn_sl_%d_%d" % (og, tt))
                    scalar.dma_start(
                        out=gt_d[tt * P:(tt + 1) * P, og * 512:(og + 1) * 512],
                        in_=T.gtt[tt % 2][:]).then_inc(SS["cgt%d" % (tt % 2)], 16)
                    E.inc(SS["cgt%d" % (tt % 2)], 16)
                    E.mark(SS["cgt%d" % (tt % 2)], "ffn_st_%d_%d" % (og, tt))
                E.mark(sAC, "ffn_ev_%d" % og)

            ns_ffn.__exit__(None, None, None)

        def emit_B(G, T):
            sync = G["sync"]; vector = G["vector"]; scalar = G["scalar"]
            tensor = G["tensor"]; gpsimd = G["gpsimd"]

            E.wait("sync", cc2, "never")


        class TensA:
            pass

        with ExitStack() as xnS:
            xnT = xnS.enter_context(nc.sbuf_tensor("xnT", [P, 32 * TEXT], BF16))

            with ExitStack() as sA:
                def sba(name, shape, dt=F32):
                    return sA.enter_context(nc.sbuf_tensor(name, shape, dt))
                T = TensA()
                T.hs_t = [sba("hs_t%d" % i, [P, H]) for i in range(2)]
                T.xn_t = sba("xn_t", [P, H])
                T.wbuf = [sba("wbuf%d" % i, [P, H]) for i in range(2)]
                T.sq_t = T.wbuf[0]
                T.wbb = [sba("wbb%d" % i, [P, H], BF16) for i in range(2)]
                T.xsb = [sba("xsb%d" % i, [P, TEXT], BF16) for i in range(2)]
                T.diag = [sba("diag%d" % i, [P, 4 * P], BF16) for i in range(2)]
                T.xc32 = [sba("xc32%d" % i, [P, TLOC]) for i in range(2)]
                T.xcb16 = [sba("xcb16%d" % i, [P, TLOC], BF16) for i in range(2)]
                T.sz32 = [sba("sz32%d" % i, [P, TLOC], BF16) for i in range(2)]
                T.xcbL = [sba("xcbL%d" % i, [P, TLOC], BF16) for i in range(2)]
                T.wxp = [sba("wxp%d" % i, [P, 288]) for i in range(2)]
                T.wxpb = [sba("wxpb%d" % i, [P, 288], BF16) for i in range(2)]
                T.ssm = T.hs_t[0][:, 0:2304]
                T.ssq = T.wbuf[0][:, 0:2304]
                T.dtrb = T.hs_t[1][:, 0:2048]
                T.bmb = T.hs_t[1][:, 2048:2176]
                T.cmb = T.hs_t[1][:, 2176:2304]
                T.dtrT = sba("dtrT", [P, 2 * TLOC], BF16)
                T.qwt = [T.hs_t[0][:].bitcast(I32),
                         T.hs_t[1][:].bitcast(I32)]
                T.wdq = T.xn_t[:].bitcast(BF16)[:, 0:H]
                T.scz = sba("scz", [P, 128])
                T.bmT = T.xcbL[0][0:16, :]
                T.cmT = T.xcbL[1][0:16, :]
                T.cw_l = cw_sb[:]
                T.cb_l = cb_sb[:]
                T.dtb_l = dtb_sb[:]
                T.dp_l = dp_sb[:]
                T.fb_l = fb_sb[:]
                T.wln_l = wln_sb[:]
                T.id_l = ident_f[:]
                T.al_l = alog_sb[:]

                mock = {e: _MockEng() for e in ENGINES}
                E.start_pass(mock)
                emit_A(mock, T)
                snapshot_ctr_A = dict(E.ctr)
                E.end_pass()

                with nc.Block() as block:
                    for engname in ENGINES:
                        def mk(engname):
                            def body(eng):
                                engs = {e: _MockEng() for e in ENGINES}
                                engs[engname] = eng
                                E.start_pass(engs)
                                emit_A(engs, T)
                                assert E.ctr == snapshot_ctr_A
                            return body
                        getattr(block, engname)(mk(engname))
                E.ctr = dict(snapshot_ctr_A)

            if "B" not in PHASES and False:
                pass
            with ExitStack() as sA2:
                def sba2(name, shape, dt=F32):
                    return sA2.enter_context(nc.sbuf_tensor(name, shape, dt))
                T2 = TensA()
                T2.woc32 = [sba2("woc32", [P, DIN])]
                T2.woc16 = [sba2("woc16", [P, DIN], BF16)]
                T2.wTr = [sba2("wTr%d" % i, [P, 512], BF16) for i in range(4)]
                T2.gbuf = [sba2("gbuf%d" % i, [P, 512]) for i in range(2)]
                T2.gtt = [sba2("gtt%d" % i, [P, 512], BF16) for i in range(2)]
                T2.fb_bc = sba2("fb_bc", [P, H])
                T2.fbrow = sba2("fbrow", [1, H])

                base_ctr = dict(E.ctr)
                E.start_pass({e: _MockEng() for e in ENGINES})
                E.ctr = dict(base_ctr)
                emit_A2({e: _MockEng() for e in ENGINES}, T2)
                snap2 = dict(E.ctr)
                E.end_pass()

                with nc.Block() as block:
                    for engname in ENGINES:
                        def mk2(engname):
                            def body(eng):
                                engs = {e: _MockEng() for e in ENGINES}
                                engs[engname] = eng
                                E.start_pass(engs)
                                E.ctr = dict(base_ctr)
                                emit_A2(engs, T2)
                                assert E.ctr == snap2
                            return body
                        getattr(block, engname)(mk2(engname))
                E.ctr = dict(snap2)

        def emit_Bv2(G, T):
            sync = G["sync"]; vector = G["vector"]; scalar = G["scalar"]
            tensor = G["tensor"]; gpsimd = G["gpsimd"]

            scalar.activation(adev_sb[:], alog_sb[:], AF.Exp).then_inc(sAC, 1)
            E.inc(sAC)
            E.mark(sAC, "b_aexp")
            E.wait("vector", sAC, "b_aexp")
            vector.tensor_scalar(adev_sb[:], adev_sb[:], -1.0, None, ALU.mult
                                 ).then_inc(sDV, 1)
            E.inc(sDV)
            E.mark(sDV, "b_adev")

            E.wait_val("sync", cc2, 1)
            for rc in range(2):
                for rk in range(2):
                    sync.dma_start(
                        out=T.dtrg[rc][:, rk * TLOC:(rk + 1) * TLOC],
                        in_=dtr_g[rk * RK + rc * P:rk * RK + (rc + 1) * P, :]
                        ).then_inc(SS["sCN"], 16)
                    E.inc(SS["sCN"], 16)
            E.wait_val("sync", cc3, 1)
            for rk in range(2):
                sync.dma_start(out=T.bmTg[:, rk * TLOC:(rk + 1) * TLOC],
                               in_=bc_g[rk * 32:rk * 32 + 16, :]
                               ).then_inc(SS["sCN"], 16)
                sync.dma_start(out=T.cmTg[:, rk * TLOC:(rk + 1) * TLOC],
                               in_=bc_g[rk * 32 + 16:rk * 32 + 32, :]
                               ).then_inc(SS["sCN"], 16)
                E.inc(SS["sCN"], 32)
            sync.dma_start(out=T.eself[:], in_=esel_in
                           ).then_inc(SS["sCN"], 16)
            E.inc(SS["sCN"], 16)
            E.mark(SS["sCN"], "b_pro")
            E.wait("gpsimd", sLD, "b_pro")
            gpsimd.tensor_copy(T.eselb[:], T.eself[:]).then_inc(sPL, 1)
            E.inc(sPL)
            E.mark(sPL, "b_esel")

            ns_bc = nc.named_scope("brep"); ns_bc.__enter__()
            for n in range(NH):
                for half, (srcT, dstR) in enumerate(
                        ((T.bmTg, T.B_rep), (T.cmTg, T.C_rep))):
                    for tc in range(4):
                        idx = n * 8 + half * 4 + tc
                        pb = psP[idx % 2]
                        E.wait("tensor", sPL, "b_esel")
                        E.wait("tensor", sLD, "b_pro")
                        if idx == 0:
                            E.wait("tensor", sAC, "ffn_ev_7")
                        if idx >= 2:
                            E.wait("tensor", sAC, "b_bc_%d" % (idx - 2))
                        tensor.matmul(pb[:],
                                      T.eselb[:, n * P:(n + 1) * P],
                                      srcT[:, tc * 512:(tc + 1) * 512],
                                      start=True, stop=True).then_inc(sTE, 1)
                        E.inc(sTE)
                        E.mark(sTE, "b_bcm_%d" % idx)
                        E.wait("scalar", sTE, "b_bcm_%d" % idx)
                        scalar.activation(
                            dstR[:, n * TFULL + tc * 512:
                                 n * TFULL + (tc + 1) * 512],
                            pb[:], AF.Copy).then_inc(sAC, 1)
                        E.inc(sAC)
                        E.mark(sAC, "b_bc_%d" % idx)

            ns_bc.__exit__(None, None, None)
            ns_sc = nc.named_scope("scan"); ns_sc.__enter__()

            def emit_yevac(db):
                ypar = db % 2
                E.wait("scalar", sTE, "b_ys_%d" % db)
                if db >= 1:
                    E.wait("scalar", sST, "b_yst_%d" % (db - 1))
                for tc in range(4):
                    inst = scalar.activation(
                        T.ysbv[ypar][:, tc * 512:(tc + 1) * 512],
                        psP[4 + tc][:], AF.Copy)
                inst.then_inc(sAC, 1)
                E.inc(sAC)
                E.mark(sAC, "b_yev_%d" % db)
                E.wait("scalar", sAC, "b_yev_%d" % db)
                for rk in range(2):
                    inst = scalar.dma_start(
                        out=yp_d[rk * DIN + db * P:rk * DIN + (db + 1) * P, :],
                        in_=T.ysbv[ypar][:, rk * TLOC:(rk + 1) * TLOC])
                    inst.then_inc(SS["cyb"], 16)
                    E.inc(SS["cyb"], 16)
                E.mark(SS["cyb"], "b_yst_%d" % db)

            for db in range(64):
                par = db % 2
                E.wait_val("sync", cc1, 1)
                if db >= 1:
                    E.wait("sync", sDV, "b_u_%d" % (db - 1))
                for rk in range(2):
                    inst = sync.dma_start(
                        out=T.xcg[0][:, rk * TLOC:(rk + 1) * TLOC],
                        in_=xcb_g[rk * DIN + db * P:rk * DIN + (db + 1) * P, :])
                    inst.then_inc(SS["sXG"], 16)
                    E.inc(SS["sXG"], 16)
                E.mark(SS["sXG"], "b_xcg_%d" % db)
                if db == 0:
                    E.wait("sync", sST, "a2_wst_1")
                if db >= 2:
                    E.wait("sync", sTE, "b_dtmm_%d" % ((db - 2) * 4 + 3))
                for rc in range(2):
                    inst = sync.dma_start(
                        out=T.wdtr[par][:, rc * P:(rc + 1) * P],
                        in_=wdt_scr[rc][:, db * P:(db + 1) * P])
                    inst.then_inc(SS["wd%d" % par], 16)
                    E.inc(SS["wd%d" % par], 16)
                E.mark(SS["wd%d" % par], "b_wdt_%d" % db)
                for tc in range(4):
                    idx = db * 4 + tc
                    E.wait("tensor", sLD, "b_wdt_%d" % db)
                    E.wait("tensor", sLD, "b_pro")
                    if idx >= 2:
                        E.wait("tensor", sAC, "b_spexp_%d" % (idx - 2))
                    if db == 0 and tc == 0:
                        E.wait("tensor", sAC, "b_bc_%d" % (NH * 8 - 1))
                    pb = psP[tc % 2]
                    for rc in range(2):
                        inst = tensor.matmul(
                            pb[:],
                            T.wdtr[par][:, rc * P:(rc + 1) * P],
                            T.dtrg[rc][:, tc * 512:(tc + 1) * 512],
                            start=(rc == 0), stop=(rc == 1))
                    inst.then_inc(sTE, 1)
                    E.inc(sTE)
                    E.mark(sTE, "b_dtmm_%d" % idx)
                    E.wait("scalar", sTE, "b_dtmm_%d" % idx)
                    if db >= 1 and tc == 0:
                        E.wait("scalar", sDV, "b_u_%d" % (db - 1))
                    scalar.activation(T.etmp[tc % 2][:], psP[tc % 2][:],
                                      AF.Exp, bias=dtb_sb[:, db:db + 1]
                                      ).then_inc(sAC, 1)
                    E.inc(sAC)
                    E.mark(sAC, "b_spexp_%d" % idx)
                    if tc >= 1:
                        scalar.activation(
                            T.dtT[0][:, (tc - 1) * 512:tc * 512],
                            T.etmp[(tc - 1) % 2][:], AF.Ln, bias=1.0
                            ).then_inc(sAC, 1)
                        E.inc(sAC)
                        E.mark(sAC, "b_spln_%d" % (idx - 1))
                scalar.activation(T.dtT[0][:, 3 * 512:4 * 512],
                                  T.etmp[1][:], AF.Ln, bias=1.0
                                  ).then_inc(sAC, 1)
                E.inc(sAC)
                E.mark(sAC, "b_spln_%d" % (db * 4 + 3))
                E.mark(sAC, "b_dtT_%d" % db)
                if db >= 1:
                    emit_yevac(db - 1)
                E.wait("vector", sAC, "b_dtT_%d" % db)
                E.wait("vector", sLD, "b_xcg_%d" % db)
                vector.scalar_tensor_tensor(T.u[0][:], T.dtT[0][:], 1.0,
                                            T.xcg[0][:], ALU.mult, ALU.mult
                                            ).then_inc(sDV, 1)
                E.inc(sDV)
                E.mark(sDV, "b_u_%d" % db)

                for np_ in range(NH // 2):
                    for j in range(2):
                        n = np_ * 2 + j
                        npar = n % 3
                        E.wait("scalar", sAC, "b_dtT_%d" % db)
                        if db > 0 or n >= 3:
                            pdb, pn = (db, n - 3) if n >= 3 else                                 (db - 1, NH - 3 + n)
                            E.wait("scalar", sDV, "b_scan_%d_%d" % (pdb, pn))
                        scalar.activation(
                            T.dA[npar][:], T.dtT[0][:], AF.Exp,
                            scale=adev_sb[:, db * NH + n:db * NH + n + 1]
                            ).then_inc(sAC, 1)
                        E.inc(sAC)
                        E.mark(sAC, "b_dA_%d_%d" % (db, n))
                    for j in range(2):
                        n = np_ * 2 + j
                        dpar = n % 3
                        if db > 0 or n >= 3:
                            pdb, pn = (db, n - 3) if n >= 3 else                                 (db - 1, NH - 3 + n)
                            E.wait("vector", sDV, "b_scan_%d_%d" % (pdb, pn))
                        vector.tensor_tensor(
                            T.dBx[dpar][:], T.u[0][:],
                            T.B_rep[:, n * TFULL:(n + 1) * TFULL], ALU.mult
                            ).then_inc(sDV, 1)
                        E.inc(sDV)
                    for j in range(2):
                        n = np_ * 2 + j
                        npar = n % 3
                        dpar = n % 3
                        E.wait("vector", sAC, "b_dA_%d_%d" % (db, n))
                        if n == 0 and db >= 2:
                            E.wait("vector", sTE, "b_ys_%d" % (db - 2))
                        if n == 6 and db >= 2:
                            E.wait("vector", sST, "b_yst_%d" % (db - 2))
                        vector.tensor_tensor_scan(
                            T.hbig[par][:, n * TFULL:(n + 1) * TFULL],
                            T.dA[npar][:], T.dBx[dpar][:], 0.0,
                            ALU.mult, ALU.add).then_inc(sDV, 1)
                        E.inc(sDV)
                        E.mark(sDV, "b_scan_%d_%d" % (db, n))
                vector.tensor_tensor(T.hbig[par][:], T.hbig[par][:],
                                     T.C_rep[:], ALU.mult).then_inc(sDV, 1)
                E.inc(sDV)
                E.mark(sDV, "b_ch_%d" % db)
                E.wait("tensor", sDV, "b_ch_%d" % db)
                if db >= 1:
                    E.wait("tensor", sAC, "b_yev_%d" % (db - 1))
                for tc in range(4):
                    for n in range(NH):
                        inst = tensor.matmul(
                            psP[4 + tc][:], ident_b[:],
                            T.hbig[par][:, n * TFULL + tc * 512:
                                        n * TFULL + (tc + 1) * 512],
                            start=(n == 0), stop=(n == NH - 1))
                inst.then_inc(sTE, 1)
                E.inc(sTE)
                E.mark(sTE, "b_ys_%d" % db)
                if db == 63:
                    emit_yevac(63)
            ns_sc.__exit__(None, None, None)
            E.wait("gpsimd", sST, "b_yst_63")
            gpsimd.collective_compute(
                "ReduceScatter", ALU.add, replica_groups=GROUPS,
                ins=[yp_d.opt()], outs=[yown_d.opt()]).then_inc(cc4, 1)

        with ExitStack() as sB:
            def sbb(name, shape, dt=F32):
                return sB.enter_context(nc.sbuf_tensor(name, shape, dt))
            TB = TensA()
            TB.dtrg = [sbb("dtrg%d" % i, [P, TFULL], BF16) for i in range(2)]

            TB.eselb = sbb("eselb", [16, NH * P], BF16)
            TB.wdtr = [sbb("wdtr%d" % i, [P, 2 * P], BF16) for i in range(2)]
            TB.B_rep = sbb("B_rep", [P, NH * TFULL], BF16)
            TB.C_rep = sbb("C_rep", [P, NH * TFULL], BF16)
            TB.xcg = [sbb("xcg0", [P, TFULL], BF16)]
            TB.dtT = [sbb("dtT0", [P, TFULL])]
            TB.eself = TB.dtT[0][0:16, 0:NH * P]
            TB.etmp = [sbb("etmp%d" % i, [P, 512]) for i in range(2)]
            TB.u = [sbb("u", [P, TFULL], BF16)]
            TB.dA = [sbb("dA%d" % i, [P, TFULL]) for i in range(3)]
            TB.dBx = [sbb("dBx%d" % i, [P, TFULL], BF16) for i in range(3)]
            TB.hbig = [sbb("hbig%d" % i, [P, NH * TFULL], BF16) for i in range(2)]
            TB.bmTg = TB.hbig[0][0:16, 0:TFULL]
            TB.cmTg = TB.hbig[0][0:16, TFULL:2 * TFULL]
            TB.ysbv = [TB.hbig[0][:, 6 * TFULL:7 * TFULL],
                       TB.hbig[1][:, 6 * TFULL:7 * TFULL]]
            TB.ysb = [None]

            if "B" in PHASES:
                base_ctr = dict(E.ctr)
                E.start_pass({e: _MockEng() for e in ENGINES})
                E.ctr = dict(base_ctr)
                emit_Bv2({e: _MockEng() for e in ENGINES}, TB)
                snapB = dict(E.ctr)
                E.end_pass()
                with nc.Block() as block:
                    for engname in ENGINES:
                        def mkB(engname):
                            def body(eng):
                                engs = {e: _MockEng() for e in ENGINES}
                                engs[engname] = eng
                                E.start_pass(engs)
                                E.ctr = dict(base_ctr)
                                emit_Bv2(engs, TB)
                                assert E.ctr == snapB
                            return body
                        getattr(block, engname)(mkB(engname))
                E.ctr = dict(snapB)

        def emit_C(G, T):
            sync = G["sync"]; vector = G["vector"]; scalar = G["scalar"]
            tensor = G["tensor"]; gpsimd = G["gpsimd"]

            ns_g = nc.named_scope("gate"); ns_g.__enter__()
            for db in (range(64) if "g" in CPARTS else []):
                par = db % 2
                E.wait_val("sync", cc4, 1)
                if db == 0:
                    E.wait("sync", sST, "p1_stxb_62")
                    E.wait("sync", sST, "p1_stxb_63")
                    E.wait("sync", sST, "p1_stz_126")
                    E.wait("sync", sST, "p1_stz_127")
                if db >= 2:
                    E.wait("sync", sDV, "c_yg_%d" % (db - 2))
                gsem = SS["g%d" % par]
                sync.dma_start(out=T.yown[par][:],
                               in_=yown_d[db * P:(db + 1) * P, :]
                               ).then_inc(gsem, 16)
                sync.dma_start(out=T.xcL[par][:],
                               in_=xcb_d[db * P:(db + 1) * P, :]
                               ).then_inc(gsem, 16)
                sync.dma_start(out=T.szL[par][:],
                               in_=szt_d[db * P:(db + 1) * P, :]
                               ).then_inc(gsem, 16)
                E.inc(gsem, 48)
                E.mark(gsem, "c_ld_%d" % db)
                E.wait("vector", sLD, "c_ld_%d" % db)
                vector.scalar_tensor_tensor(T.t1[par][:], T.xcL[par][:],
                                            dp_sb[:, db:db + 1],
                                            T.yown[par][:], ALU.mult, ALU.add)
                vector.scalar_tensor_tensor(
                    T.ygate[:, db * TLOC:(db + 1) * TLOC],
                    T.t1[par][:], 1.0, T.szL[par][:], ALU.mult, ALU.mult
                    ).then_inc(sDV, 1)
                E.inc(sDV)
                E.mark(sDV, "c_yg_%d" % db)

            ns_g.__exit__(None, None, None)
            ns_op = nc.named_scope("oproj"); ns_op.__enter__()
            for hc in (range(32) if "o" in CPARTS else []):
                for ck in range(2):
                    it = hc * 2 + ck
                    if it >= 2:
                        E.wait("sync", sAC, "c_wc_%d" % (it - 2))
                    osem = SS["o%d" % ck]
                    for hv in range(2):
                        inst = sync.dma_start(
                            out=T.wo2[ck][:, hv * 2048:(hv + 1) * 2048],
                            in_=woutT_t[hc][:, ck * 4096 + hv * 2048:
                                            ck * 4096 + (hv + 1) * 2048])
                        inst.then_inc(osem, 16)
                        E.inc(osem, 16)
                    E.mark(osem, "c_wo_%d" % it)
                    E.wait("scalar", sLD, "c_wo_%d" % it)
                    if it >= 2:
                        E.wait("scalar", sTE, "c_mm_%d_%d" % (hc - 1, ck))
                    scalar.activation(T.wo2b[ck][:], T.wo2[ck][:], AF.Copy
                                      ).then_inc(sAC, 1)
                    E.inc(sAC)
                    E.mark(sAC, "c_wc_%d" % it)
                for ck in range(2):
                    E.wait("tensor", sAC, "c_wc_%d" % (hc * 2 + ck))
                    if ck == 0:
                        E.wait("tensor", sDV, "c_yg_63")
                        if hc >= 1:
                            E.wait("tensor", sAC, "c_sev_%d" % (hc - 1))
                    for dcl in range(32):
                        dc = ck * 32 + dcl
                        for tc in range(2):
                            inst = tensor.matmul(
                                psP[6 + tc][:],
                                T.wo2b[ck][:, dcl * P:(dcl + 1) * P],
                                T.ygate[:, dc * TLOC + tc * 512:
                                        dc * TLOC + (tc + 1) * 512],
                                start=(dc == 0), stop=(dc == 63))
                    inst.then_inc(sTE, 1)
                    E.inc(sTE)
                    E.mark(sTE, "c_mm_%d_%d" % (hc, ck))
                E.mark(sTE, "c_mm_%d" % hc)
                E.wait("scalar", sTE, "c_mm_%d_1" % hc)
                if hc >= 1:
                    E.wait("scalar", sST, "c_sst_%d" % (hc - 1))
                scalar.activation(T.stt_[0][:, 0:512], psP[6][:], AF.Copy)
                scalar.activation(T.stt_[0][:, 512:1024], psP[7][:],
                                  AF.Copy).then_inc(sAC, 1)
                E.inc(sAC)
                E.mark(sAC, "c_sev_%d" % hc)
                E.wait("scalar", sAC, "c_sev_%d" % hc)
                scalar.dma_start(out=st_d[hc * P:(hc + 1) * P, :],
                                 in_=T.stt_[0][:]).then_inc(SS["cst"], 16)
                E.inc(SS["cst"], 16)
                E.mark(SS["cst"], "c_sst_%d" % hc)

            ns_op.__exit__(None, None, None)
            ns_fin = nc.named_scope("fin"); ns_fin.__enter__()
            for tt in (range(8) if "f" in CPARTS else []):
                par = tt % 2
                E.wait("sync", sST, "c_sst_31")
                E.wait("sync", sST, "ffn_st_7_6")
                E.wait("sync", sST, "ffn_st_7_7")
                E.wait("sync", sTE, "c_mm_31_1")
                if tt >= 2:
                    E.wait("sync", sDV, "c_out_%d" % (tt - 2))
                fsem = SS["f%d" % par]
                sync.dma_start(out=T.s_tok[par][:],
                               in_=st_d[:, tt * P:(tt + 1) * P],
                               transpose=True).then_inc(fsem, 16)
                sync.dma_start(out=T.g_tok[par][:],
                               in_=gt_d[tt * P:(tt + 1) * P, :]
                               ).then_inc(fsem, 16)
                sync.dma_start(out=T.xrs[par][:],
                               in_=hs_ext[3 + tt * P:3 + (tt + 1) * P, :]
                               ).then_inc(fsem, 16)
                E.inc(fsem, 48)
                E.mark(fsem, "c_fin_%d" % tt)
                E.wait("vector", sLD, "c_fin_%d" % tt)
                if tt >= 2:
                    E.wait("vector", sST, "c_ost_%d" % (tt - 2))
                vector.tensor_tensor(T.sg[par][:], T.s_tok[par][:],
                                     T.g_tok[par][:], ALU.mult)
                vector.scalar_tensor_tensor(T.outt[par][:], T.sg[par][:],
                                            1.0, T.xrs[par][:],
                                            ALU.mult, ALU.add).then_inc(sDV, 1)
                E.inc(sDV)
                E.mark(sDV, "c_out_%d" % tt)
                E.wait("scalar", sDV, "c_out_%d" % tt)
                scalar.dma_start(out=out_ext[tt * P:(tt + 1) * P, :],
                                 in_=T.outt[par][:]
                                 ).then_inc(SS["cot%d" % par], 16)
                E.inc(SS["cot%d" % par], 16)
                E.mark(SS["cot%d" % par], "c_ost_%d" % tt)
            if DEBUG_TAPS:
                for nm, srcd in [("d_xct", xct_d), ("d_szt", szt_d),
                                 ("d_dtr", dtr_d), ("d_bc", bc_d),
                                 ("d_yown", yown_d), ("d_gt", gt_d),
                                 ("d_st", st_d), ("d_yp", yp_d),
                                 ("d_wf", wffn_scr),
                                 ("d_xcbg", xcb_g)]:
                    scalar.dma_start(out=dbg[nm], in_=srcd).then_inc(sST, 16)
                    E.inc(sST, 16)
            ns_fin.__exit__(None, None, None)
            if "f" in CPARTS:
                E.wait("scalar", sST, "c_ost_6")
                E.wait("scalar", sST, "c_ost_7")

        with ExitStack() as sC:
            def sbc(name, shape, dt=F32):
                return sC.enter_context(nc.sbuf_tensor(name, shape, dt))
            TC = TensA()
            TC.ygate = sbc("ygate", [P, 64 * TLOC], BF16)
            TC.yown = [sbc("yown%d" % i, [P, TLOC], BF16) for i in range(2)]
            TC.t1 = [sbc("t1%d" % i, [P, TLOC]) for i in range(2)]
            TC.xcL = [sbc("xcL%d" % i, [P, TLOC], BF16) for i in range(2)]
            TC.szL = [sbc("szL%d" % i, [P, TLOC], BF16) for i in range(2)]

            TC.wo2 = [sbc("wo2_%d" % i, [P, 32 * P]) for i in range(2)]
            TC.wo2b = [sbc("wo2b_%d" % i, [P, 32 * P], BF16) for i in range(2)]
            TC.stt_ = [sbc("stt_0", [P, TLOC], BF16)]
            ygf = TC.ygate[:].bitcast(F32)
            TC.xrs = [ygf[:, 0:H], ygf[:, H:2 * H]]
            TC.outt = [ygf[:, 2 * H:3 * H], ygf[:, 3 * H:4 * H]]
            TC.s_tok = [TC.ygate[:, 8 * H:9 * H], TC.ygate[:, 9 * H:10 * H]]
            TC.g_tok = [TC.ygate[:, 10 * H:11 * H], TC.ygate[:, 11 * H:12 * H]]
            TC.sg = [TC.ygate[:, 12 * H:13 * H], TC.ygate[:, 13 * H:14 * H]]

            if "C" in PHASES:
                base_ctr = dict(E.ctr)
                E.start_pass({e: _MockEng() for e in ENGINES})
                E.ctr = dict(base_ctr)
                emit_C({e: _MockEng() for e in ENGINES}, TC)
                snapC = dict(E.ctr)
                E.end_pass()
                with nc.Block() as block:
                    for engname in ENGINES:
                        def mkC(engname):
                            def body(eng):
                                engs = {e: _MockEng() for e in ENGINES}
                                engs[engname] = eng
                                E.start_pass(engs)
                                E.ctr = dict(base_ctr)
                                emit_C(engs, TC)
                                assert E.ctr == snapC
                            return body
                        getattr(block, engname)(mkC(engname))

    return nc


def _stage_inputs(inputs):
    hs = inputs["L0_attn_hidden_states"]
    win = inputs["L0_attn_in_proj_w"]
    convw = inputs["L0_attn_conv_w"]
    convb = inputs["L0_attn_conv_b"]
    xpw = inputs["L0_attn_x_proj_w"]
    dtw = inputs["L0_attn_dt_proj_w"]
    dtb = inputs["L0_attn_dt_bias"]
    alog = inputs["L0_attn_A_log"]
    dpar = inputs["L0_attn_D_param"]
    wout = inputs["L0_attn_out_proj_w"]
    dtln = inputs["L0_attn_dt_ln_w"]
    bln = inputs["L0_attn_B_ln_w"]
    cln = inputs["L0_attn_C_ln_w"]
    qw = inputs["L0_ffn_qweight"]
    scl = inputs["L0_ffn_scales"]
    zrs = inputs["L0_ffn_scaled_zeros"]
    fbias = inputs["L0_ffn_bias"]

    f32 = np.float32
    win_t = np.ascontiguousarray(
        win.reshape(32, P, 128, P).transpose(2, 1, 0, 3)
    ).reshape(128, P, 32 * P).astype(f32)
    woT = np.ascontiguousarray(wout.T)
    woutT_t = np.ascontiguousarray(
        woT.reshape(64, P, 32, P).transpose(2, 1, 0, 3)
    ).reshape(32, P, 64 * P).astype(f32)
    wdtT = np.ascontiguousarray(dtw.T).astype(f32)
    wxpT_t = np.ascontiguousarray(xpw.T).reshape(64, P, 288).astype(f32)
    qw2d = np.ascontiguousarray(qw.reshape(H, H)).astype(np.int32)
    scales2 = np.ascontiguousarray(scl.reshape(H, 32)).astype(f32)
    zeros2 = np.ascontiguousarray(zrs.reshape(H, 32)).astype(f32)
    cw4 = np.ascontiguousarray(
        convw.reshape(64, P, 4).transpose(1, 0, 2)).reshape(P, 256).astype(f32)
    cb1 = np.ascontiguousarray(convb.reshape(64, P).T).astype(f32)
    dtb1 = np.ascontiguousarray(dtb.reshape(64, P).T).astype(f32)
    dp1 = np.ascontiguousarray(dpar.reshape(64, P).T).astype(f32)
    fb = np.ascontiguousarray(np.asarray(fbias).reshape(32, P).T).astype(f32)
    fb_row = np.ascontiguousarray(np.asarray(fbias).reshape(1, H)).astype(f32)
    wln = np.concatenate([dtln, bln, cln]).reshape(1, 288).astype(f32)
    ident = np.eye(P, dtype=f32)

    shared = dict(win_t=win_t, woutT_t=woutT_t, wdtT_in=wdtT, wxpT_t=wxpT_t,
                  qw2d=qw2d, scales2=scales2, zeros2=zeros2, cw4=cw4, cb1=cb1,
                  dtb1=dtb1, dp1=dp1, fbias_in=fb, fbias_row=fb_row,
                  wln_in=wln, ident_in=ident)

    in_maps = []
    for c in range(8):
        b, hf = c // 2, c % 2
        t0 = hf * TLOC
        if hf == 0:
            halo = np.zeros((3, H), dtype=f32)
        else:
            halo = hs[b, t0 - 3:t0, :].astype(f32)
        hs_ext = np.ascontiguousarray(
            np.concatenate([halo, hs[b, t0:t0 + TLOC, :].astype(f32)], axis=0))
        alog_l = np.ascontiguousarray(
            alog.reshape(64, P, 16)[:, :, hf * NH:(hf + 1) * NH]
            .transpose(1, 0, 2)).reshape(P, 64 * NH).astype(f32)
        esel = np.zeros((16, NH, P), dtype=f32)
        for n in range(NH):
            esel[hf * NH + n, n, :] = 1.0
        esel = esel.reshape(16, NH * P)
        m = dict(shared)
        m.update(hs_ext=hs_ext, alog_l=alog_l, esel_in=esel)
        in_maps.append(m)
    return in_maps


_NC_CACHE = {}


def kernel(**inputs):
    in_maps = _stage_inputs(inputs)
    if "nc" not in _NC_CACHE:
        _NC_CACHE["nc"] = build_program()
    nc = _NC_CACHE["nc"]
    res = run_bass_kernel_spmd(nc, in_maps, core_ids=list(range(8)))
    outs = [res.results[c]["out_ext"] for c in range(8)]
    full = np.concatenate(outs, axis=0).reshape(4, 2048, H).astype(np.float32)
    return full
